# version 02
# baseline (speedup 1.0000x reference)
import numpy as np
import ml_dtypes
import concourse.bass as bass
import concourse.mybir as mybir
from concourse.bass_utils import run_bass_kernel_spmd

F32 = mybir.dt.float32
BF16 = mybir.dt.bfloat16
U8 = mybir.dt.uint8
AF = mybir.ActivationFunctionType
ALU = mybir.AluOpType

PAIRS = [[0, 1], [2, 3], [4, 5], [6, 7]]
D = 2048
E = 4096
EL = 2048
H = 64
PDIM = 64
NA = 128
NB = 16
T = 2048
TO = 1024
CH = 128
NCH = TO // CH
WARM = 128
DG = D // 128
EG = E // 128
EGL = EL // 128

bfd = ml_dtypes.bfloat16
SEMS = ["ld", "gd", "ts", "ss", "vs", "cc", "od"]


class Bld:
    def __init__(self, nc, blk):
        self.nc = nc
        self.blk = blk
        self.h = {n: nc.alloc_semaphore(n) for n in SEMS}
        self.c = {n: 0 for n in SEMS}
        self.scopes = []
        self.names = set()

    def inc(self, instr, name, n):
        instr.then_inc(self.h[name], n)
        self.c[name] += n
        return self.c[name]

    def w(self, eng, name, v=None):
        v = self.c[name] if v is None else v
        if v > 0:
            eng.wait_ge(self.h[name], v)

    def push(self):
        self.scopes.append([])

    def pop(self):
        self.join_all()
        for cm in reversed(self.scopes.pop()):
            cm.__exit__(None, None, None)

    def _alloc(self, cm):
        t = cm.__enter__()
        if self.scopes:
            self.scopes[-1].append(cm)
        return t

    def sb(self, name, shape, dt):
        assert name not in self.names, name
        self.names.add(name)
        return self._alloc(self.nc.sbuf_tensor(name, shape, dt))

    def ps(self, name, shape, dt=F32):
        assert name not in self.names, name
        self.names.add(name)
        return self._alloc(self.nc.psum_tensor(name, shape, dt))

    def join_all(self):
        snap = dict(self.c)
        for engname in ["sync", "gpsimd", "tensor", "scalar", "vector"]:
            def body(e, snap=snap):
                for n in SEMS:
                    if snap[n] > 0:
                        e.wait_ge(self.h[n], snap[n])
            getattr(self.blk, engname)(body)


def build_nc(upto=None, debug_keys=()):
    nc = bass.Bass("TRN2", target_bir_lowering=False)

    def di(name, shape, dt=F32):
        return nc.dram_tensor(name, shape, dt, kind="ExternalInput")

    xT = di("xT", [D, TO])
    xwT = di("xwT", [D, WARM])
    W_inprojT = di("W_inprojT", [D, 2 * E])
    W_xprojT = di("W_xprojT", [E, 4416])
    conv_a_w = di("conv_a_w", [E, 4])
    conv_a_b = di("conv_a_b", [E, 1])
    W_dtT = di("W_dtT", [H, H])
    dt_b_col = di("dt_b_col", [H, 1])
    Alog_a = di("Alog_a", [H, 1])
    D_a_row = di("D_a_row", [1, E])
    W_outprojT = di("W_outprojT", [E, D])
    ffn_WT = di("ffn_WT", [D, D])
    ffn_gamma = di("ffn_gamma", [D, 1])
    ffn_beta = di("ffn_beta", [D, 1])
    ffn_lnw = di("ffn_lnw", [D, 1])
    ffn_lnb = di("ffn_lnb", [D, 1])
    norm_w = di("norm_w", [D, 1])
    W_skip = di("W_skip", [D, E])
    W_in = di("W_in", [D, E])
    conv_b_w = di("conv_b_w", [E, 4])
    conv_b_b = di("conv_b_b", [E, 1])
    W_d1 = di("W_d1", [E, 128])
    W_d2 = di("W_d2", [128, E])
    b_d2 = di("b_d2", [E, 1])
    W_Bp = di("W_Bp", [E, NB])
    W_Cp = di("W_Cp", [E, NB])
    Alog_b = di("Alog_b", [E, NB])
    W_D_col = di("W_D_col", [E, 1])
    W_out_in = di("W_out_in", [E, D])
    m_send = di("m_send", [128, 1])
    r_recv = di("r_recv", [128, 1])
    sel_r0 = di("sel_r0", [128, 1], U8)

    out = nc.dram_tensor("out", [D, TO], F32, kind="ExternalOutput")

    def drm(name, shape, dt):
        return nc.dram_tensor(name, shape, dt)

    z_spill = drm("z_spill", [E, TO], BF16)
    X_spill = drm("X_spill", [E, TO], BF16)
    yg_spill = drm("yg_spill", [E, TO], BF16)
    signW_spill = drm("signW_spill", [D, D], BF16)
    skip_spill = drm("skip_spill", [EL, T], BF16)
    delta_spill = drm("delta_spill", [E, TO], F32)
    snap_spill = drm("snap_spill", [NCH * NA, E], BF16)
    cs_dram = drm("cs_dram", [H, TO], F32)
    ysk_spill = drm("ysk_spill", [EL, T], BF16)
    Brow_dram = drm("Brow_dram", [NB, T], BF16)
    Crow_dram = drm("Crow_dram", [NB, T], BF16)

    c2_in = drm("c2_in", [NA, E], BF16)
    c2_out = drm("c2_out", [2 * NA, E], BF16)
    c1_in = drm("c1_in", [D, TO], BF16)
    c1_out = drm("c1_out", [2 * D, TO], BF16)
    NF3 = EL + 2 * NB
    c3f_in = drm("c3f_in", [NF3, TO], F32)
    c3f_out = drm("c3f_out", [2 * NF3, TO], F32)
    c3b_in = drm("c3b_in", [EL, TO], BF16)
    c3b_out = drm("c3b_out", [2 * EL, TO], BF16)
    c4_in = [drm(f"c4_in{j}", [2 * D, 512], F32) for j in range(2)]
    c4_out = [drm(f"c4_out{j}", [D, 512], F32) for j in range(2)]

    ident_bf_d = nc.inline_tensor(np.eye(128, dtype=bfd), "ident_bf")
    ident_f_d = nc.inline_tensor(np.eye(128, dtype=np.float32), "ident_f")
    rst_np = np.ones((H, TO), np.float32)
    rst_np[:, ::CH] = 0.0
    rst_d = nc.inline_tensor(rst_np, "rst_a")
    tril_d = nc.inline_tensor(np.triu(np.ones((CH, CH), np.float32)), "tril")

    blk_cm = nc.Block()
    blk = blk_cm.__enter__()
    s = Bld(nc, blk)

    def dbg(key, ap, shape, dt=F32):
        if key in debug_keys:
            t = nc.dram_tensor("dbg_" + key, shape, dt, kind="ExternalOutput")

            def em(sy, t=t, ap=ap):
                for n in SEMS:
                    s.w(sy, n)
                s.inc(sy.dma_start(out=t[:], in_=ap), "od", 16)
            blk.sync(em)

    def fin():
        def sy(sx):
            s.w(sx, "vs")
            s.w(sx, "ss")
            s.w(sx, "ts")
            for i in range(DG):
                s.inc(sx.dma_start(out=out[128 * i:128 * (i + 1), :], in_=resid[i][:]), "od", 16)
            s.w(sx, "od")
        blk.sync(sy)
        blk_cm.__exit__(None, None, None)
        return nc

    resid = [s.sb(f"resid{i}", [128, TO], F32) for i in range(DG)]
    identt = s.sb("identt", [128, 128], BF16)
    identf = s.sb("identf", [128, 128], F32)
    ones1 = s.sb("ones1", [1, 128], BF16)
    ones1f = s.sb("ones1f", [1, 128], F32)
    onescol = s.sb("onescol", [128, 1], F32)
    eps_t = s.sb("eps_t", [128, 1], F32)
    one_t = s.sb("one_t", [128, 1], F32)
    cab = s.sb("cab", [128, EG], F32)
    cbb = s.sb("cbb", [128, EG], F32)
    caw = s.sb("caw", [128, 4 * EG], F32)
    cbw = s.sb("cbw", [128, 4 * EG], F32)
    Aa_t = s.sb("Aa_t", [H, 1], F32)
    gscale = s.sb("gscale", [128, DG], F32)
    fbeta = s.sb("fbeta", [128, DG], F32)
    flnw = s.sb("flnw", [128, DG], F32)
    flnb = s.sb("flnb", [128, DG], F32)
    nrmw = s.sb("nrmw", [128, DG], F32)
    wdcol = s.sb("wdcol", [128, EGL], F32)
    Ab_t = s.sb("Ab_t", [128, NB * EGL], F32)
    msend_t = s.sb("msend_t", [128, 1], F32)
    rrecv_t = s.sb("rrecv_t", [128, 1], F32)
    selr0_t = s.sb("selr0_t", [128, 1], U8)
    onescol_bf = s.sb("onescol_bf", [128, 1], BF16)

    def p0_ld(sy):
        for i in range(DG):
            s.inc(sy.dma_start(out=resid[i][:], in_=xT[128 * i:128 * (i + 1), :]), "ld", 16)
        s.inc(sy.dma_start(out=identt[:], in_=ident_bf_d[:]), "ld", 16)
        s.inc(sy.dma_start(out=identf[:], in_=ident_f_d[:]), "ld", 16)
        s.inc(sy.dma_start(out=Aa_t[:], in_=Alog_a[:]), "ld", 16)
        s.inc(sy.dma_start(out=msend_t[:], in_=m_send[:]), "ld", 16)
        s.inc(sy.dma_start(out=rrecv_t[:], in_=r_recv[:]), "ld", 16)
        s.inc(sy.dma_start(out=selr0_t[:], in_=sel_r0[:]), "ld", 16)
        for eg in range(EG):
            s.inc(sy.dma_start(out=caw[:, 4 * eg:4 * eg + 4], in_=conv_a_w[128 * eg:128 * (eg + 1), :]), "ld", 16)
            s.inc(sy.dma_start(out=cbw[:, 4 * eg:4 * eg + 4], in_=conv_b_w[128 * eg:128 * (eg + 1), :]), "ld", 16)
            s.inc(sy.dma_start(out=cab[:, eg:eg + 1], in_=conv_a_b[128 * eg:128 * (eg + 1), :]), "ld", 16)
            s.inc(sy.dma_start(out=cbb[:, eg:eg + 1], in_=conv_b_b[128 * eg:128 * (eg + 1), :]), "ld", 16)
        for i in range(DG):
            s.inc(sy.dma_start(out=gscale[:, i:i + 1], in_=ffn_gamma[128 * i:128 * (i + 1), :]), "ld", 16)
            s.inc(sy.dma_start(out=fbeta[:, i:i + 1], in_=ffn_beta[128 * i:128 * (i + 1), :]), "ld", 16)
            s.inc(sy.dma_start(out=flnw[:, i:i + 1], in_=ffn_lnw[128 * i:128 * (i + 1), :]), "ld", 16)
            s.inc(sy.dma_start(out=flnb[:, i:i + 1], in_=ffn_lnb[128 * i:128 * (i + 1), :]), "ld", 16)
            s.inc(sy.dma_start(out=nrmw[:, i:i + 1], in_=norm_w[128 * i:128 * (i + 1), :]), "ld", 16)
        for eg in range(EGL):
            s.inc(sy.dma_start(out=wdcol[:, eg:eg + 1], in_=W_D_col[128 * eg:128 * (eg + 1), :]), "ld", 16)
            s.inc(sy.dma_start(out=Ab_t[:, NB * eg:NB * (eg + 1)], in_=Alog_b[128 * eg:128 * (eg + 1), :]), "ld", 16)
    blk.sync(p0_ld)

    def p0_ve(v):
        s.w(v, "ld")
        s.inc(v.memset(ones1[:], 1.0), "vs", 1)
        s.inc(v.memset(ones1f[:], 1.0), "vs", 1)
        s.inc(v.memset(onescol[:], 1.0), "vs", 1)
        s.inc(v.memset(onescol_bf[:], 1.0), "vs", 1)
        s.inc(v.memset(eps_t[:], 1e-5), "vs", 1)
        s.inc(v.memset(one_t[:], 1.0), "vs", 1)
    blk.vector(p0_ve)

    def p0_sc(sc):
        s.w(sc, "ld")
        s.inc(sc.activation(Aa_t[:], Aa_t[:], AF.Exp), "ss", 1)
        s.inc(sc.activation(Ab_t[:], Ab_t[:], AF.Exp), "ss", 1)
    blk.scalar(p0_sc)

    def p0_ve2(v):
        s.w(v, "ss")
        s.inc(v.tensor_scalar(Aa_t[:], Aa_t[:], -1.0, None, ALU.mult), "vs", 1)
        s.inc(v.tensor_scalar(Ab_t[:], Ab_t[:], -1.0, None, ALU.mult), "vs", 1)
    blk.vector(p0_ve2)

    def expand_f32(tag, row_ap, wd, out_ap_fn):
        pse = s.ps(f"{tag}_pse", [128, 512], F32)
        nst = (wd + 511) // 512
        for st in range(nst):
            a = st * 512
            bw = min(wd, a + 512) - a

            def te(t, a=a, bw=bw):
                s.w(t, "vs")
                s.w(t, "ss")
                s.inc(t.matmul(pse[:, 0:bw], ones1f[:], row_ap[0:1, a:a + bw], start=True, stop=True), "ts", 1)
            blk.tensor(te)

            def scv(sc, a=a, bw=bw):
                s.w(sc, "ts")
                s.inc(sc.copy(out_ap_fn(a, bw), pse[:, 0:bw]), "ss", 1)
            blk.scalar(scv)

    def ln_stats(tag, tiles, tc, rstd_row, nmr_row, bf=False):
        srow = s.sb(f"{tag}_srow", [1, tc], F32)
        sqt = s.sb(f"{tag}_sqt", [128, 512], F32)
        ps_s = s.ps(f"{tag}_pss", [1, 512], F32)
        ps_q = s.ps(f"{tag}_psq", [1, 512], F32)
        oc = onescol_bf if bf else onescol
        nst = (tc + 511) // 512
        for st in range(nst):
            c0 = st * 512
            wd = min(tc, c0 + 512) - c0
            for pg in range(DG):
                def scq(sc, pg=pg, c0=c0, wd=wd):
                    s.w(sc, "vs")
                    s.w(sc, "ld")
                    if pg > 0:
                        s.w(sc, "ts")
                    s.inc(sc.activation(sqt[:, 0:wd], tiles[pg][:, c0:c0 + wd], AF.Square), "ss", 1)
                blk.scalar(scq)

                def te(t, pg=pg, c0=c0, wd=wd):
                    s.w(t, "ss")
                    s.inc(t.matmul(ps_s[:, 0:wd], oc[:], tiles[pg][:, c0:c0 + wd],
                                   start=(pg == 0), stop=(pg == DG - 1)), "ts", 1)
                    s.inc(t.matmul(ps_q[:, 0:wd], onescol[:], sqt[:, 0:wd],
                                   start=(pg == 0), stop=(pg == DG - 1)), "ts", 1)
                blk.tensor(te)

            def ve(v, c0=c0, wd=wd):
                s.w(v, "ts")
                s.inc(v.tensor_scalar(srow[:, c0:c0 + wd], ps_s[:, 0:wd], 1.0 / D, None, ALU.mult), "vs", 1)
                s.inc(v.tensor_scalar(rstd_row[:, c0:c0 + wd], ps_q[:, 0:wd], 1.0 / D, None, ALU.mult), "vs", 1)
            blk.vector(ve)

        def ve3(v):
            s.inc(v.tensor_tensor(nmr_row[:, 0:tc], srow[:], srow[:], ALU.mult), "vs", 1)
            s.inc(v.tensor_tensor(rstd_row[:, 0:tc], rstd_row[:, 0:tc], nmr_row[:, 0:tc], ALU.subtract), "vs", 1)
        blk.vector(ve3)

        def sc2(sc):
            s.w(sc, "vs")
            s.inc(sc.activation(rstd_row[:, 0:tc], rstd_row[:, 0:tc], AF.Ln, bias=eps_t[0:1, 0:1]), "ss", 1)
            s.inc(sc.activation(rstd_row[:, 0:tc], rstd_row[:, 0:tc], AF.Exp, scale=-0.5), "ss", 1)
        blk.scalar(sc2)

        def ve4(v):
            s.w(v, "ss")
            s.inc(v.tensor_tensor(nmr_row[:, 0:tc], srow[:], rstd_row[:, 0:tc], ALU.mult), "vs", 1)
            s.inc(v.tensor_scalar(nmr_row[:, 0:tc], nmr_row[:, 0:tc], -1.0, None, ALU.mult), "vs", 1)
        blk.vector(ve4)

    s.push()
    dt2 = s.sb("dt2", [H, TO], F32)
    cs = s.sb("cs", [H, TO], F32)
    B_fm = s.sb("B_fm", [NA, TO], BF16)
    C_fm = s.sb("C_fm", [NA, TO], BF16)
    s.push()
    xc = [s.sb(f"xc{eg}", [128, TO], BF16) for eg in range(EG)]
    s.push()
    ln1 = [s.sb(f"ln1_{i}", [128, WARM + TO], BF16) for i in range(DG)]
    s.push()
    xw = [s.sb(f"xw{i}", [128, WARM], BF16) for i in range(DG)]

    def ldxw(g):
        for i in range(DG):
            s.inc(g.dma_start(out=xw[i][:], in_=xwT[128 * i:128 * (i + 1), :]), "gd", 16)
    blk.gpsimd(ldxw)
    r1 = s.sb("r1", [1, TO], F32)
    n1 = s.sb("n1", [1, TO], F32)
    r1w = s.sb("r1w", [1, WARM], F32)
    n1w = s.sb("n1w", [1, WARM], F32)

    def wtg(sc):
        s.w(sc, "gd")
    blk.scalar(wtg)
    ln_stats("l1o", resid, TO, r1, n1)
    ln_stats("l1w", xw, WARM, r1w, n1w, bf=True)
    tmpa = s.sb("tmpa", [128, 512], F32)
    psEr = s.ps("psEr", [128, 512], F32)
    psEn = s.ps("psEn", [128, 512], F32)
    for st in range(2):
        c0 = st * 512

        def tex(t, c0=c0, st=st):
            s.w(t, "vs")
            s.w(t, "ss")
            if st > 0:
                s.w(t, "vs")
            s.inc(t.matmul(psEr[:], ones1f[:], r1[0:1, c0:c0 + 512], start=True, stop=True), "ts", 1)
            s.inc(t.matmul(psEn[:], ones1f[:], n1[0:1, c0:c0 + 512], start=True, stop=True), "ts", 1)
        blk.tensor(tex)
        for pg in range(DG):
            def ve(v, pg=pg, c0=c0):
                s.w(v, "ts")
                s.inc(v.tensor_tensor(tmpa[:], resid[pg][:, c0:c0 + 512], psEr[:], ALU.mult), "vs", 1)
                s.inc(v.tensor_tensor(ln1[pg][:, WARM + c0:WARM + c0 + 512], tmpa[:], psEn[:], ALU.add), "vs", 1)
            blk.vector(ve)

    def tex2(t):
        s.w(t, "vs")
        s.inc(t.matmul(psEr[:, 0:WARM], ones1f[:], r1w[:], start=True, stop=True), "ts", 1)
        s.inc(t.matmul(psEn[:, 0:WARM], ones1f[:], n1w[:], start=True, stop=True), "ts", 1)
    blk.tensor(tex2)
    for pg in range(DG):
        def ve(v, pg=pg):
            s.w(v, "ts")
            s.inc(v.tensor_tensor(tmpa[:, 0:WARM], xw[pg][:], psEr[:, 0:WARM], ALU.mult), "vs", 1)
            s.inc(v.tensor_tensor(ln1[pg][:, 0:WARM], tmpa[:, 0:WARM], psEn[:, 0:WARM], ALU.add), "vs", 1)
        blk.vector(ve)
    s.pop()
    dbg("ln1", ln1[0][:, :], [128, WARM + TO], BF16)

    s.push()
    wbufB = [s.sb(f"wbufB{i}", [128, 16 * 128], BF16) for i in range(2)]
    ztile = [s.sb(f"ztile{i}", [128, TO], BF16) for i in range(2)]
    psB = [s.ps(f"psB{i}", [128, 512], F32) for i in range(4)]
    bankB = [0] * 4
    wfree = [0, 0]
    zfree = [0, 0]
    for mz in range(32):
        mo = 32 + mz
        wb = wbufB[mz % 2]
        zt = ztile[mz % 2]

        def gp(g, mo=mo, wb=wb, mz=mz):
            s.w(g, "ts", wfree[mz % 2])
            s.inc(g.dma_start(out=wb[:].rearrange("p (k m) -> p k m", k=16),
                              in_=W_inprojT[:, 128 * mo:128 * (mo + 1)]
                              .rearrange("(k p) m -> p k m", p=128)), "gd", 16)
        blk.gpsimd(gp)
        bids = [(2 * mz) % 4, (2 * mz + 1) % 4]

        def te(t, mz=mz, wb=wb, bids=bids):
            s.w(t, "gd")
            if mz == 0:
                s.w(t, "vs")
            for bi in bids:
                s.w(t, "ss", bankB[bi])
            for k in range(DG):
                for ti in range(2):
                    s.inc(t.matmul(psB[bids[ti]][:], wb[:, 128 * k:128 * (k + 1)],
                                   ln1[k][:, WARM + 512 * ti:WARM + 512 * (ti + 1)],
                                   start=(k == 0), stop=(k == DG - 1)), "ts", 1)
            wfree[mz % 2] = s.c["ts"]
        blk.tensor(te)

        def sc(scx, mz=mz, bids=bids, zt=zt):
            s.w(scx, "ts")
            s.w(scx, "od", zfree[mz % 2])
            for ti in range(2):
                s.inc(scx.activation(zt[:, 512 * ti:512 * (ti + 1)], psB[bids[ti]][:], AF.Silu), "ss", 1)
            for bi in bids:
                bankB[bi] = s.c["ss"]
        blk.scalar(sc)

        def sy(sx, mz=mz, zt=zt):
            s.w(sx, "ss")
            s.inc(sx.dma_start(out=z_spill[128 * mz:128 * (mz + 1), :], in_=zt[:]), "od", 16)
            zfree[mz % 2] = s.c["od"]
        blk.sync(sy)
    s.pop()

    s.push()
    wbufX = [s.sb(f"wbufX{i}", [128, 16 * 128], BF16) for i in range(2)]
    xbt = [s.sb(f"xbt{i}", [128, WARM + TO], BF16) for i in range(2)]
    cacc = s.sb("cacc", [128, TO], F32)
    psXx = [s.ps(f"psXx{i}", [128, 512], F32) for i in range(6)]
    bankX = [0] * 6
    wfreeX = [0, 0]
    xbfree = [0, 0]
    tilesB2 = [(0, WARM), (WARM, 512), (WARM + 512, 512)]
    for eg in range(EG):
        wb = wbufX[eg % 2]
        xbe = xbt[eg % 2]

        def gp(g, eg=eg, wb=wb):
            s.w(g, "ts", wfreeX[eg % 2])
            s.inc(g.dma_start(out=wb[:].rearrange("p (k m) -> p k m", k=16),
                              in_=W_inprojT[:, 128 * eg:128 * (eg + 1)]
                              .rearrange("(k p) m -> p k m", p=128)), "gd", 16)
        blk.gpsimd(gp)
        bids = [(3 * eg) % 6, (3 * eg + 1) % 6, (3 * eg + 2) % 6]

        def te(t, eg=eg, wb=wb, bids=bids):
            s.w(t, "gd")
            for bi in bids:
                s.w(t, "ss", bankX[bi])
            for k in range(DG):
                for ti, (c0, wd) in enumerate(tilesB2):
                    s.inc(t.matmul(psXx[bids[ti]][:, 0:wd], wb[:, 128 * k:128 * (k + 1)],
                                   ln1[k][:, c0:c0 + wd],
                                   start=(k == 0), stop=(k == DG - 1)), "ts", 1)
            wfreeX[eg % 2] = s.c["ts"]
        blk.tensor(te)

        def sc(scx, eg=eg, bids=bids, xbe=xbe):
            s.w(scx, "ts")
            s.w(scx, "vs", xbfree[eg % 2])
            for ti, (c0, wd) in enumerate(tilesB2):
                s.inc(scx.copy(xbe[:, c0:c0 + wd], psXx[bids[ti]][:, 0:wd]), "ss", 1)
            for bi in bids:
                bankX[bi] = s.c["ss"]
        blk.scalar(sc)

        def ve(v, eg=eg, xbe=xbe):
            s.w(v, "ss")
            s.inc(v.tensor_scalar(cacc[:], xbe[:, WARM - 3:WARM - 3 + TO],
                                  caw[:, 4 * eg:4 * eg + 1], None, ALU.mult), "vs", 1)
            for k in range(1, 4):
                s.inc(v.scalar_tensor_tensor(cacc[:], xbe[:, WARM - 3 + k:WARM - 3 + k + TO],
                                             caw[:, 4 * eg + k:4 * eg + k + 1], cacc[:],
                                             ALU.mult, ALU.add), "vs", 1)
            xbfree[eg % 2] = s.c["vs"]
        blk.vector(ve)

        def sc2(scx, eg=eg):
            s.w(scx, "vs")
            s.inc(scx.activation(xc[eg][:], cacc[:], AF.Silu, bias=cab[:, eg:eg + 1]), "ss", 1)
        blk.scalar(sc2)
    s.pop()
    s.pop()
    dbg("xc0", xc[0][:, :], [128, TO], BF16)
    if upto == "B":
        return fin()

    s.push()
    wbufD = [s.sb(f"wbufD{i}", [128, 32 * 128], BF16) for i in range(2)]
    xtile = [s.sb(f"xtile{i}", [128, TO], BF16) for i in range(2)]
    s.push()
    psD = [s.ps(f"psD{i}", [128, 512], F32) for i in range(4)]
    bankD = [0] * 4
    wfreeD = [0, 0]
    xtfree = [0, 0]
    for mo in range(32):
        wb = wbufD[mo % 2]
        xte = xtile[mo % 2]

        def gp(g, mo=mo, wb=wb):
            s.w(g, "ts", wfreeD[mo % 2])
            s.inc(g.dma_start(out=wb[:].rearrange("p (k m) -> p k m", k=32),
                              in_=W_xprojT[:, 128 * mo:128 * (mo + 1)]
                              .rearrange("(k p) m -> p k m", p=128)), "gd", 16)
        blk.gpsimd(gp)
        bids = [(2 * mo) % 4, (2 * mo + 1) % 4]

        def te(t, mo=mo, wb=wb, bids=bids):
            s.w(t, "gd")
            if mo == 0:
                s.w(t, "ss")
            for bi in bids:
                s.w(t, "ss", bankD[bi])
            for k in range(EG):
                for ti in range(2):
                    s.inc(t.matmul(psD[bids[ti]][:], wb[:, 128 * k:128 * (k + 1)],
                                   xc[k][:, 512 * ti:512 * (ti + 1)],
                                   start=(k == 0), stop=(k == EG - 1)), "ts", 1)
            wfreeD[mo % 2] = s.c["ts"]
        blk.tensor(te)

        def sc(scx, mo=mo, bids=bids, xte=xte):
            s.w(scx, "ts")
            s.w(scx, "od", xtfree[mo % 2])
            for ti in range(2):
                s.inc(scx.copy(xte[:, 512 * ti:512 * (ti + 1)], psD[bids[ti]][:]), "ss", 1)
            for bi in bids:
                bankD[bi] = s.c["ss"]
        blk.scalar(sc)

        def sy(sx, mo=mo, xte=xte):
            s.w(sx, "ss")
            s.inc(sx.dma_start(out=X_spill[128 * mo:128 * (mo + 1), :], in_=xte[:]), "od", 16)
            xtfree[mo % 2] = s.c["od"]
        blk.sync(sy)

    s.pop()
    wdt = s.sb("wdt", [128, 32 * 64], BF16)
    wBx = s.sb("wBx", [128, 32 * 128], BF16)
    wCx = s.sb("wCx", [128, 32 * 128], BF16)
    dtp = s.sb("dtp", [H, TO], BF16)

    def gp2(g):
        s.w(g, "ts", wfreeD[0])
        s.w(g, "ts", wfreeD[1])
        s.inc(g.dma_start(out=wdt[:].rearrange("p (k m) -> p k m", k=32),
                          in_=W_xprojT[:, 4096:4160].rearrange("(k p) m -> p k m", p=128)), "gd", 16)
        s.inc(g.dma_start(out=wBx[:].rearrange("p (k m) -> p k m", k=32),
                          in_=W_xprojT[:, 4160:4288].rearrange("(k p) m -> p k m", p=128)), "gd", 16)
        s.inc(g.dma_start(out=wCx[:].rearrange("p (k m) -> p k m", k=32),
                          in_=W_xprojT[:, 4288:4416].rearrange("(k p) m -> p k m", p=128)), "gd", 16)
    blk.gpsimd(gp2)
    s.push()
    psdt = s.ps("psdt", [64, 512], F32)
    psBb = s.ps("psBb", [128, 512], F32)
    psCc = s.ps("psCc", [128, 512], F32)
    for ti in range(2):
        def te(t, ti=ti):
            s.w(t, "gd")
            s.w(t, "ss")
            for k in range(EG):
                s.inc(t.matmul(psdt[:], wdt[:, 64 * k:64 * (k + 1)], xc[k][:, 512 * ti:512 * (ti + 1)],
                               start=(k == 0), stop=(k == EG - 1)), "ts", 1)
            for k in range(EG):
                s.inc(t.matmul(psBb[:], wBx[:, 128 * k:128 * (k + 1)], xc[k][:, 512 * ti:512 * (ti + 1)],
                               start=(k == 0), stop=(k == EG - 1)), "ts", 1)
            for k in range(EG):
                s.inc(t.matmul(psCc[:], wCx[:, 128 * k:128 * (k + 1)], xc[k][:, 512 * ti:512 * (ti + 1)],
                               start=(k == 0), stop=(k == EG - 1)), "ts", 1)
        blk.tensor(te)

        def sc(scx, ti=ti):
            s.w(scx, "ts")
            s.inc(scx.copy(dtp[:, 512 * ti:512 * (ti + 1)], psdt[:]), "ss", 1)
            s.inc(scx.copy(B_fm[:, 512 * ti:512 * (ti + 1)], psBb[:]), "ss", 1)
            s.inc(scx.copy(C_fm[:, 512 * ti:512 * (ti + 1)], psCc[:]), "ss", 1)
        blk.scalar(sc)

    s.pop()
    wdtp = s.sb("wdtp", [H, H], BF16)
    dtbcol = s.sb("dtbcol", [H, 1], F32)

    def gp3(g):
        s.inc(g.dma_start(out=wdtp[:], in_=W_dtT[:]), "gd", 16)
    blk.gpsimd(gp3)

    def sy2(sx):
        s.inc(sx.dma_start(out=dtbcol[:], in_=dt_b_col[:]), "ld", 16)
    blk.sync(sy2)
    psd2 = s.ps("psd2", [64, 512], F32)
    for ti in range(2):
        def te(t, ti=ti):
            s.w(t, "gd")
            s.w(t, "ss")
            if ti > 0:
                s.w(t, "vs")
            s.inc(t.matmul(psd2[:], wdtp[:], dtp[:, 512 * ti:512 * (ti + 1)], start=True, stop=True), "ts", 1)
        blk.tensor(te)

        def ve(v, ti=ti):
            s.w(v, "ts")
            s.w(v, "ld")
            s.inc(v.tensor_scalar(dt2[:, 512 * ti:512 * (ti + 1)], psd2[:], dtbcol[:, 0:1],
                                  None, ALU.add), "vs", 1)
        blk.vector(ve)

        def sc(scx, ti=ti):
            s.w(scx, "vs")
            s.inc(scx.activation(dt2[:, 512 * ti:512 * (ti + 1)], dt2[:, 512 * ti:512 * (ti + 1)], AF.Exp), "ss", 1)
            s.inc(scx.activation(dt2[:, 512 * ti:512 * (ti + 1)], dt2[:, 512 * ti:512 * (ti + 1)],
                                 AF.Ln, bias=one_t[0:64, 0:1]), "ss", 1)
        blk.scalar(sc)

    A_sc = s.sb("A_sc", [H, TO], F32)
    rstt = s.sb("rstt", [H, TO], F32)

    def sy3(sx):
        s.inc(sx.dma_start(out=rstt[:], in_=rst_d[:]), "ld", 16)
    blk.sync(sy3)

    def ve5(v):
        s.w(v, "ss")
        s.w(v, "ld")
        s.inc(v.tensor_scalar(A_sc[:], dt2[:], Aa_t[:, 0:1], None, ALU.mult), "vs", 1)
        s.inc(v.tensor_tensor_scan(cs[:], rstt[:], A_sc[:], 0.0, ALU.mult, ALU.add), "vs", 1)
    blk.vector(ve5)

    def sycs(sx):
        s.w(sx, "vs")
        s.inc(sx.dma_start(out=cs_dram[:], in_=cs[:]), "od", 16)
    blk.sync(sycs)
    s.pop()
    s.pop()
    dbg("dt2", dt2[:, :], [H, TO], F32)
    dbg("cs", cs[:, :], [H, TO], F32)
    dbg("B_fm", B_fm[:, :], [NA, TO], BF16)
    if upto == "D":
        return fin()

    s.push()
    csT = s.sb("csT", [128, H * NCH], F32)
    dtT = s.sb("dtT", [128, H * NCH], F32)
    ecT = s.sb("ecT", [128, H * NCH], F32)
    S_in = s.sb("S_in", [NA, E], BF16)
    s.push()
    Scum = s.sb("Scum", [NA, E], F32)
    Scum_bf = s.sb("Scum_bf", [NA, E], BF16)
    exptotT = s.sb("exptotT", [128, H * NCH], F32)
    totT = s.sb("totT", [128, H], F32)
    wcol_all = s.sb("wcol_all", [128, H], F32)
    ecrow = s.sb("ecrow", [1, H], F32)
    etrow = s.sb("etrow", [1, H], F32)
    totrow = s.sb("totrow", [1, H], F32)
    ecrow_exp = s.sb("ecrow_exp", [1, H], F32)
    Xld = s.sb("Xld", [128, 32 * 128], BF16)
    Xtm = s.sb("Xtm", [128, E], BF16)
    Btm = s.sb("Btm", [128, NA], BF16)
    Bd = [s.sb(f"Bd{i}", [128, NA], BF16) for i in range(2)]
    pstf = s.ps("pstf", [128, 512], F32)
    pstb = s.ps("pstb", [128, 128], BF16)
    psS = s.ps("psS", [128, 512], F32)
    pse2 = s.ps("pse2", [128, 192], F32)

    def ve6(v):
        s.inc(v.memset(Scum[:], 0.0), "vs", 1)
        s.inc(v.memset(Scum_bf[:], 0.0), "vs", 1)
        s.inc(v.memset(ecrow[:], 0.0), "vs", 1)
    blk.vector(ve6)

    bd_free = [0, 0]
    psS_free = [0] * 8
    for c in range(NCH):
        c0 = c * CH

        def te(t, c=c, c0=c0):
            s.w(t, "vs")
            s.w(t, "ss")
            s.inc(t.transpose(pstf[:, 0:64], cs[0:H, c0:c0 + CH], identf[0:64, 0:64]), "ts", 1)
            s.inc(t.transpose(pstf[:, 64:128], dt2[0:H, c0:c0 + CH], identf[0:64, 0:64]), "ts", 1)
        blk.tensor(te)

        def syt(sx, c=c, c0=c0):
            s.w(sx, "vs")
            if c == 0:
                s.w(sx, "od")
            s.inc(sx.dma_start(out=totrow[:],
                               in_=cs_dram.ap().rearrange("h t -> t h")[c0 + CH - 1:c0 + CH, 0:H],
                               allow_slow_non_contiguous=True), "ld", 16)
        blk.sync(syt)

        def sc(scx, c=c):
            s.w(scx, "ts")
            s.w(scx, "vs")
            s.w(scx, "ld")
            s.inc(scx.copy(csT[:, H * c:H * (c + 1)], pstf[:, 0:64]), "ss", 1)
            s.inc(scx.copy(dtT[:, H * c:H * (c + 1)], pstf[:, 64:128]), "ss", 1)
            s.inc(scx.activation(etrow[:], totrow[:], AF.Exp), "ss", 1)
            s.inc(scx.activation(ecrow_exp[:], ecrow[:], AF.Exp), "ss", 1)
        blk.scalar(sc)

        def te2(t, c=c):
            s.w(t, "ss")
            s.inc(t.matmul(pse2[:, 0:64], ones1f[:], etrow[:], start=True, stop=True), "ts", 1)
            s.inc(t.matmul(pse2[:, 64:128], ones1f[:], ecrow_exp[:], start=True, stop=True), "ts", 1)
            s.inc(t.matmul(pse2[:, 128:192], ones1f[:], totrow[:], start=True, stop=True), "ts", 1)
        blk.tensor(te2)

        def sc3(scx, c=c):
            s.w(scx, "ts")
            s.inc(scx.copy(exptotT[:, H * c:H * (c + 1)], pse2[:, 0:64]), "ss", 1)
            s.inc(scx.copy(ecT[:, H * c:H * (c + 1)], pse2[:, 64:128]), "ss", 1)
            s.inc(scx.copy(totT[:], pse2[:, 128:192]), "ss", 1)
        blk.scalar(sc3)

        def ve7(v, c=c):
            s.w(v, "ss")
            s.inc(v.tensor_tensor(ecrow[:], ecrow[:], totrow[:], ALU.add), "vs", 1)
            s.inc(v.tensor_tensor(wcol_all[:], totT[:], csT[:, H * c:H * (c + 1)], ALU.subtract), "vs", 1)
        blk.vector(ve7)

        def sc3b(scx, c=c):
            s.w(scx, "vs")
            s.inc(scx.activation(wcol_all[:], wcol_all[:], AF.Exp), "ss", 1)
        blk.scalar(sc3b)

        def ve7b(v, c=c):
            s.w(v, "ss")
            s.inc(v.tensor_tensor(wcol_all[:], wcol_all[:], dtT[:, H * c:H * (c + 1)], ALU.mult), "vs", 1)
        blk.vector(ve7b)

        def gp(g, c=c, c0=c0):
            s.w(g, "vs")
            s.inc(g.dma_start(out=Xld[:].rearrange("p (k m) -> p k m", k=32),
                              in_=X_spill[:, c0:c0 + CH].rearrange("(k p) m -> p k m", p=128)), "gd", 16)
        blk.gpsimd(gp)

        for eg in range(EG):
            def te3(t, c=c, eg=eg):
                if eg == 0:
                    s.w(t, "gd")
                s.w(t, "vs")
                s.inc(t.transpose(pstb[:], Xld[:, 128 * eg:128 * (eg + 1)], identt[:]), "ts", 1)
            blk.tensor(te3)

            def ve8(v, eg=eg):
                s.w(v, "ts")
                s.inc(v.tensor_copy(Xtm[:, 128 * eg:128 * (eg + 1)], pstb[:]), "vs", 1)
            blk.vector(ve8)

        def te4(t, c=c, c0=c0):
            s.w(t, "vs")
            s.inc(t.transpose(pstb[:], B_fm[:, c0:c0 + CH], identt[:]), "ts", 1)
        blk.tensor(te4)

        def ve9(v, c=c):
            s.w(v, "ts")
            s.inc(v.tensor_copy(Btm[:], pstb[:]), "vs", 1)
        blk.vector(ve9)

        def sy4(sx, c=c):
            s.w(sx, "vs")
            s.w(sx, "ss")
            s.inc(sx.dma_start(out=snap_spill[NA * c:NA * (c + 1), :], in_=Scum_bf[:]), "od", 16)
        blk.sync(sy4)
        snap_od = s.c["od"]

        for h in range(H):
            bdi = h % 2
            slot = h % 8

            def ve10(v, c=c, h=h, bdi=bdi):
                s.w(v, "ts", bd_free[bdi])
                s.inc(v.tensor_scalar(Bd[bdi][:], Btm[:], wcol_all[:, h:h + 1], None, ALU.mult), "vs", 1)
            blk.vector(ve10)

            def te5(t, c=c, h=h, bdi=bdi, slot=slot):
                s.w(t, "vs")
                s.w(t, "vs", psS_free[slot])
                s.inc(t.matmul(psS[:, 64 * slot:64 * (slot + 1)], Bd[bdi][:], Xtm[:, 64 * h:64 * (h + 1)],
                               start=True, stop=True), "ts", 1)
                bd_free[bdi] = s.c["ts"]
            blk.tensor(te5)

            def ve11(v, c=c, h=h, slot=slot):
                s.w(v, "ts")
                s.inc(v.scalar_tensor_tensor(Scum[:, 64 * h:64 * (h + 1)], Scum[:, 64 * h:64 * (h + 1)],
                                             exptotT[:, H * c + h:H * c + h + 1],
                                             psS[:, 64 * slot:64 * (slot + 1)], ALU.mult, ALU.add), "vs", 1)
                psS_free[slot] = s.c["vs"]
            blk.vector(ve11)

        def sc4(scx, c=c, snap_od=snap_od):
            s.w(scx, "vs")
            s.w(scx, "od", snap_od)
            s.inc(scx.copy(Scum_bf[:], Scum[:]), "ss", 1)
        blk.scalar(sc4)

    c2s = s.sb("c2s", [NA, E], BF16)

    def ve12(v):
        s.w(v, "ss")
        s.inc(v.tensor_scalar(c2s[:], Scum_bf[:], msend_t[:, 0:1], None, ALU.mult), "vs", 1)
    blk.vector(ve12)

    def gp4(g):
        s.w(g, "vs")
        s.inc(g.dma_start(out=c2_in[:], in_=c2s[:]), "gd", 16)
        s.w(g, "gd")
        s.inc(g.collective_compute("AllGather", ALU.bypass, replica_groups=PAIRS,
                                   ins=[c2_in.ap().opt()], outs=[c2_out.ap().opt()]), "cc", 1)
        s.w(g, "cc")
        s.inc(g.dma_start(out=S_in[:], in_=c2_out[0:NA, :]), "gd", 16)
    blk.gpsimd(gp4)

    def ve13(v):
        s.w(v, "gd")
        s.inc(v.tensor_scalar(S_in[:], S_in[:], rrecv_t[:, 0:1], None, ALU.mult), "vs", 1)
    blk.vector(ve13)
    dbg("Scum", Scum[:, :], [NA, E], F32)
    dbg("S_in", S_in[:, :], [NA, E], BF16)
    s.pop()
    if upto == "E1":
        return fin()

    s.push()
    tril_t = s.sb("tril_t", [CH, CH], F32)
    Dbt = s.sb("Dbt", [128, E], BF16)
    darow = s.sb("darow", [1, E], BF16)
    Xld2 = s.sb("Xld2", [128, 32 * 128], BF16)
    zld2 = s.sb("zld2", [128, 32 * 128], BF16)
    Xtm2 = s.sb("Xtm2", [128, E], BF16)
    ztm2 = s.sb("ztm2", [128, E], BF16)
    snapc = s.sb("snapc", [NA, E], BF16)
    Ytm = s.sb("Ytm", [128, E], F32)
    ytmbf = s.sb("ytmbf", [128, E], BF16)
    ygc = s.sb("ygc", [128, 32 * 128], BF16)
    CBt = s.sb("CBt", [128, CH], F32)
    expcsT2 = s.sb("expcsT2", [128, H], F32)
    Lh = [s.sb(f"Lh{i}", [128, CH], F32) for i in range(2)]
    csrow_q = [s.sb(f"csrq{i}", [1, 16 * CH], F32) for i in range(2)]
    Mch = [s.sb(f"Mch{i}", [128, CH], BF16) for i in range(2)]
    seff = [s.sb(f"seff{i}", [NA, PDIM], BF16) for i in range(2)]
    psY = s.ps("psY", [128, 512], F32)
    psOf = s.ps("psOf", [128, 512], F32)
    psCB = s.ps("psCB", [128, 128], F32)
    psXP = s.ps("psXP", [128, 256], F32)
    pstb2 = s.ps("pstb2", [128, 128], BF16)
    psDb = s.ps("psDb", [128, 512], F32)

    def sy5(sx):
        s.inc(sx.dma_start(out=tril_t[:], in_=tril_d[:]), "ld", 16)
    blk.sync(sy5)

    def gp5d(g):
        s.inc(g.dma_start(out=darow[:], in_=D_a_row[:]), "gd", 16)
    blk.gpsimd(gp5d)
    for st in range(8):
        def te6(t, st=st):
            s.w(t, "ld")
            s.w(t, "gd")
            if st > 0:
                s.w(t, "ss")
            s.inc(t.matmul(psDb[:], ones1[:], darow[0:1, 512 * st:512 * (st + 1)],
                           start=True, stop=True), "ts", 1)
        blk.tensor(te6)

        def sc5(scx, st=st):
            s.w(scx, "ts")
            s.inc(scx.copy(Dbt[:, 512 * st:512 * (st + 1)], psDb[:]), "ss", 1)
        blk.scalar(sc5)

    psXP_free = [0, 0]
    LM_free = [0, 0]
    se_free = [0, 0]
    psY_free = [0] * 8
    csq_free = [0, 0]
    for c in range(NCH):
        c0 = c * CH

        def gp5(g, c=c, c0=c0):
            s.w(g, "vs")
            s.w(g, "ts")
            s.inc(g.dma_start(out=Xld2[:].rearrange("p (k m) -> p k m", k=32),
                              in_=X_spill[:, c0:c0 + CH].rearrange("(k p) m -> p k m", p=128)), "gd", 16)
            s.inc(g.dma_start(out=zld2[:].rearrange("p (k m) -> p k m", k=32),
                              in_=z_spill[:, c0:c0 + CH].rearrange("(k p) m -> p k m", p=128)), "gd", 16)
            s.inc(g.dma_start(out=snapc[:], in_=snap_spill[NA * c:NA * (c + 1), :]), "gd", 16)
        blk.gpsimd(gp5)

        for eg in range(EG):
            def te7(t, c=c, eg=eg):
                if eg == 0:
                    s.w(t, "gd")
                    s.w(t, "ss")
                s.w(t, "vs")
                s.inc(t.transpose(pstb2[:], Xld2[:, 128 * eg:128 * (eg + 1)], identt[:]), "ts", 1)
            blk.tensor(te7)

            def ve15(v, eg=eg):
                s.w(v, "ts")
                s.inc(v.tensor_copy(Xtm2[:, 128 * eg:128 * (eg + 1)], pstb2[:]), "vs", 1)
            blk.vector(ve15)

            def te8(t, c=c, eg=eg):
                s.w(t, "vs")
                s.inc(t.transpose(pstb2[:], zld2[:, 128 * eg:128 * (eg + 1)], identt[:]), "ts", 1)
            blk.tensor(te8)

            def ve16(v, eg=eg):
                s.w(v, "ts")
                s.inc(v.tensor_copy(ztm2[:, 128 * eg:128 * (eg + 1)], pstb2[:]), "vs", 1)
            blk.vector(ve16)

        def te9(t, c=c, c0=c0):
            s.w(t, "vs")
            s.inc(t.matmul(psCB[:], B_fm[:, c0:c0 + CH], C_fm[:, c0:c0 + CH], start=True, stop=True), "ts", 1)
        blk.tensor(te9)

        def ve17(v, c=c):
            s.w(v, "ts")
            s.w(v, "ld")
            s.inc(v.tensor_tensor(CBt[:], psCB[:], tril_t[:], ALU.mult), "vs", 1)
        blk.vector(ve17)

        def sc6(scx, c=c):
            s.w(scx, "vs")
            s.inc(scx.activation(expcsT2[:], csT[:, H * c:H * (c + 1)], AF.Exp), "ss", 1)
        blk.scalar(sc6)

        for h in range(H):
            i2 = h % 2
            slot = h % 8
            if h % 16 == 0:
                q = h // 16
                qi = q % 2

                def syq(sx, c=c, c0=c0, q=q, qi=qi):
                    s.w(sx, "ts", csq_free[qi])
                    s.inc(sx.dma_start(out=csrow_q[qi].ap().rearrange("a (h l) -> a h l", h=16),
                                       in_=cs_dram[16 * q:16 * (q + 1), c0:c0 + CH]
                                       .rearrange("(a h) l -> a h l", a=1)), "ld", 16)
                blk.sync(syq)

            def te10(t, c=c, h=h, i2=i2):
                qi = (h // 16) % 2
                s.w(t, "vs", psXP_free[i2])
                s.w(t, "ss")
                s.w(t, "ld")
                s.inc(t.matmul(psXP[:, 128 * i2:128 * (i2 + 1)], ones1f[:],
                               csrow_q[qi][0:1, CH * (h % 16):CH * (h % 16 + 1)],
                               start=True, stop=True), "ts", 1)
                csq_free[(h // 16) % 2] = s.c["ts"]
            blk.tensor(te10)

            def ve19(v, c=c, h=h, i2=i2):
                s.w(v, "ts")
                s.w(v, "ss", LM_free[i2])
                s.inc(v.tensor_scalar(Lh[i2][:], psXP[:, 128 * i2:128 * (i2 + 1)],
                                      csT[:, H * c + h:H * c + h + 1], 0.0,
                                      ALU.subtract, ALU.min), "vs", 1)
                psXP_free[i2] = s.c["vs"]
            blk.vector(ve19)

            def sc7(scx, c=c, h=h, i2=i2):
                s.w(scx, "vs")
                s.inc(scx.activation(Lh[i2][:], Lh[i2][:], AF.Exp), "ss", 1)
                LM_free[i2] = s.c["ss"]
            blk.scalar(sc7)

            def ve20(v, c=c, h=h, i2=i2):
                s.w(v, "ss")
                s.w(v, "ts", se_free[i2])
                s.inc(v.scalar_tensor_tensor(Mch[i2][:], Lh[i2][:], dtT[:, H * c + h:H * c + h + 1],
                                             CBt[:], ALU.mult, ALU.mult), "vs", 1)
                s.inc(v.scalar_tensor_tensor(seff[i2][:], S_in[:, 64 * h:64 * (h + 1)],
                                             ecT[:, H * c + h:H * c + h + 1],
                                             snapc[:, 64 * h:64 * (h + 1)], ALU.mult, ALU.add), "vs", 1)
            blk.vector(ve20)

            def te11(t, c=c, h=h, i2=i2, slot=slot, c0=c0):
                s.w(t, "vs")
                s.w(t, "vs", psY_free[slot])
                s.inc(t.matmul(psY[:, 64 * slot:64 * (slot + 1)], Mch[i2][:], Xtm2[:, 64 * h:64 * (h + 1)],
                               start=True, stop=True), "ts", 1)
                s.inc(t.matmul(psOf[:, 64 * slot:64 * (slot + 1)], C_fm[:, c0:c0 + CH], seff[i2][:],
                               start=True, stop=True), "ts", 1)
                se_free[i2] = s.c["ts"]
            blk.tensor(te11)

            def ve21(v, c=c, h=h, slot=slot):
                s.w(v, "ts")
                s.inc(v.tensor_scalar(Ytm[:, 64 * h:64 * (h + 1)], psOf[:, 64 * slot:64 * (slot + 1)],
                                      expcsT2[:, h:h + 1], None, ALU.mult), "vs", 1)
                s.inc(v.tensor_tensor(Ytm[:, 64 * h:64 * (h + 1)], Ytm[:, 64 * h:64 * (h + 1)],
                                      psY[:, 64 * slot:64 * (slot + 1)], ALU.add), "vs", 1)
                s.inc(v.scalar_tensor_tensor(Ytm[:, 64 * h:64 * (h + 1)], Xtm2[:, 64 * h:64 * (h + 1)],
                                             Dbt[:, 64 * h:64 * h + 1], Ytm[:, 64 * h:64 * (h + 1)],
                                             ALU.mult, ALU.add), "vs", 1)
                psY_free[slot] = s.c["vs"]
            blk.vector(ve21)

        def ve22(v, c=c):
            s.inc(v.tensor_tensor(ytmbf[:], Ytm[:], ztm2[:], ALU.mult), "vs", 1)
        blk.vector(ve22)

        for eg in range(EG):
            def te12(t, c=c, eg=eg):
                s.w(t, "vs")
                s.inc(t.transpose(pstb2[:], ytmbf[:, 128 * eg:128 * (eg + 1)], identt[:]), "ts", 1)
            blk.tensor(te12)

            def ve23(v, eg=eg):
                s.w(v, "ts")
                s.inc(v.tensor_copy(ygc[:, 128 * eg:128 * (eg + 1)], pstb2[:]), "vs", 1)
            blk.vector(ve23)

        def sy6(sx, c=c, c0=c0):
            s.w(sx, "vs")
            s.inc(sx.dma_start(out=yg_spill[:, c0:c0 + CH].rearrange("(k p) m -> p k m", p=128),
                               in_=ygc[:].rearrange("p (k m) -> p k m", k=32)), "od", 16)
        blk.sync(sy6)
    s.pop()
    s.pop()
    s.pop()
    dbg("yg0", yg_spill[0:128, :], [128, TO], BF16)
    if upto == "E2":
        return fin()

    s.push()
    wbufF = s.sb("wbufF", [128, 32 * 1024], BF16)
    ygk = [s.sb(f"ygk{i}", [128, 512], BF16) for i in range(32)]
    psFb = [s.ps(f"psFb{i}", [128, 512], F32) for i in range(8)]
    yg_free = [0]
    psF_free = [0] * 8
    for dgw in range(2):
        def gpw(g, dgw=dgw):
            s.w(g, "ts")
            s.inc(g.dma_start(out=wbufF[:].rearrange("p (k m) -> p k m", k=32),
                              in_=W_outprojT[:, 1024 * dgw:1024 * (dgw + 1)]
                              .rearrange("(k p) m -> p k m", p=128)), "gd", 16)
        blk.gpsimd(gpw)
        for tt in range(2):
            def gp6(g, tt=tt, dgw=dgw):
                s.w(g, "ts", yg_free[0])
                for k in range(EG):
                    s.inc(g.dma_start(out=ygk[k][:],
                                      in_=yg_spill[128 * k:128 * (k + 1), 512 * tt:512 * (tt + 1)]), "gd", 16)
            blk.gpsimd(gp6)
            for k in range(EG):
                def te13(t, tt=tt, k=k):
                    s.w(t, "gd")
                    if k == 0:
                        for bi in range(8):
                            s.w(t, "vs", psF_free[bi])
                    for dg8 in range(8):
                        s.inc(t.matmul(psFb[dg8][:], wbufF[:, 1024 * k + 128 * dg8:1024 * k + 128 * (dg8 + 1)],
                                       ygk[k][:], start=(k == 0), stop=(k == EG - 1)), "ts", 1)
                    yg_free[0] = s.c["ts"]
                blk.tensor(te13)

            for dg8 in range(8):
                def ve24(v, dgw=dgw, tt=tt, dg8=dg8):
                    s.w(v, "ts")
                    dg = dgw * 8 + dg8
                    s.inc(v.tensor_tensor(resid[dg][:, 512 * tt:512 * (tt + 1)],
                                          resid[dg][:, 512 * tt:512 * (tt + 1)],
                                          psFb[dg8][:], ALU.add), "vs", 1)
                    psF_free[dg8] = s.c["vs"]
                blk.vector(ve24)
    s.pop()
    dbg("resid1", resid[0][:, :], [128, TO], F32)
    if upto == "F":
        return fin()

    def ffn_phase(tag, first):
        s.push()
        ln2 = [s.sb(f"{tag}ln2_{i}", [128, TO], BF16) for i in range(DG)]
        r2 = s.sb(f"{tag}r2", [1, TO], F32)
        n2 = s.sb(f"{tag}n2", [1, TO], F32)
        r2b = s.sb(f"{tag}r2b", [128, TO], F32)
        n2b = s.sb(f"{tag}n2b", [128, TO], F32)
        tmpg = s.sb(f"{tag}tmpg", [128, TO], F32)
        sgw = s.sb(f"{tag}sgw", [128, 16 * 128], BF16)
        s.push()
        ln_stats(f"{tag}l2", resid, TO, r2, n2)
        expand_f32(f"{tag}r2x", r2, TO, lambda a, w: r2b[:, a:a + w])
        expand_f32(f"{tag}n2x", n2, TO, lambda a, w: n2b[:, a:a + w])
        for pg in range(DG):
            def ve(v, pg=pg):
                if pg == 0:
                    s.w(v, "ss")
                s.inc(v.tensor_tensor(tmpg[:], resid[pg][:], r2b[:], ALU.mult), "vs", 1)
                s.inc(v.tensor_tensor(tmpg[:], tmpg[:], n2b[:], ALU.add), "vs", 1)
                s.inc(v.scalar_tensor_tensor(ln2[pg][:], tmpg[:], flnw[:, pg:pg + 1],
                                             flnb[:, pg:pg + 1].broadcast_to([128, TO]),
                                             ALU.mult, ALU.add), "vs", 1)
            blk.vector(ve)

        if first:
            amax = s.sb("amax", [128, DG], F32)
            wraw = s.sb(f"{tag}wraw", [128, 16 * 128], BF16)
            sg_od = [0]
            for mo in range(DG):
                def gp(g, mo=mo):
                    s.w(g, "vs")
                    s.w(g, "ss")
                    s.inc(g.dma_start(out=wraw[:].rearrange("p (k m) -> p k m", k=16),
                                      in_=ffn_WT[:, 128 * mo:128 * (mo + 1)]
                                      .rearrange("(k p) m -> p k m", p=128)), "gd", 16)
                blk.gpsimd(gp)

                def ve25(v, mo=mo):
                    s.w(v, "gd")
                    s.inc(v.tensor_reduce(amax[:, mo:mo + 1], wraw[:], mybir.AxisListType.X,
                                          ALU.max, apply_absolute_value=True), "vs", 1)
                blk.vector(ve25)

                def sc8(scx, mo=mo):
                    s.w(scx, "gd")
                    s.w(scx, "od", sg_od[0])
                    s.inc(scx.activation(sgw[:], wraw[:], AF.Sign), "ss", 1)
                blk.scalar(sc8)

                def sy7(sx, mo=mo):
                    s.w(sx, "ss")
                    s.inc(sx.dma_start(out=signW_spill[:, 128 * mo:128 * (mo + 1)]
                                       .rearrange("(k p) m -> p k m", p=128),
                                       in_=sgw[:].rearrange("p (k m) -> p k m", k=16)), "od", 16)
                    sg_od[0] = s.c["od"]
                blk.sync(sy7)
            am1 = s.sb("am1", [128, 1], F32)
            amaxc = s.sb("amaxc", [128, DG], F32)
            amr = s.sb("amr", [1, 128], F32)
            amrc = s.sb("amrc", [1, 128], F32)
            ams = s.sb("ams", [1, 1], F32)
            psam = s.ps("psam", [128, 128], F32)

            def sc26(scx):
                s.w(scx, "vs")
                s.inc(scx.copy(amaxc[:], amax[:]), "ss", 1)
            blk.scalar(sc26)

            def ve26(v):
                s.w(v, "ss")
                s.inc(v.tensor_reduce(am1[:], amaxc[:], mybir.AxisListType.X, ALU.max), "vs", 1)
            blk.vector(ve26)

            def te14(t):
                s.w(t, "vs")
                s.inc(t.transpose(psam[0:1, 0:128], am1[:], identf[:]), "ts", 1)
            blk.tensor(te14)

            def ve27a(v):
                s.w(v, "ts")
                s.inc(v.tensor_copy(amr[:], psam[0:1, 0:128]), "vs", 1)
            blk.vector(ve27a)

            def sc27(scx):
                s.w(scx, "vs")
                s.inc(scx.copy(amrc[:], amr[:]), "ss", 1)
            blk.scalar(sc27)

            def ve27(v):
                s.w(v, "ss")
                s.inc(v.tensor_reduce(ams[:], amrc[:], mybir.AxisListType.X, ALU.max), "vs", 1)
            blk.vector(ve27)

            def te15(t):
                s.w(t, "vs")
                s.inc(t.matmul(psam[:, 0:1], ones1f[:], ams[:], start=True, stop=True), "ts", 1)
            blk.tensor(te15)

            def ve28(v):
                s.w(v, "ts")
                s.inc(v.tensor_scalar(gscale[:], gscale[:], psam[:, 0:1], None, ALU.mult), "vs", 1)
            blk.vector(ve28)

        s.pop()
        if first:
            dbg(tag + "amax", amax[:, :], [128, DG], F32)
            dbg(tag + "am1", am1[:, :], [128, 1], F32)
            dbg(tag + "ams", ams[:, :], [1, 1], F32)
        dbg(tag + "gsc", gscale[:, :], [128, DG], F32)
        dbg(tag + "rstd", r2[0:1, :], [1, TO], F32)
        dbg(tag + "nmr", n2[0:1, :], [1, TO], F32)
        dbg(tag + "ln2", ln2[0][:, :], [128, TO], BF16)
        psG = [s.ps(f"{tag}psG{i}", [128, 512], F32) for i in range(8)]
        bank_G = [0] * 8
        sg_free = [0]
        for dg in range(DG):
            def gp7(g, dg=dg):
                s.w(g, "ts", sg_free[0])
                if dg == 0:
                    s.w(g, "od")
                    s.w(g, "ss")
                s.inc(g.dma_start(out=sgw[:].rearrange("p (k m) -> p k m", k=16),
                                  in_=signW_spill[:, 128 * dg:128 * (dg + 1)]
                                  .rearrange("(k p) m -> p k m", p=128)), "gd", 16)
            blk.gpsimd(gp7)
            bids = [(2 * dg) % 8, (2 * dg + 1) % 8]

            def te16(t, dg=dg, bids=bids):
                s.w(t, "gd")
                if dg == 0:
                    s.w(t, "vs")
                for bi in bids:
                    s.w(t, "vs", bank_G[bi])
                for k in range(DG):
                    for ti in range(2):
                        s.inc(t.matmul(psG[bids[ti]][:], sgw[:, 128 * k:128 * (k + 1)],
                                       ln2[k][:, 512 * ti:512 * (ti + 1)],
                                       start=(k == 0), stop=(k == DG - 1)), "ts", 1)
                sg_free[0] = s.c["ts"]
            blk.tensor(te16)

            def ve29(v, dg=dg, bids=bids):
                s.w(v, "ts")
                for ti in range(2):
                    s.inc(v.scalar_tensor_tensor(tmpg[:, 0:512], psG[bids[ti]][:], gscale[:, dg:dg + 1],
                                                 fbeta[:, dg:dg + 1].broadcast_to([128, 512]),
                                                 ALU.mult, ALU.add), "vs", 1)
                    s.inc(v.tensor_tensor(resid[dg][:, 512 * ti:512 * (ti + 1)],
                                          resid[dg][:, 512 * ti:512 * (ti + 1)],
                                          tmpg[:, 0:512], ALU.add), "vs", 1)
                for bi in bids:
                    bank_G[bi] = s.c["vs"]
            blk.vector(ve29)
        s.pop()

    ffn_phase("g", True)
    dbg("resid2", resid[0][:, :], [128, TO], F32)
    if upto == "G":
        return fin()

    s.push()
    rstd3_bt = s.sb("rstd3_bt", [128, TO], F32)
    nmr3_bt = s.sb("nmr3_bt", [128, TO], F32)
    xcb = [s.sb(f"xcb{eg}", [128, TO], BF16) for eg in range(EGL)]

    s.push()
    rn = [s.sb(f"rn{i}", [128, TO], BF16) for i in range(DG)]
    rn_warm = [s.sb(f"rnw{i}", [128, WARM], BF16) for i in range(DG)]
    s.push()
    r3 = s.sb("r3", [1, TO], F32)
    n3 = s.sb("n3", [1, TO], F32)
    ln_stats("l3", resid, TO, r3, n3)
    expand_f32("r3x", r3, TO, lambda a, w: rstd3_bt[:, a:a + w])
    expand_f32("n3x", n3, TO, lambda a, w: nmr3_bt[:, a:a + w])
    ln3t = s.sb("ln3t", [128, TO], F32)
    rr3 = s.sb("rr3", [1, TO], F32)
    rr3b = s.sb("rr3b", [128, TO], F32)
    sq3 = s.sb("sq3", [128, 512], F32)
    ps3 = s.ps("ps3", [1, 512], F32)
    for st in range(2):
        c0 = st * 512
        for pg in range(DG):
            def ve(v, pg=pg, st=st, c0=c0):
                s.w(v, "ss")
                s.inc(v.tensor_tensor(ln3t[:, 0:512], resid[pg][:, c0:c0 + 512],
                                      rstd3_bt[:, c0:c0 + 512], ALU.mult), "vs", 1)
                s.inc(v.tensor_tensor(ln3t[:, 0:512], ln3t[:, 0:512],
                                      nmr3_bt[:, c0:c0 + 512], ALU.add), "vs", 1)
            blk.vector(ve)

            def scq(scx, pg=pg):
                s.w(scx, "vs")
                if pg > 0:
                    s.w(scx, "ts")
                s.inc(scx.activation(sq3[:], ln3t[:, 0:512], AF.Square), "ss", 1)
            blk.scalar(scq)

            def te(t, pg=pg):
                s.w(t, "ss")
                s.inc(t.matmul(ps3[:], onescol[:], sq3[:], start=(pg == 0), stop=(pg == DG - 1)), "ts", 1)
            blk.tensor(te)

        def ve30(v, c0=c0):
            s.w(v, "ts")
            s.inc(v.tensor_scalar(rr3[:, c0:c0 + 512], ps3[:], 1.0 / D, None, ALU.mult), "vs", 1)
        blk.vector(ve30)

    def sc9(scx):
        s.w(scx, "vs")
        s.inc(scx.activation(rr3[:], rr3[:], AF.Ln, bias=eps_t[0:1, 0:1]), "ss", 1)
        s.inc(scx.activation(rr3[:], rr3[:], AF.Exp, scale=-0.5), "ss", 1)
    blk.scalar(sc9)
    expand_f32("rr3x", rr3, TO, lambda a, w: rr3b[:, a:a + w])
    for pg in range(DG):
        def ve(v, pg=pg):
            if pg == 0:
                s.w(v, "ss")
            s.inc(v.tensor_tensor(ln3t[:], resid[pg][:], rstd3_bt[:], ALU.mult), "vs", 1)
            s.inc(v.tensor_tensor(ln3t[:], ln3t[:], nmr3_bt[:], ALU.add), "vs", 1)
            s.inc(v.tensor_tensor(ln3t[:], ln3t[:], rr3b[:], ALU.mult), "vs", 1)
            s.inc(v.tensor_scalar(rn[pg][:], ln3t[:], nrmw[:, pg:pg + 1], None, ALU.mult), "vs", 1)
        blk.vector(ve)
    dbg("hr3", r3[0:1, :], [1, TO], F32)
    dbg("hn3", n3[0:1, :], [1, TO], F32)
    dbg("hrr3", rr3[0:1, :], [1, TO], F32)
    s.pop()
    dbg("rn", rn[0][:, :], [128, TO], BF16)

    def gp8(g):
        s.w(g, "vs")
        for i in range(DG):
            s.inc(g.dma_start(out=c1_in[128 * i:128 * (i + 1), :], in_=rn[i][:]), "gd", 16)
        s.w(g, "gd")
        s.inc(g.collective_compute("AllGather", ALU.bypass, replica_groups=PAIRS,
                                   ins=[c1_in.ap().opt()], outs=[c1_out.ap().opt()]), "cc", 1)
        s.w(g, "cc")
        for i in range(DG):
            s.inc(g.dma_start(out=rn_warm[i][:], in_=c1_out[128 * i:128 * (i + 1), TO - WARM:TO]), "gd", 16)
    blk.gpsimd(gp8)

    def ve31(v):
        s.w(v, "gd")
        for i in range(DG):
            s.inc(v.tensor_scalar(rn_warm[i][:], rn_warm[i][:], rrecv_t[:, 0:1], None, ALU.mult), "vs", 1)
    blk.vector(ve31)

    s.push()
    rnh = [s.sb(f"rnh{i}", [128, TO], BF16) for i in range(DG)]
    wbufH = [s.sb(f"wbufH{i}", [128, 16 * 128], BF16) for i in range(2)]
    sktile = s.sb("sktile", [128, TO], BF16)
    psH = [s.ps(f"psH{i}", [128, 512], F32) for i in range(8)]
    bank_H = [0] * 8
    w_H = [0, 0]
    sk_od = [0]
    for thalf in range(2):
        def gph(g, thalf=thalf):
            s.w(g, "cc")
            if thalf > 0:
                s.w(g, "ts")
            for i in range(DG):
                s.inc(g.dma_start(out=rnh[i][:],
                                  in_=c1_out[thalf * D + 128 * i:thalf * D + 128 * (i + 1), :]), "gd", 16)
        blk.gpsimd(gph)
        for eg in range(EGL):
            wb = wbufH[eg % 2]

            def gp(g, eg=eg, wb=wb, thalf=thalf):
                s.w(g, "ts", w_H[eg % 2])
                s.inc(g.dma_start(out=wb[:].rearrange("p (k m) -> p k m", k=16),
                                  in_=W_skip[:, 128 * eg:128 * (eg + 1)]
                                  .rearrange("(k p) m -> p k m", p=128)), "gd", 16)
            blk.gpsimd(gp)
            bids = [(2 * eg) % 8, (2 * eg + 1) % 8]

            def te(t, eg=eg, wb=wb, bids=bids):
                s.w(t, "gd")
                for bi in bids:
                    s.w(t, "ss", bank_H[bi])
                for k in range(DG):
                    for ti in range(2):
                        s.inc(t.matmul(psH[bids[ti]][:], wb[:, 128 * k:128 * (k + 1)],
                                       rnh[k][:, 512 * ti:512 * (ti + 1)],
                                       start=(k == 0), stop=(k == DG - 1)), "ts", 1)
                w_H[eg % 2] = s.c["ts"]
            blk.tensor(te)

            def sc(scx, eg=eg, bids=bids):
                s.w(scx, "ts")
                s.w(scx, "od", sk_od[0])
                for ti in range(2):
                    s.inc(scx.activation(sktile[:, 512 * ti:512 * (ti + 1)], psH[bids[ti]][:], AF.Silu), "ss", 1)
                for bi in bids:
                    bank_H[bi] = s.c["ss"]
            blk.scalar(sc)

            def sy(sx, eg=eg, thalf=thalf):
                s.w(sx, "ss")
                s.inc(sx.dma_start(out=skip_spill[128 * eg:128 * (eg + 1), thalf * TO:(thalf + 1) * TO],
                                   in_=sktile[:]), "od", 16)
                sk_od[0] = s.c["od"]
            blk.sync(sy)
    s.pop()

    s.push()
    s.push()
    xcbh_t = [s.sb(f"xcbht{i}", [128, TO], BF16) for i in range(2)]
    wbufI = [s.sb(f"wbufI{i}", [128, 16 * 128], BF16) for i in range(2)]
    xin_eg = [s.sb(f"xin{i}", [128, WARM + TO], BF16) for i in range(2)]
    cacc2 = s.sb("cacc2", [128, TO], F32)
    psI = [s.ps(f"psI{i}", [128, 512], F32) for i in range(6)]
    bank_I = [0] * 6
    w_I = [0, 0]
    xin_I = [0, 0]
    xcbh_od = [0, 0]
    for eg in range(EG):
        wb = wbufI[eg % 2]
        xie = xin_eg[eg % 2]

        def gp(g, eg=eg, wb=wb):
            s.w(g, "ts", w_I[eg % 2])
            s.inc(g.dma_start(out=wb[:].rearrange("p (k m) -> p k m", k=16),
                              in_=W_in[:, 128 * eg:128 * (eg + 1)]
                              .rearrange("(k p) m -> p k m", p=128)), "gd", 16)
        blk.gpsimd(gp)
        bids = [(3 * eg) % 6, (3 * eg + 1) % 6, (3 * eg + 2) % 6]

        def te(t, eg=eg, wb=wb, bids=bids):
            s.w(t, "gd")
            for bi in bids:
                s.w(t, "ss", bank_I[bi])
            for k in range(DG):
                s.inc(t.matmul(psI[bids[0]][:, 0:WARM], wb[:, 128 * k:128 * (k + 1)], rn_warm[k][:],
                               start=(k == 0), stop=(k == DG - 1)), "ts", 1)
                for ti in range(2):
                    s.inc(t.matmul(psI[bids[1 + ti]][:], wb[:, 128 * k:128 * (k + 1)],
                                   rn[k][:, 512 * ti:512 * (ti + 1)],
                                   start=(k == 0), stop=(k == DG - 1)), "ts", 1)
            w_I[eg % 2] = s.c["ts"]
        blk.tensor(te)

        def sc(scx, eg=eg, bids=bids, xie=xie):
            s.w(scx, "ts")
            s.w(scx, "vs", xin_I[eg % 2])
            s.inc(scx.copy(xie[:, 0:WARM], psI[bids[0]][:, 0:WARM]), "ss", 1)
            for ti in range(2):
                s.inc(scx.copy(xie[:, WARM + 512 * ti:WARM + 512 * (ti + 1)], psI[bids[1 + ti]][:]), "ss", 1)
            for bi in bids:
                bank_I[bi] = s.c["ss"]
        blk.scalar(sc)

        def ve(v, eg=eg, xie=xie):
            s.w(v, "ss")
            s.inc(v.tensor_scalar(cacc2[:], xie[:, WARM - 3:WARM - 3 + TO],
                                  cbw[:, 4 * eg:4 * eg + 1], None, ALU.mult), "vs", 1)
            for k in range(1, 4):
                s.inc(v.scalar_tensor_tensor(cacc2[:], xie[:, WARM - 3 + k:WARM - 3 + k + TO],
                                             cbw[:, 4 * eg + k:4 * eg + k + 1], cacc2[:],
                                             ALU.mult, ALU.add), "vs", 1)
            xin_I[eg % 2] = s.c["vs"]
        blk.vector(ve)

        if eg < EGL:
            def sc2(scx, eg=eg):
                s.w(scx, "vs")
                s.inc(scx.activation(xcb[eg][:], cacc2[:], AF.Silu, bias=cbb[:, eg:eg + 1]), "ss", 1)
            blk.scalar(sc2)
        else:
            def sc2(scx, eg=eg):
                s.w(scx, "vs")
                s.w(scx, "od", xcbh_od[eg % 2])
                s.inc(scx.activation(xcbh_t[eg % 2][:], cacc2[:], AF.Silu, bias=cbb[:, eg:eg + 1]), "ss", 1)
            blk.scalar(sc2)

            def syh(sx, eg=eg):
                s.w(sx, "ss")
                s.inc(sx.dma_start(out=c3b_in[128 * (eg - EGL):128 * (eg - EGL + 1), :],
                                   in_=xcbh_t[eg % 2][:]), "od", 16)
                xcbh_od[eg % 2] = s.c["od"]
            blk.sync(syh)
    s.pop()
    dbg("xcb0", xcb[0][:, :], [128, TO], BF16)

    wd1 = s.sb("wd1", [128, 32 * 128], BF16)
    wBt = s.sb("wBt", [128, 32 * 16], BF16)
    wCt = s.sb("wCt", [128, 32 * 16], BF16)
    d1t = s.sb("d1t", [128, TO], BF16)
    B2t = s.sb("B2t", [NB, TO], F32)
    C2tt = s.sb("C2tt", [NB, TO], F32)
    psJ = [s.ps(f"psJ{i}", [128, 512], F32) for i in range(6)]

    def gp9(g):
        s.inc(g.dma_start(out=wd1[:].rearrange("p (k m) -> p k m", k=32),
                          in_=W_d1.ap().rearrange("(k p) m -> p k m", p=128)), "gd", 16)
        s.inc(g.dma_start(out=wBt[:].rearrange("p (k m) -> p k m", k=32),
                          in_=W_Bp.ap().rearrange("(k p) m -> p k m", p=128)), "gd", 16)
        s.inc(g.dma_start(out=wCt[:].rearrange("p (k m) -> p k m", k=32),
                          in_=W_Cp.ap().rearrange("(k p) m -> p k m", p=128)), "gd", 16)
    blk.gpsimd(gp9)

    xhk = [s.sb(f"xhk{i}", [128, TO], BF16) for i in range(2)]
    xhk_free = [0, 0]
    for ti in range(2):
        for k in range(EG):
            if k >= EGL:
                def gph2(g, k=k, ti=ti):
                    s.w(g, "ts", xhk_free[k % 2])
                    if k == EGL and ti == 0:
                        s.w(g, "od")
                    s.inc(g.dma_start(out=xhk[k % 2][:],
                                      in_=c3b_in[128 * (k - EGL):128 * (k - EGL + 1), :]), "gd", 16)
                blk.gpsimd(gph2)

            def te17(t, k=k, ti=ti):
                s.w(t, "gd")
                if k == 0 and ti == 0:
                    s.w(t, "ss")
                rhs = xcb[k][:, 512 * ti:512 * (ti + 1)] if k < EGL \
                    else xhk[k % 2][:, 512 * ti:512 * (ti + 1)]
                s.inc(t.matmul(psJ[ti][:], wd1[:, 128 * k:128 * (k + 1)], rhs,
                               start=(k == 0), stop=(k == EG - 1)), "ts", 1)
                s.inc(t.matmul(psJ[2 + ti][0:16, :], wBt[:, 16 * k:16 * (k + 1)], rhs,
                               start=(k == 0), stop=(k == EG - 1)), "ts", 1)
                s.inc(t.matmul(psJ[4 + ti][0:16, :], wCt[:, 16 * k:16 * (k + 1)], rhs,
                               start=(k == 0), stop=(k == EG - 1)), "ts", 1)
                if k >= EGL:
                    xhk_free[k % 2] = s.c["ts"]
            blk.tensor(te17)

    def sc10(scx):
        s.w(scx, "ts")
        for ti in range(2):
            s.inc(scx.copy(d1t[:, 512 * ti:512 * (ti + 1)], psJ[ti][:]), "ss", 1)
            s.inc(scx.copy(B2t[:, 512 * ti:512 * (ti + 1)], psJ[2 + ti][0:16, :]), "ss", 1)
            s.inc(scx.copy(C2tt[:, 512 * ti:512 * (ti + 1)], psJ[4 + ti][0:16, :]), "ss", 1)
    blk.scalar(sc10)

    wd2 = s.sb("wd2", [128, 32 * 128], BF16)
    bdc = s.sb("bdc", [128, EG], F32)
    dtile = [s.sb(f"dtile{i}", [128, TO], F32) for i in range(2)]

    def gp10(g):
        s.w(g, "ss")
        s.inc(g.dma_start(out=wd2[:], in_=W_d2[:]), "gd", 16)
    blk.gpsimd(gp10)

    def sy9(sx):
        for eg in range(EG):
            s.inc(sx.dma_start(out=bdc[:, eg:eg + 1], in_=b_d2[128 * eg:128 * (eg + 1), :]), "ld", 16)
    blk.sync(sy9)
    dt_free = [0, 0]
    psJ_free = [0] * 6
    for mo in range(EG):
        dte = dtile[mo % 2]
        bids = [(2 * mo) % 6, (2 * mo + 1) % 6]

        def te18(t, mo=mo, bids=bids):
            if mo == 0:
                s.w(t, "gd")
                s.w(t, "ld")
            for bi in bids:
                s.w(t, "ss", psJ_free[bi])
            for ti in range(2):
                s.inc(t.matmul(psJ[bids[ti]][:], wd2[:, 128 * mo:128 * (mo + 1)],
                               d1t[:, 512 * ti:512 * (ti + 1)], start=True, stop=True), "ts", 1)
        blk.tensor(te18)

        def sc11(scx, mo=mo, bids=bids, dte=dte):
            s.w(scx, "ts")
            s.w(scx, "od", dt_free[mo % 2])
            for ti in range(2):
                s.inc(scx.activation(dte[:, 512 * ti:512 * (ti + 1)], psJ[bids[ti]][:],
                                     AF.Exp, bias=bdc[:, mo:mo + 1]), "ss", 1)
                s.inc(scx.activation(dte[:, 512 * ti:512 * (ti + 1)], dte[:, 512 * ti:512 * (ti + 1)],
                                     AF.Ln, bias=one_t[:, 0:1]), "ss", 1)
            for bi in bids:
                psJ_free[bi] = s.c["ss"]
        blk.scalar(sc11)

        def sy10(sx, mo=mo, dte=dte):
            s.w(sx, "ss")
            s.inc(sx.dma_start(out=delta_spill[128 * mo:128 * (mo + 1), :], in_=dte[:]), "od", 16)
            dt_free[mo % 2] = s.c["od"]
        blk.sync(sy10)

    def gp11(g):
        s.w(g, "od")
        s.inc(g.dma_start(out=c3f_in[0:EL, :], in_=delta_spill[EL:E, :]), "gd", 16)
    blk.gpsimd(gp11)

    def sy11(sx):
        s.w(sx, "ss")
        s.inc(sx.dma_start(out=c3f_in[EL:EL + NB, :], in_=B2t[:]), "od", 16)
        s.inc(sx.dma_start(out=c3f_in[EL + NB:EL + 2 * NB, :], in_=C2tt[:]), "od", 16)
    blk.sync(sy11)

    def gp12(g):
        s.w(g, "od")
        s.w(g, "gd")
        s.inc(g.collective_compute("AllGather", ALU.bypass, replica_groups=PAIRS,
                                   ins=[c3f_in.ap().opt()], outs=[c3f_out.ap().opt()]), "cc", 1)
        s.w(g, "cc")
        s.inc(g.collective_compute("AllGather", ALU.bypass, replica_groups=PAIRS,
                                   ins=[c3b_in.ap().opt()], outs=[c3b_out.ap().opt()]), "cc", 1)
    blk.gpsimd(gp12)

    B_full = s.sb("B_full", [NB, T], BF16)
    C_full = s.sb("C_full", [NB, T], BF16)

    def gp13(g):
        s.w(g, "cc")
        s.inc(g.dma_start(out=B_full[:, 0:TO], in_=c3f_out[EL:EL + NB, :]), "gd", 16)
        s.inc(g.dma_start(out=B_full[:, TO:T], in_=c3f_out[NF3 + EL:NF3 + EL + NB, :]), "gd", 16)
        s.inc(g.dma_start(out=C_full[:, 0:TO], in_=c3f_out[EL + NB:EL + 2 * NB, :]), "gd", 16)
        s.inc(g.dma_start(out=C_full[:, TO:T], in_=c3f_out[NF3 + EL + NB:NF3 + EL + 2 * NB, :]), "gd", 16)
    blk.gpsimd(gp13)

    def sy12(sx):
        s.w(sx, "gd")
        s.inc(sx.dma_start(out=Brow_dram[:], in_=B_full[:]), "od", 16)
        s.inc(sx.dma_start(out=Crow_dram[:], in_=C_full[:]), "od", 16)
    blk.sync(sy12)
    s.pop()
    s.pop()

    s.push()
    Brow_n = [s.sb(f"Brow_n{i}", [1, T], BF16) for i in range(2)]
    Crow_n = [s.sb(f"Crow_n{i}", [1, T], BF16) for i in range(2)]
    brow_free = [0, 0]
    dsc = s.sb("dsc", [128, T], F32)
    xsc = s.sb("xsc", [128, T], BF16)
    wsc = s.sb("wsc", [128, T], F32)
    d_loc = s.sb("d_loc", [128, TO], F32)
    d_s0 = s.sb("d_s0", [128, TO], F32)
    d_s1 = s.sb("d_s1", [128, TO], F32)
    x_s0 = s.sb("x_s0", [128, TO], BF16)
    x_s1 = s.sb("x_s1", [128, TO], BF16)
    a_n = s.sb("a_n", [128, T], F32)
    u_n = s.sb("u_n", [128, T], F32)
    h_n = s.sb("h_n", [128, T], F32)
    y_eg = s.sb("y_eg", [128, T], F32)
    sk_eg = s.sb("sk_eg", [128, T], BF16)
    ysk_eg = s.sb("ysk_eg", [128, T], BF16)
    psBt = s.ps("psBt", [128, 2048], F32)
    psCt = s.ps("psCt", [128, 2048], F32)
    ysk_od2 = [0]
    for eg in range(EGL):
        def gp(g, eg=eg):
            s.w(g, "vs")
            s.inc(g.dma_start(out=d_loc[:], in_=delta_spill[128 * eg:128 * (eg + 1), :]), "gd", 16)
            s.inc(g.dma_start(out=d_s0[:], in_=c3f_out[128 * eg:128 * (eg + 1), :]), "gd", 16)
            s.inc(g.dma_start(out=d_s1[:], in_=c3f_out[NF3 + 128 * eg:NF3 + 128 * (eg + 1), :]), "gd", 16)
            s.inc(g.dma_start(out=x_s0[:], in_=c3b_out[128 * eg:128 * (eg + 1), :]), "gd", 16)
            s.inc(g.dma_start(out=x_s1[:], in_=c3b_out[EL + 128 * eg:EL + 128 * (eg + 1), :]), "gd", 16)
            s.inc(g.dma_start(out=sk_eg[:], in_=skip_spill[128 * eg:128 * (eg + 1), :]), "gd", 16)
        blk.gpsimd(gp)

        def ve(v, eg=eg):
            s.w(v, "gd")
            mask = selr0_t[:, 0:1].broadcast_to([128, TO])
            s.inc(v.select(dsc[:, 0:TO], mask, d_loc[:], d_s0[:]), "vs", 1)
            s.inc(v.select(dsc[:, TO:T], mask, d_s1[:], d_loc[:]), "vs", 1)
            s.inc(v.select(xsc[:, 0:TO], mask, xcb[eg][:], x_s0[:]), "vs", 1)
            s.inc(v.select(xsc[:, TO:T], mask, x_s1[:], xcb[eg][:]), "vs", 1)
            s.inc(v.tensor_tensor(wsc[:], dsc[:], xsc[:], ALU.mult), "vs", 1)
        blk.vector(ve)
        for n in range(NB):
            n2i = n % 2

            def gpn(g, eg=eg, n=n, n2i=n2i):
                s.w(g, "ts", brow_free[n2i])
                if eg == 0 and n == 0:
                    s.w(g, "od")
                s.inc(g.dma_start(out=Brow_n[n2i][:], in_=Brow_dram[n:n + 1, :]), "gd", 16)
                s.inc(g.dma_start(out=Crow_n[n2i][:], in_=Crow_dram[n:n + 1, :]), "gd", 16)
            blk.gpsimd(gpn)

            def te19(t, eg=eg, n=n, n2i=n2i):
                s.w(t, "gd")
                s.w(t, "vs")
                for q4 in range(4):
                    s.inc(t.matmul(psBt[:, 512 * q4:512 * (q4 + 1)], ones1[:],
                                   Brow_n[n2i][0:1, 512 * q4:512 * (q4 + 1)],
                                   start=True, stop=True), "ts", 1)
                    s.inc(t.matmul(psCt[:, 512 * q4:512 * (q4 + 1)], ones1[:],
                                   Crow_n[n2i][0:1, 512 * q4:512 * (q4 + 1)],
                                   start=True, stop=True), "ts", 1)
                brow_free[n2i] = s.c["ts"]
            blk.tensor(te19)

            def sc12(scx, eg=eg, n=n):
                s.w(scx, "vs")
                s.inc(scx.activation(a_n[:], dsc[:], AF.Exp,
                                     scale=Ab_t[:, NB * eg + n:NB * eg + n + 1]), "ss", 1)
            blk.scalar(sc12)

            def ve32(v, eg=eg, n=n):
                s.w(v, "ts")
                s.w(v, "ss")
                s.inc(v.tensor_tensor(u_n[:], wsc[:], psBt[:], ALU.mult), "vs", 1)
                s.inc(v.tensor_tensor_scan(h_n[:], a_n[:], u_n[:], 0.0, ALU.mult, ALU.add), "vs", 1)
                if n == 0:
                    s.inc(v.tensor_tensor(y_eg[:], h_n[:], psCt[:], ALU.mult), "vs", 1)
                else:
                    s.inc(v.tensor_tensor(h_n[:], h_n[:], psCt[:], ALU.mult), "vs", 1)
                    s.inc(v.tensor_tensor(y_eg[:], y_eg[:], h_n[:], ALU.add), "vs", 1)
            blk.vector(ve32)

        def ve33(v, eg=eg):
            s.w(v, "od", ysk_od2[0])
            s.inc(v.scalar_tensor_tensor(y_eg[:], xsc[:], wdcol[:, eg:eg + 1], y_eg[:],
                                         ALU.mult, ALU.add), "vs", 1)
            s.inc(v.tensor_tensor(ysk_eg[:], y_eg[:], sk_eg[:], ALU.mult), "vs", 1)
        blk.vector(ve33)

        def sy13(sx, eg=eg):
            s.w(sx, "vs")
            s.inc(sx.dma_start(out=ysk_spill[128 * eg:128 * (eg + 1), :], in_=ysk_eg[:]), "od", 16)
            ysk_od2[0] = s.c["od"]
        blk.sync(sy13)
    s.pop()
    dbg("delta0", delta_spill[0:128, :], [128, TO], F32)
    dbg("ysk5", ysk_spill[128 * 5:128 * 6, :], [128, T], BF16)
    dbg("ysk15", ysk_spill[128 * 15:128 * 16, :], [128, T], BF16)
    dbg("skip0", skip_spill[0:128, :], [128, T], BF16)
    dbg("ysk0", ysk_spill[0:128, :], [128, T], BF16)
    dbg("ds0", c3f_out[0:128, :], [128, TO], F32)
    dbg("ds1", c3f_out[NF3:NF3 + 128, :], [128, TO], F32)

    s.push()
    wbufO = s.sb("wbufO", [128, 16 * 8 * 128], BF16)
    yskt = [s.sb(f"yskt{i}", [128, 512], BF16) for i in range(16)]
    otile = s.sb("otile", [128, 512], F32)
    dbg_ps = s.sb("dbg_ps", [128, 512], F32)
    dbg_yt = s.sb("dbg_yt", [128, 512], BF16)
    dbg_wo = s.sb("dbg_wo", [128, 128], BF16)
    psO2 = [s.ps(f"psO2_{i}", [128, 512], F32) for i in range(8)]
    ysk_free = [0, 0]
    psO2_free = [0] * 8
    for dgw in range(2):
        def gpw2(g, dgw=dgw):
            s.w(g, "ts")
            s.inc(g.dma_start(out=wbufO[:].rearrange("p (k m) -> p k m", k=16),
                              in_=W_out_in[0:EL, 1024 * dgw:1024 * (dgw + 1)]
                              .rearrange("(k p) m -> p k m", p=128)), "gd", 16)
        blk.gpsimd(gpw2)
        for tt in range(4):
            def gp15(g, tt=tt, dgw=dgw):
                s.w(g, "ts", ysk_free[0])
                for k in range(EGL):
                    s.inc(g.dma_start(out=yskt[k][:],
                                      in_=ysk_spill[128 * k:128 * (k + 1), 512 * tt:512 * (tt + 1)]), "gd", 16)
            blk.gpsimd(gp15)
            for k in range(EGL):
                def te20(t, tt=tt, k=k, dgw=dgw):
                    s.w(t, "gd")
                    if k == 0:
                        for bi in range(8):
                            s.w(t, "vs", psO2_free[bi])
                    for dg8 in range(8):
                        s.inc(t.matmul(psO2[dg8][:], wbufO[:, 1024 * k + 128 * dg8:1024 * k + 128 * (dg8 + 1)],
                                       yskt[k][:], start=(k == 0), stop=(k == EGL - 1)), "ts", 1)
                    ysk_free[0] = s.c["ts"]
                blk.tensor(te20)
                if dgw == 0 and tt == 0 and k == 0:
                    def vdbg(v):
                        s.w(v, "ts")
                        s.inc(v.tensor_copy(dbg_yt[:], yskt[0][:]), "vs", 1)
                        s.inc(v.tensor_copy(dbg_wo[:], wbufO[:, 0:128]), "vs", 1)
                    blk.vector(vdbg)
            for dg8 in range(8):
                def ve34(v, dgw=dgw, tt=tt, dg8=dg8):
                    s.w(v, "ts")
                    s.w(v, "od")
                    s.inc(v.tensor_copy(otile[:], psO2[dg8][:]), "vs", 1)
                    if dgw == 0 and tt == 0 and dg8 == 0:
                        s.inc(v.tensor_copy(dbg_ps[:], psO2[0][:]), "vs", 1)
                    psO2_free[dg8] = s.c["vs"]
                blk.vector(ve34)

                def sy14(sx, dgw=dgw, tt=tt, dg8=dg8):
                    s.w(sx, "vs")
                    dg = dgw * 8 + dg8
                    j = tt % 2
                    half = 0 if tt < 2 else 1
                    s.inc(sx.dma_start(out=c4_in[j][half * D + 128 * dg:half * D + 128 * (dg + 1), :],
                                       in_=otile[:]), "od", 16)
                blk.sync(sy14)
    dbg("h6ps", dbg_ps[:, :], [128, 512], F32)
    dbg("h6yt", dbg_yt[:, :], [128, 512], BF16)
    dbg("h6wo", dbg_wo[:, :], [128, 128], BF16)
    dbg("c4pre", c4_in[0][0:128, :], [128, 512], F32)
    dbg("c4preb", c4_in[0][D:D + 128, :], [128, 512], F32)
    for j in range(2):
        def gp16(g, j=j):
            s.w(g, "od")
            s.w(g, "cc")
            s.inc(g.collective_compute("ReduceScatter", ALU.add, replica_groups=PAIRS,
                                       ins=[c4_in[j].ap().opt()], outs=[c4_out[j].ap().opt()]), "cc", 1)
        blk.gpsimd(gp16)

    rs_t = s.sb("rs_t", [128, 512], F32)
    ln3b = s.sb("ln3b", [128, 512], F32)
    rs_free = [0]
    for j in range(2):
        for dg in range(DG):
            def gp17(g, j=j, dg=dg):
                s.w(g, "cc")
                s.w(g, "vs", rs_free[0])
                s.inc(g.dma_start(out=rs_t[:], in_=c4_out[j][128 * dg:128 * (dg + 1), :]), "gd", 16)
            blk.gpsimd(gp17)

            def ve35(v, j=j, dg=dg):
                s.w(v, "gd")
                s.inc(v.tensor_tensor(ln3b[:], resid[dg][:, 512 * j:512 * (j + 1)],
                                      rstd3_bt[:, 512 * j:512 * (j + 1)], ALU.mult), "vs", 1)
                s.inc(v.tensor_tensor(ln3b[:], ln3b[:], nmr3_bt[:, 512 * j:512 * (j + 1)], ALU.add), "vs", 1)
                s.inc(v.tensor_tensor(ln3b[:], ln3b[:], rs_t[:], ALU.add), "vs", 1)
                s.inc(v.tensor_tensor(resid[dg][:, 512 * j:512 * (j + 1)],
                                      resid[dg][:, 512 * j:512 * (j + 1)], ln3b[:], ALU.add), "vs", 1)
                rs_free[0] = s.c["vs"]
            blk.vector(ve35)
    s.pop()
    s.pop()
    dbg("resid3", resid[0][:, :], [128, TO], F32)
    dbg("c4o0", c4_out[0][0:128, :], [128, 512], F32)
    dbg("c4i0a", c4_in[0][0:128, :], [128, 512], F32)
    dbg("c4i0b", c4_in[0][D:D + 128, :], [128, 512], F32)
    if upto == "H":
        return fin()

    ffn_phase("i", False)
    return fin()


def _prep_core_inputs(core, inp):
    b, r = core // 2, core % 2
    t0 = r * TO
    f32 = np.float32

    def c(a):
        return np.ascontiguousarray(np.asarray(a), dtype=f32)

    x = np.asarray(inp["L0_attn_a_x"])
    xT = c(x[b, t0:t0 + TO, :].T)
    if r == 0:
        xwT = np.zeros((D, WARM), f32)
    else:
        xwT = c(x[b, t0 - WARM:t0, :].T)

    def roll(a, ax):
        return np.ascontiguousarray(np.roll(a, -r * EL, axis=ax))

    return {
        "xT": xT, "xwT": xwT,
        "W_inprojT": c(np.asarray(inp["L0_attn_a_in_proj_w"]).T),
        "W_xprojT": c(np.asarray(inp["L0_attn_a_x_proj_w"]).T),
        "conv_a_w": c(inp["L0_attn_a_conv_w"]),
        "conv_a_b": c(inp["L0_attn_a_conv_b"]).reshape(E, 1),
        "W_dtT": c(np.asarray(inp["L0_attn_a_dt_proj_w"]).T),
        "dt_b_col": c(inp["L0_attn_a_dt_proj_b"]).reshape(H, 1),
        "Alog_a": c(inp["L0_attn_a_A_log"]).reshape(H, 1),
        "D_a_row": np.repeat(c(inp["L0_attn_a_D"]), PDIM).reshape(1, E),
        "W_outprojT": c(np.asarray(inp["L0_attn_a_out_proj_w"]).T),
        "ffn_WT": c(np.asarray(inp["L0_ffn_weight"]).T),
        "ffn_gamma": c(inp["L0_ffn_gamma"]).reshape(D, 1),
        "ffn_beta": c(inp["L0_ffn_beta"]).reshape(D, 1),
        "ffn_lnw": c(inp["L0_ffn_ln_weight"]).reshape(D, 1),
        "ffn_lnb": c(inp["L0_ffn_ln_bias"]).reshape(D, 1),
        "norm_w": c(inp["L0_attn_b_norm_w"]).reshape(D, 1),
        "W_skip": roll(c(inp["L0_attn_b_W_skip"]), 1),
        "W_in": roll(c(inp["L0_attn_b_W_in"]), 1),
        "conv_b_w": roll(c(np.asarray(inp["L0_attn_b_conv_w"]).reshape(4, E).T), 0),
        "conv_b_b": roll(c(inp["L0_attn_b_conv_b"]), 0).reshape(E, 1),
        "W_d1": roll(c(inp["L0_attn_b_W_delta_1"]), 0),
        "W_d2": roll(c(inp["L0_attn_b_W_delta_2"]), 1),
        "b_d2": roll(c(inp["L0_attn_b_b_delta_2"]), 0).reshape(E, 1),
        "W_Bp": roll(c(inp["L0_attn_b_W_B"]), 0),
        "W_Cp": roll(c(inp["L0_attn_b_W_C"]), 0),
        "Alog_b": roll(c(inp["L0_attn_b_A_log"]), 0),
        "W_D_col": roll(c(inp["L0_attn_b_W_D"]), 0).reshape(E, 1),
        "W_out_in": roll(c(inp["L0_attn_b_W_out"]), 0),
        "m_send": np.full((128, 1), 1.0 - r, f32),
        "r_recv": np.full((128, 1), float(r), f32),
        "sel_r0": np.full((128, 1), 1 - r, np.uint8),
    }


def kernel(**inputs):
    nc = build_nc()
    in_maps = [_prep_core_inputs(cc, inputs) for cc in range(8)]
    res = run_bass_kernel_spmd(nc, in_maps, core_ids=list(range(8)))
    o = np.zeros((4, T, D), np.float32)
    for cc in range(8):
        b, r = cc // 2, cc % 2
        o[b, r * TO:(r + 1) * TO, :] = np.asarray(res.results[cc]["out"]).T
    return o


# version 03
# speedup vs baseline: 1.1991x; 1.1991x over previous
import numpy as np
import ml_dtypes
import concourse.bass as bass
import concourse.mybir as mybir
from concourse.bass_utils import run_bass_kernel_spmd

F32 = mybir.dt.float32
BF16 = mybir.dt.bfloat16
U8 = mybir.dt.uint8
AF = mybir.ActivationFunctionType
ALU = mybir.AluOpType

PAIRS = [[0, 1], [2, 3], [4, 5], [6, 7]]
D = 2048
E = 4096
EL = 2048
H = 64
PDIM = 64
NA = 128
NB = 16
T = 2048
TO = 1024
CH = 128
NCH = TO // CH
WARM = 128
DG = D // 128
EG = E // 128
EGL = EL // 128

bfd = ml_dtypes.bfloat16
SEMS = ["ld", "gd", "ts", "ss", "vs", "cc", "od"]


class Bld:
    def __init__(self, nc, blk):
        self.nc = nc
        self.blk = blk
        self.h = {n: nc.alloc_semaphore(n) for n in SEMS}
        self.c = {n: 0 for n in SEMS}
        self.scopes = []
        self.names = set()

    def inc(self, instr, name, n):
        instr.then_inc(self.h[name], n)
        self.c[name] += n
        return self.c[name]

    def w(self, eng, name, v=None):
        v = self.c[name] if v is None else v
        if v > 0:
            eng.wait_ge(self.h[name], v)

    def push(self):
        self.scopes.append([])

    def pop(self):
        self.join_all()
        for cm in reversed(self.scopes.pop()):
            cm.__exit__(None, None, None)

    def _alloc(self, cm):
        t = cm.__enter__()
        if self.scopes:
            self.scopes[-1].append(cm)
        return t

    def sb(self, name, shape, dt):
        assert name not in self.names, name
        self.names.add(name)
        return self._alloc(self.nc.sbuf_tensor(name, shape, dt))

    def ps(self, name, shape, dt=F32):
        assert name not in self.names, name
        self.names.add(name)
        return self._alloc(self.nc.psum_tensor(name, shape, dt))

    def join_all(self):
        snap = dict(self.c)
        for engname in ["sync", "gpsimd", "tensor", "scalar", "vector"]:
            def body(e, snap=snap):
                for n in SEMS:
                    if snap[n] > 0:
                        e.wait_ge(self.h[n], snap[n])
            getattr(self.blk, engname)(body)


def build_nc(upto=None, debug_keys=()):
    nc = bass.Bass("TRN2", target_bir_lowering=False)

    def di(name, shape, dt=F32):
        return nc.dram_tensor(name, shape, dt, kind="ExternalInput")

    xT = di("xT", [D, TO])
    xwT = di("xwT", [D, WARM])
    W_inprojT = di("W_inprojT", [D, 2 * E])
    W_xprojT = di("W_xprojT", [E, 4416])
    conv_a_w = di("conv_a_w", [E, 4])
    conv_a_b = di("conv_a_b", [E, 1])
    W_dtT = di("W_dtT", [H, H])
    dt_b_col = di("dt_b_col", [H, 1])
    Alog_a = di("Alog_a", [H, 1])
    D_a_row = di("D_a_row", [1, E])
    W_outprojT = di("W_outprojT", [E, D])
    ffn_WT = di("ffn_WT", [D, D])
    ffn_gamma = di("ffn_gamma", [D, 1])
    ffn_beta = di("ffn_beta", [D, 1])
    ffn_lnw = di("ffn_lnw", [D, 1])
    ffn_lnb = di("ffn_lnb", [D, 1])
    norm_w = di("norm_w", [D, 1])
    W_skip = di("W_skip", [D, E])
    W_in = di("W_in", [D, E])
    conv_b_w = di("conv_b_w", [E, 4])
    conv_b_b = di("conv_b_b", [E, 1])
    W_d1 = di("W_d1", [E, 128])
    W_d2 = di("W_d2", [128, E])
    b_d2 = di("b_d2", [E, 1])
    W_Bp = di("W_Bp", [E, NB])
    W_Cp = di("W_Cp", [E, NB])
    Alog_b = di("Alog_b", [E, NB])
    W_D_col = di("W_D_col", [E, 1])
    W_out_in = di("W_out_in", [E, D])
    m_send = di("m_send", [128, 1])
    r_recv = di("r_recv", [128, 1])
    sel_r0 = di("sel_r0", [128, 1], U8)

    out = nc.dram_tensor("out", [D, TO], F32, kind="ExternalOutput")

    def drm(name, shape, dt):
        return nc.dram_tensor(name, shape, dt)

    z_spill = drm("z_spill", [E, TO], BF16)
    X_spill = drm("X_spill", [E, TO], BF16)
    yg_spill = drm("yg_spill", [E, TO], BF16)
    signW_spill = drm("signW_spill", [D, D], BF16)
    skip_spill = drm("skip_spill", [EL, T], BF16)
    delta_spill = drm("delta_spill", [E, TO], F32)
    snap_spill = drm("snap_spill", [NCH * NA, E], BF16)
    cs_dram = drm("cs_dram", [H, TO], F32)
    ysk_spill = drm("ysk_spill", [EL, T], BF16)
    Brow_dram = drm("Brow_dram", [NB, T], BF16)
    Crow_dram = drm("Crow_dram", [NB, T], BF16)

    c2_in = drm("c2_in", [NA, E], BF16)
    c2_out = drm("c2_out", [2 * NA, E], BF16)
    c1_in = drm("c1_in", [D, TO], BF16)
    c1_out = drm("c1_out", [2 * D, TO], BF16)
    NF3 = EL + 2 * NB
    c3f_in = drm("c3f_in", [NF3, TO], F32)
    c3f_out = drm("c3f_out", [2 * NF3, TO], F32)
    c3b_in = drm("c3b_in", [EL, TO], BF16)
    c3b_out = drm("c3b_out", [2 * EL, TO], BF16)
    c4_in = [drm(f"c4_in{j}", [2 * D, 512], F32) for j in range(2)]
    c4_out = [drm(f"c4_out{j}", [D, 512], F32) for j in range(2)]

    ident_bf_d = nc.inline_tensor(np.eye(128, dtype=bfd), "ident_bf")
    ident_f_d = nc.inline_tensor(np.eye(128, dtype=np.float32), "ident_f")
    rst_np = np.ones((H, TO), np.float32)
    rst_np[:, ::CH] = 0.0
    rst_d = nc.inline_tensor(rst_np, "rst_a")
    tril_d = nc.inline_tensor(np.triu(np.ones((CH, CH), np.float32)), "tril")

    blk_cm = nc.Block()
    blk = blk_cm.__enter__()
    s = Bld(nc, blk)

    def dbg(key, ap, shape, dt=F32):
        if key in debug_keys:
            t = nc.dram_tensor("dbg_" + key, shape, dt, kind="ExternalOutput")

            def em(sy, t=t, ap=ap):
                for n in SEMS:
                    s.w(sy, n)
                s.inc(sy.dma_start(out=t[:], in_=ap), "od", 16)
            blk.sync(em)

    def fin():
        def sy(sx):
            s.w(sx, "vs")
            s.w(sx, "ss")
            s.w(sx, "ts")
            for i in range(DG):
                s.inc(sx.dma_start(out=out[128 * i:128 * (i + 1), :], in_=resid[i][:]), "od", 16)
            s.w(sx, "od")
        blk.sync(sy)
        blk_cm.__exit__(None, None, None)
        return nc

    resid = [s.sb(f"resid{i}", [128, TO], F32) for i in range(DG)]
    identt = s.sb("identt", [128, 128], BF16)
    identf = s.sb("identf", [128, 128], F32)
    ones1 = s.sb("ones1", [1, 128], BF16)
    ones1f = s.sb("ones1f", [1, 128], F32)
    onescol = s.sb("onescol", [128, 1], F32)
    eps_t = s.sb("eps_t", [128, 1], F32)
    one_t = s.sb("one_t", [128, 1], F32)
    cab = s.sb("cab", [128, EG], F32)
    cbb = s.sb("cbb", [128, EG], F32)
    caw = s.sb("caw", [128, 4 * EG], F32)
    cbw = s.sb("cbw", [128, 4 * EG], F32)
    Aa_t = s.sb("Aa_t", [H, 1], F32)
    gscale = s.sb("gscale", [128, DG], F32)
    fbeta = s.sb("fbeta", [128, DG], F32)
    flnw = s.sb("flnw", [128, DG], F32)
    flnb = s.sb("flnb", [128, DG], F32)
    nrmw = s.sb("nrmw", [128, DG], F32)
    wdcol = s.sb("wdcol", [128, EGL], F32)
    Ab_t = s.sb("Ab_t", [128, NB * EGL], F32)
    msend_t = s.sb("msend_t", [128, 1], F32)
    rrecv_t = s.sb("rrecv_t", [128, 1], F32)
    selr0_t = s.sb("selr0_t", [128, 1], U8)
    onescol_bf = s.sb("onescol_bf", [128, 1], BF16)

    def p0_ld(sy):
        for i in range(DG):
            s.inc(sy.dma_start(out=resid[i][:], in_=xT[128 * i:128 * (i + 1), :]), "ld", 16)
        s.inc(sy.dma_start(out=identt[:], in_=ident_bf_d[:]), "ld", 16)
        s.inc(sy.dma_start(out=identf[:], in_=ident_f_d[:]), "ld", 16)
        s.inc(sy.dma_start(out=Aa_t[:], in_=Alog_a[:]), "ld", 16)
        s.inc(sy.dma_start(out=msend_t[:], in_=m_send[:]), "ld", 16)
        s.inc(sy.dma_start(out=rrecv_t[:], in_=r_recv[:]), "ld", 16)
        s.inc(sy.dma_start(out=selr0_t[:], in_=sel_r0[:]), "ld", 16)
        for eg in range(EG):
            s.inc(sy.dma_start(out=caw[:, 4 * eg:4 * eg + 4], in_=conv_a_w[128 * eg:128 * (eg + 1), :]), "ld", 16)
            s.inc(sy.dma_start(out=cbw[:, 4 * eg:4 * eg + 4], in_=conv_b_w[128 * eg:128 * (eg + 1), :]), "ld", 16)
            s.inc(sy.dma_start(out=cab[:, eg:eg + 1], in_=conv_a_b[128 * eg:128 * (eg + 1), :]), "ld", 16)
            s.inc(sy.dma_start(out=cbb[:, eg:eg + 1], in_=conv_b_b[128 * eg:128 * (eg + 1), :]), "ld", 16)
        for i in range(DG):
            s.inc(sy.dma_start(out=gscale[:, i:i + 1], in_=ffn_gamma[128 * i:128 * (i + 1), :]), "ld", 16)
            s.inc(sy.dma_start(out=fbeta[:, i:i + 1], in_=ffn_beta[128 * i:128 * (i + 1), :]), "ld", 16)
            s.inc(sy.dma_start(out=flnw[:, i:i + 1], in_=ffn_lnw[128 * i:128 * (i + 1), :]), "ld", 16)
            s.inc(sy.dma_start(out=flnb[:, i:i + 1], in_=ffn_lnb[128 * i:128 * (i + 1), :]), "ld", 16)
            s.inc(sy.dma_start(out=nrmw[:, i:i + 1], in_=norm_w[128 * i:128 * (i + 1), :]), "ld", 16)
        for eg in range(EGL):
            s.inc(sy.dma_start(out=wdcol[:, eg:eg + 1], in_=W_D_col[128 * eg:128 * (eg + 1), :]), "ld", 16)
            s.inc(sy.dma_start(out=Ab_t[:, NB * eg:NB * (eg + 1)], in_=Alog_b[128 * eg:128 * (eg + 1), :]), "ld", 16)
    blk.sync(p0_ld)

    def p0_ve(v):
        s.w(v, "ld")
        s.inc(v.memset(ones1[:], 1.0), "vs", 1)
        s.inc(v.memset(ones1f[:], 1.0), "vs", 1)
        s.inc(v.memset(onescol[:], 1.0), "vs", 1)
        s.inc(v.memset(onescol_bf[:], 1.0), "vs", 1)
        s.inc(v.memset(eps_t[:], 1e-5), "vs", 1)
        s.inc(v.memset(one_t[:], 1.0), "vs", 1)
    blk.vector(p0_ve)

    def p0_sc(sc):
        s.w(sc, "ld")
        s.inc(sc.activation(Aa_t[:], Aa_t[:], AF.Exp), "ss", 1)
        s.inc(sc.activation(Ab_t[:], Ab_t[:], AF.Exp), "ss", 1)
    blk.scalar(p0_sc)

    def p0_ve2(v):
        s.w(v, "ss")
        s.inc(v.tensor_scalar(Aa_t[:], Aa_t[:], -1.0, None, ALU.mult), "vs", 1)
        s.inc(v.tensor_scalar(Ab_t[:], Ab_t[:], -1.0, None, ALU.mult), "vs", 1)
    blk.vector(p0_ve2)

    def expand_f32(tag, row_ap, wd, out_ap_fn):
        pse = s.ps(f"{tag}_pse", [128, 512], F32)
        nst = (wd + 511) // 512
        for st in range(nst):
            a = st * 512
            bw = min(wd, a + 512) - a

            def te(t, a=a, bw=bw):
                s.w(t, "vs")
                s.w(t, "ss")
                s.inc(t.matmul(pse[:, 0:bw], ones1f[:], row_ap[0:1, a:a + bw], start=True, stop=True), "ts", 1)
            blk.tensor(te)

            def scv(sc, a=a, bw=bw):
                s.w(sc, "ts")
                s.inc(sc.copy(out_ap_fn(a, bw), pse[:, 0:bw]), "ss", 1)
            blk.scalar(scv)

    def ln_stats(tag, tiles, tc, rstd_row, nmr_row, bf=False):
        srow = s.sb(f"{tag}_srow", [1, tc], F32)
        sqt = s.sb(f"{tag}_sqt", [128, 512], F32)
        ps_s = s.ps(f"{tag}_pss", [1, 512], F32)
        ps_q = s.ps(f"{tag}_psq", [1, 512], F32)
        oc = onescol_bf if bf else onescol
        nst = (tc + 511) // 512
        for st in range(nst):
            c0 = st * 512
            wd = min(tc, c0 + 512) - c0
            for pg in range(DG):
                def scq(sc, pg=pg, c0=c0, wd=wd):
                    s.w(sc, "vs")
                    s.w(sc, "ld")
                    if pg > 0:
                        s.w(sc, "ts")
                    s.inc(sc.activation(sqt[:, 0:wd], tiles[pg][:, c0:c0 + wd], AF.Square), "ss", 1)
                blk.scalar(scq)

                def te(t, pg=pg, c0=c0, wd=wd):
                    s.w(t, "ss")
                    s.inc(t.matmul(ps_s[:, 0:wd], oc[:], tiles[pg][:, c0:c0 + wd],
                                   start=(pg == 0), stop=(pg == DG - 1)), "ts", 1)
                    s.inc(t.matmul(ps_q[:, 0:wd], onescol[:], sqt[:, 0:wd],
                                   start=(pg == 0), stop=(pg == DG - 1)), "ts", 1)
                blk.tensor(te)

            def ve(v, c0=c0, wd=wd):
                s.w(v, "ts")
                s.inc(v.tensor_scalar(srow[:, c0:c0 + wd], ps_s[:, 0:wd], 1.0 / D, None, ALU.mult), "vs", 1)
                s.inc(v.tensor_scalar(rstd_row[:, c0:c0 + wd], ps_q[:, 0:wd], 1.0 / D, None, ALU.mult), "vs", 1)
            blk.vector(ve)

        def ve3(v):
            s.inc(v.tensor_tensor(nmr_row[:, 0:tc], srow[:], srow[:], ALU.mult), "vs", 1)
            s.inc(v.tensor_tensor(rstd_row[:, 0:tc], rstd_row[:, 0:tc], nmr_row[:, 0:tc], ALU.subtract), "vs", 1)
        blk.vector(ve3)

        def sc2(sc):
            s.w(sc, "vs")
            s.inc(sc.activation(rstd_row[:, 0:tc], rstd_row[:, 0:tc], AF.Ln, bias=eps_t[0:1, 0:1]), "ss", 1)
            s.inc(sc.activation(rstd_row[:, 0:tc], rstd_row[:, 0:tc], AF.Exp, scale=-0.5), "ss", 1)
        blk.scalar(sc2)

        def ve4(v):
            s.w(v, "ss")
            s.inc(v.tensor_tensor(nmr_row[:, 0:tc], srow[:], rstd_row[:, 0:tc], ALU.mult), "vs", 1)
            s.inc(v.tensor_scalar(nmr_row[:, 0:tc], nmr_row[:, 0:tc], -1.0, None, ALU.mult), "vs", 1)
        blk.vector(ve4)

    s.push()
    dt2 = s.sb("dt2", [H, TO], F32)
    cs = s.sb("cs", [H, TO], F32)
    B_fm = s.sb("B_fm", [NA, TO], BF16)
    C_fm = s.sb("C_fm", [NA, TO], BF16)
    s.push()
    xc = [s.sb(f"xc{eg}", [128, TO], BF16) for eg in range(EG)]
    s.push()
    ln1 = [s.sb(f"ln1_{i}", [128, WARM + TO], BF16) for i in range(DG)]
    s.push()
    xw = [s.sb(f"xw{i}", [128, WARM], BF16) for i in range(DG)]

    def ldxw(g):
        for i in range(DG):
            s.inc(g.dma_start(out=xw[i][:], in_=xwT[128 * i:128 * (i + 1), :]), "gd", 16)
    blk.gpsimd(ldxw)
    r1 = s.sb("r1", [1, TO], F32)
    n1 = s.sb("n1", [1, TO], F32)
    r1w = s.sb("r1w", [1, WARM], F32)
    n1w = s.sb("n1w", [1, WARM], F32)

    def wtg(sc):
        s.w(sc, "gd")
    blk.scalar(wtg)
    ln_stats("l1o", resid, TO, r1, n1)
    ln_stats("l1w", xw, WARM, r1w, n1w, bf=True)
    tmpa = s.sb("tmpa", [128, 512], F32)
    psEr = s.ps("psEr", [128, 512], F32)
    psEn = s.ps("psEn", [128, 512], F32)
    for st in range(2):
        c0 = st * 512

        def tex(t, c0=c0, st=st):
            s.w(t, "vs")
            s.w(t, "ss")
            if st > 0:
                s.w(t, "vs")
            s.inc(t.matmul(psEr[:], ones1f[:], r1[0:1, c0:c0 + 512], start=True, stop=True), "ts", 1)
            s.inc(t.matmul(psEn[:], ones1f[:], n1[0:1, c0:c0 + 512], start=True, stop=True), "ts", 1)
        blk.tensor(tex)
        for pg in range(DG):
            def ve(v, pg=pg, c0=c0):
                s.w(v, "ts")
                s.inc(v.tensor_tensor(tmpa[:], resid[pg][:, c0:c0 + 512], psEr[:], ALU.mult), "vs", 1)
                s.inc(v.tensor_tensor(ln1[pg][:, WARM + c0:WARM + c0 + 512], tmpa[:], psEn[:], ALU.add), "vs", 1)
            blk.vector(ve)

    def tex2(t):
        s.w(t, "vs")
        s.inc(t.matmul(psEr[:, 0:WARM], ones1f[:], r1w[:], start=True, stop=True), "ts", 1)
        s.inc(t.matmul(psEn[:, 0:WARM], ones1f[:], n1w[:], start=True, stop=True), "ts", 1)
    blk.tensor(tex2)
    for pg in range(DG):
        def ve(v, pg=pg):
            s.w(v, "ts")
            s.inc(v.tensor_tensor(tmpa[:, 0:WARM], xw[pg][:], psEr[:, 0:WARM], ALU.mult), "vs", 1)
            s.inc(v.tensor_tensor(ln1[pg][:, 0:WARM], tmpa[:, 0:WARM], psEn[:, 0:WARM], ALU.add), "vs", 1)
        blk.vector(ve)
    s.pop()
    dbg("ln1", ln1[0][:, :], [128, WARM + TO], BF16)

    s.push()
    wbufB = [s.sb(f"wbufB{i}", [128, 16 * 128], BF16) for i in range(2)]
    ztile = [s.sb(f"ztile{i}", [128, TO], BF16) for i in range(2)]
    psB = [s.ps(f"psB{i}", [128, 512], F32) for i in range(4)]
    bankB = [0] * 4
    wfree = [0, 0]
    zfree = [0, 0]
    for mz in range(32):
        mo = 32 + mz
        wb = wbufB[mz % 2]
        zt = ztile[mz % 2]

        def gp(g, mo=mo, wb=wb, mz=mz):
            s.w(g, "ts", wfree[mz % 2])
            s.inc(g.dma_start(out=wb[:].rearrange("p (k m) -> p k m", k=16),
                              in_=W_inprojT[:, 128 * mo:128 * (mo + 1)]
                              .rearrange("(k p) m -> p k m", p=128)), "gd", 16)
        blk.gpsimd(gp)
        bids = [(2 * mz) % 4, (2 * mz + 1) % 4]

        def te(t, mz=mz, wb=wb, bids=bids):
            s.w(t, "gd")
            if mz == 0:
                s.w(t, "vs")
            for bi in bids:
                s.w(t, "ss", bankB[bi])
            for k in range(DG):
                for ti in range(2):
                    s.inc(t.matmul(psB[bids[ti]][:], wb[:, 128 * k:128 * (k + 1)],
                                   ln1[k][:, WARM + 512 * ti:WARM + 512 * (ti + 1)],
                                   start=(k == 0), stop=(k == DG - 1)), "ts", 1)
            wfree[mz % 2] = s.c["ts"]
        blk.tensor(te)

        def sc(scx, mz=mz, bids=bids, zt=zt):
            s.w(scx, "ts")
            s.w(scx, "od", zfree[mz % 2])
            for ti in range(2):
                s.inc(scx.activation(zt[:, 512 * ti:512 * (ti + 1)], psB[bids[ti]][:], AF.Silu), "ss", 1)
            for bi in bids:
                bankB[bi] = s.c["ss"]
        blk.scalar(sc)

        def sy(sx, mz=mz, zt=zt):
            s.w(sx, "ss")
            s.inc(sx.dma_start(out=z_spill[128 * mz:128 * (mz + 1), :], in_=zt[:]), "od", 16)
            zfree[mz % 2] = s.c["od"]
        blk.sync(sy)
    s.pop()

    s.push()
    wbufX = [s.sb(f"wbufX{i}", [128, 16 * 128], BF16) for i in range(2)]
    xbt = [s.sb(f"xbt{i}", [128, WARM + TO], BF16) for i in range(2)]
    cacc = s.sb("cacc", [128, TO], F32)
    psXx = [s.ps(f"psXx{i}", [128, 512], F32) for i in range(6)]
    bankX = [0] * 6
    wfreeX = [0, 0]
    xbfree = [0, 0]
    tilesB2 = [(0, WARM), (WARM, 512), (WARM + 512, 512)]
    for eg in range(EG):
        wb = wbufX[eg % 2]
        xbe = xbt[eg % 2]

        def gp(g, eg=eg, wb=wb):
            s.w(g, "ts", wfreeX[eg % 2])
            s.inc(g.dma_start(out=wb[:].rearrange("p (k m) -> p k m", k=16),
                              in_=W_inprojT[:, 128 * eg:128 * (eg + 1)]
                              .rearrange("(k p) m -> p k m", p=128)), "gd", 16)
        blk.gpsimd(gp)
        bids = [(3 * eg) % 6, (3 * eg + 1) % 6, (3 * eg + 2) % 6]

        def te(t, eg=eg, wb=wb, bids=bids):
            s.w(t, "gd")
            for bi in bids:
                s.w(t, "ss", bankX[bi])
            for k in range(DG):
                for ti, (c0, wd) in enumerate(tilesB2):
                    s.inc(t.matmul(psXx[bids[ti]][:, 0:wd], wb[:, 128 * k:128 * (k + 1)],
                                   ln1[k][:, c0:c0 + wd],
                                   start=(k == 0), stop=(k == DG - 1)), "ts", 1)
            wfreeX[eg % 2] = s.c["ts"]
        blk.tensor(te)

        def sc(scx, eg=eg, bids=bids, xbe=xbe):
            s.w(scx, "ts")
            s.w(scx, "vs", xbfree[eg % 2])
            for ti, (c0, wd) in enumerate(tilesB2):
                s.inc(scx.copy(xbe[:, c0:c0 + wd], psXx[bids[ti]][:, 0:wd]), "ss", 1)
            for bi in bids:
                bankX[bi] = s.c["ss"]
        blk.scalar(sc)

        def ve(v, eg=eg, xbe=xbe):
            s.w(v, "ss")
            s.inc(v.tensor_scalar(cacc[:], xbe[:, WARM - 3:WARM - 3 + TO],
                                  caw[:, 4 * eg:4 * eg + 1], None, ALU.mult), "vs", 1)
            for k in range(1, 4):
                s.inc(v.scalar_tensor_tensor(cacc[:], xbe[:, WARM - 3 + k:WARM - 3 + k + TO],
                                             caw[:, 4 * eg + k:4 * eg + k + 1], cacc[:],
                                             ALU.mult, ALU.add), "vs", 1)
            xbfree[eg % 2] = s.c["vs"]
        blk.vector(ve)

        def sc2(scx, eg=eg):
            s.w(scx, "vs")
            s.inc(scx.activation(xc[eg][:], cacc[:], AF.Silu, bias=cab[:, eg:eg + 1]), "ss", 1)
        blk.scalar(sc2)
    s.pop()
    s.pop()
    dbg("xc0", xc[0][:, :], [128, TO], BF16)
    if upto == "B":
        return fin()

    s.push()
    wbufD = [s.sb(f"wbufD{i}", [128, 32 * 128], BF16) for i in range(2)]
    xtile = [s.sb(f"xtile{i}", [128, TO], BF16) for i in range(2)]
    s.push()
    psD = [s.ps(f"psD{i}", [128, 512], F32) for i in range(4)]
    bankD = [0] * 4
    wfreeD = [0, 0]
    xtfree = [0, 0]
    for mo in range(32):
        wb = wbufD[mo % 2]
        xte = xtile[mo % 2]

        def gp(g, mo=mo, wb=wb):
            s.w(g, "ts", wfreeD[mo % 2])
            s.inc(g.dma_start(out=wb[:].rearrange("p (k m) -> p k m", k=32),
                              in_=W_xprojT[:, 128 * mo:128 * (mo + 1)]
                              .rearrange("(k p) m -> p k m", p=128)), "gd", 16)
        blk.gpsimd(gp)
        bids = [(2 * mo) % 4, (2 * mo + 1) % 4]

        def te(t, mo=mo, wb=wb, bids=bids):
            s.w(t, "gd")
            if mo == 0:
                s.w(t, "ss")
            for bi in bids:
                s.w(t, "ss", bankD[bi])
            for k in range(EG):
                for ti in range(2):
                    s.inc(t.matmul(psD[bids[ti]][:], wb[:, 128 * k:128 * (k + 1)],
                                   xc[k][:, 512 * ti:512 * (ti + 1)],
                                   start=(k == 0), stop=(k == EG - 1)), "ts", 1)
            wfreeD[mo % 2] = s.c["ts"]
        blk.tensor(te)

        def sc(scx, mo=mo, bids=bids, xte=xte):
            s.w(scx, "ts")
            s.w(scx, "od", xtfree[mo % 2])
            for ti in range(2):
                s.inc(scx.copy(xte[:, 512 * ti:512 * (ti + 1)], psD[bids[ti]][:]), "ss", 1)
            for bi in bids:
                bankD[bi] = s.c["ss"]
        blk.scalar(sc)

        def sy(sx, mo=mo, xte=xte):
            s.w(sx, "ss")
            s.inc(sx.dma_start(out=X_spill[128 * mo:128 * (mo + 1), :], in_=xte[:]), "od", 16)
            xtfree[mo % 2] = s.c["od"]
        blk.sync(sy)

    s.pop()
    wdt = s.sb("wdt", [128, 32 * 64], BF16)
    wBx = s.sb("wBx", [128, 32 * 128], BF16)
    wCx = s.sb("wCx", [128, 32 * 128], BF16)
    dtp = s.sb("dtp", [H, TO], BF16)

    def gp2(g):
        s.w(g, "ts", wfreeD[0])
        s.w(g, "ts", wfreeD[1])
        s.inc(g.dma_start(out=wdt[:].rearrange("p (k m) -> p k m", k=32),
                          in_=W_xprojT[:, 4096:4160].rearrange("(k p) m -> p k m", p=128)), "gd", 16)
        s.inc(g.dma_start(out=wBx[:].rearrange("p (k m) -> p k m", k=32),
                          in_=W_xprojT[:, 4160:4288].rearrange("(k p) m -> p k m", p=128)), "gd", 16)
        s.inc(g.dma_start(out=wCx[:].rearrange("p (k m) -> p k m", k=32),
                          in_=W_xprojT[:, 4288:4416].rearrange("(k p) m -> p k m", p=128)), "gd", 16)
    blk.gpsimd(gp2)
    s.push()
    psdt = s.ps("psdt", [64, 512], F32)
    psBb = s.ps("psBb", [128, 512], F32)
    psCc = s.ps("psCc", [128, 512], F32)
    for ti in range(2):
        def te(t, ti=ti):
            s.w(t, "gd")
            s.w(t, "ss")
            for k in range(EG):
                s.inc(t.matmul(psdt[:], wdt[:, 64 * k:64 * (k + 1)], xc[k][:, 512 * ti:512 * (ti + 1)],
                               start=(k == 0), stop=(k == EG - 1)), "ts", 1)
            for k in range(EG):
                s.inc(t.matmul(psBb[:], wBx[:, 128 * k:128 * (k + 1)], xc[k][:, 512 * ti:512 * (ti + 1)],
                               start=(k == 0), stop=(k == EG - 1)), "ts", 1)
            for k in range(EG):
                s.inc(t.matmul(psCc[:], wCx[:, 128 * k:128 * (k + 1)], xc[k][:, 512 * ti:512 * (ti + 1)],
                               start=(k == 0), stop=(k == EG - 1)), "ts", 1)
        blk.tensor(te)

        def sc(scx, ti=ti):
            s.w(scx, "ts")
            s.inc(scx.copy(dtp[:, 512 * ti:512 * (ti + 1)], psdt[:]), "ss", 1)
            s.inc(scx.copy(B_fm[:, 512 * ti:512 * (ti + 1)], psBb[:]), "ss", 1)
            s.inc(scx.copy(C_fm[:, 512 * ti:512 * (ti + 1)], psCc[:]), "ss", 1)
        blk.scalar(sc)

    s.pop()
    wdtp = s.sb("wdtp", [H, H], BF16)
    dtbcol = s.sb("dtbcol", [H, 1], F32)

    def gp3(g):
        s.inc(g.dma_start(out=wdtp[:], in_=W_dtT[:]), "gd", 16)
    blk.gpsimd(gp3)

    def sy2(sx):
        s.inc(sx.dma_start(out=dtbcol[:], in_=dt_b_col[:]), "ld", 16)
    blk.sync(sy2)
    psd2 = s.ps("psd2", [64, 512], F32)
    for ti in range(2):
        def te(t, ti=ti):
            s.w(t, "gd")
            s.w(t, "ss")
            if ti > 0:
                s.w(t, "vs")
            s.inc(t.matmul(psd2[:], wdtp[:], dtp[:, 512 * ti:512 * (ti + 1)], start=True, stop=True), "ts", 1)
        blk.tensor(te)

        def ve(v, ti=ti):
            s.w(v, "ts")
            s.w(v, "ld")
            s.inc(v.tensor_scalar(dt2[:, 512 * ti:512 * (ti + 1)], psd2[:], dtbcol[:, 0:1],
                                  None, ALU.add), "vs", 1)
        blk.vector(ve)

        def sc(scx, ti=ti):
            s.w(scx, "vs")
            s.inc(scx.activation(dt2[:, 512 * ti:512 * (ti + 1)], dt2[:, 512 * ti:512 * (ti + 1)], AF.Exp), "ss", 1)
            s.inc(scx.activation(dt2[:, 512 * ti:512 * (ti + 1)], dt2[:, 512 * ti:512 * (ti + 1)],
                                 AF.Ln, bias=one_t[0:64, 0:1]), "ss", 1)
        blk.scalar(sc)

    A_sc = s.sb("A_sc", [H, TO], F32)
    rstt = s.sb("rstt", [H, TO], F32)

    def sy3(sx):
        s.inc(sx.dma_start(out=rstt[:], in_=rst_d[:]), "ld", 16)
    blk.sync(sy3)

    def ve5(v):
        s.w(v, "ss")
        s.w(v, "ld")
        s.inc(v.tensor_scalar(A_sc[:], dt2[:], Aa_t[:, 0:1], None, ALU.mult), "vs", 1)
        s.inc(v.tensor_tensor_scan(cs[:], rstt[:], A_sc[:], 0.0, ALU.mult, ALU.add), "vs", 1)
    blk.vector(ve5)

    def sycs(sx):
        s.w(sx, "vs")
        s.inc(sx.dma_start(out=cs_dram[:], in_=cs[:]), "od", 16)
    blk.sync(sycs)
    s.pop()
    s.pop()
    dbg("dt2", dt2[:, :], [H, TO], F32)
    dbg("cs", cs[:, :], [H, TO], F32)
    dbg("B_fm", B_fm[:, :], [NA, TO], BF16)
    if upto == "D":
        return fin()

    s.push()
    csT = s.sb("csT", [128, H * NCH], F32)
    dtT = s.sb("dtT", [128, H * NCH], F32)
    ecT = s.sb("ecT", [128, H * NCH], F32)
    S_in = s.sb("S_in", [NA, E], BF16)
    s.push()
    Scum = s.sb("Scum", [NA, E], F32)
    Scum_bf = s.sb("Scum_bf", [NA, E], BF16)
    exptotT = s.sb("exptotT", [128, H * NCH], F32)
    totT = s.sb("totT", [128, H], F32)
    wcol_all = s.sb("wcol_all", [128, H], F32)
    ecrow = s.sb("ecrow", [1, H], F32)
    etrow = s.sb("etrow", [1, H], F32)
    totrow = s.sb("totrow", [1, H], F32)
    ecrow_exp = s.sb("ecrow_exp", [1, H], F32)
    Xld = s.sb("Xld", [128, 32 * 128], BF16)
    Xtm = s.sb("Xtm", [128, E], BF16)
    Btm = s.sb("Btm", [128, NA], BF16)
    Bd_w = s.sb("Bd_w", [128, 16 * 128], BF16)
    pstf = s.ps("pstf", [128, 512], F32)
    pstb = s.ps("pstb", [128, 1024], BF16)
    psS = s.ps("psS", [128, 512], F32)
    psS2 = s.ps("psS2", [128, 512], F32)
    pse2 = s.ps("pse2", [128, 192], F32)

    def ve6(v):
        s.inc(v.memset(Scum[:], 0.0), "vs", 1)
        s.inc(v.memset(Scum_bf[:], 0.0), "vs", 1)
        s.inc(v.memset(ecrow[:], 0.0), "vs", 1)
    blk.vector(ve6)

    bd_free = [0]
    psS_free = [0]
    for c in range(NCH):
        c0 = c * CH

        def te(t, c=c, c0=c0):
            s.w(t, "vs")
            s.w(t, "ss")
            s.inc(t.transpose(pstf[:, 0:64], cs[0:H, c0:c0 + CH], identf[0:64, 0:64]), "ts", 1)
            s.inc(t.transpose(pstf[:, 64:128], dt2[0:H, c0:c0 + CH], identf[0:64, 0:64]), "ts", 1)
        blk.tensor(te)

        def syt(sx, c=c, c0=c0):
            s.w(sx, "vs")
            if c == 0:
                s.w(sx, "od")
            s.inc(sx.dma_start(out=totrow[:],
                               in_=cs_dram.ap().rearrange("h t -> t h")[c0 + CH - 1:c0 + CH, 0:H],
                               allow_slow_non_contiguous=True), "ld", 16)
        blk.sync(syt)

        def sc(scx, c=c):
            s.w(scx, "ts")
            s.w(scx, "vs")
            s.w(scx, "ld")
            s.inc(scx.copy(csT[:, H * c:H * (c + 1)], pstf[:, 0:64]), "ss", 1)
            s.inc(scx.copy(dtT[:, H * c:H * (c + 1)], pstf[:, 64:128]), "ss", 1)
            s.inc(scx.activation(etrow[:], totrow[:], AF.Exp), "ss", 1)
            s.inc(scx.activation(ecrow_exp[:], ecrow[:], AF.Exp), "ss", 1)
        blk.scalar(sc)

        def te2(t, c=c):
            s.w(t, "ss")
            s.inc(t.matmul(pse2[:, 0:64], ones1f[:], etrow[:], start=True, stop=True), "ts", 1)
            s.inc(t.matmul(pse2[:, 64:128], ones1f[:], ecrow_exp[:], start=True, stop=True), "ts", 1)
            s.inc(t.matmul(pse2[:, 128:192], ones1f[:], totrow[:], start=True, stop=True), "ts", 1)
        blk.tensor(te2)

        def sc3(scx, c=c):
            s.w(scx, "ts")
            s.inc(scx.copy(exptotT[:, H * c:H * (c + 1)], pse2[:, 0:64]), "ss", 1)
            s.inc(scx.copy(ecT[:, H * c:H * (c + 1)], pse2[:, 64:128]), "ss", 1)
            s.inc(scx.copy(totT[:], pse2[:, 128:192]), "ss", 1)
        blk.scalar(sc3)

        def ve7(v, c=c):
            s.w(v, "ss")
            s.inc(v.tensor_tensor(ecrow[:], ecrow[:], totrow[:], ALU.add), "vs", 1)
            s.inc(v.tensor_tensor(wcol_all[:], totT[:], csT[:, H * c:H * (c + 1)], ALU.subtract), "vs", 1)
        blk.vector(ve7)

        def sc3b(scx, c=c):
            s.w(scx, "vs")
            s.inc(scx.activation(wcol_all[:], wcol_all[:], AF.Exp), "ss", 1)
        blk.scalar(sc3b)

        def ve7b(v, c=c):
            s.w(v, "ss")
            s.inc(v.tensor_tensor(wcol_all[:], wcol_all[:], dtT[:, H * c:H * (c + 1)], ALU.mult), "vs", 1)
        blk.vector(ve7b)

        def gp(g, c=c, c0=c0):
            s.w(g, "vs")
            s.inc(g.dma_start(out=Xld[:].rearrange("p (k m) -> p k m", k=32),
                              in_=X_spill[:, c0:c0 + CH].rearrange("(k p) m -> p k m", p=128)), "gd", 16)
        blk.gpsimd(gp)

        for gb in range(4):
            def te3(t, c=c, gb=gb):
                if gb == 0:
                    s.w(t, "gd")
                s.w(t, "vs")
                for j in range(8):
                    s.inc(t.transpose(pstb[:, 128 * j:128 * (j + 1)],
                                      Xld[:, 128 * (8 * gb + j):128 * (8 * gb + j + 1)], identt[:]), "ts", 1)
            blk.tensor(te3)

            def ve8(v, gb=gb):
                s.w(v, "ts")
                s.inc(v.tensor_copy(Xtm[:, 1024 * gb:1024 * (gb + 1)], pstb[:]), "vs", 1)
            blk.vector(ve8)

        def te4(t, c=c, c0=c0):
            s.w(t, "vs")
            s.inc(t.transpose(pstb[:, 0:128], B_fm[:, c0:c0 + CH], identt[:]), "ts", 1)
        blk.tensor(te4)

        def ve9(v, c=c):
            s.w(v, "ts")
            s.inc(v.tensor_copy(Btm[:], pstb[:, 0:128]), "vs", 1)
        blk.vector(ve9)

        def sy4(sx, c=c):
            s.w(sx, "vs")
            s.w(sx, "ss")
            s.inc(sx.dma_start(out=snap_spill[NA * c:NA * (c + 1), :], in_=Scum_bf[:]), "od", 16)
        blk.sync(sy4)
        snap_od = s.c["od"]

        for hb in range(4):
            def ve10(v, c=c, hb=hb):
                s.w(v, "ts", bd_free[0])
                for j in range(16):
                    h = 16 * hb + j
                    s.inc(v.tensor_scalar(Bd_w[:, 128 * j:128 * (j + 1)], Btm[:],
                                          wcol_all[:, h:h + 1], None, ALU.mult), "vs", 1)
            blk.vector(ve10)

            def te5(t, c=c, hb=hb):
                s.w(t, "vs")
                s.w(t, "vs", psS_free[0])
                for j in range(16):
                    h = 16 * hb + j
                    s.inc(t.matmul(psS[:, 64 * j:64 * (j + 1)] if j < 8 else psS2[:, 64 * (j - 8):64 * (j - 7)],
                                   Bd_w[:, 128 * j:128 * (j + 1)], Xtm[:, 64 * h:64 * (h + 1)],
                                   start=True, stop=True), "ts", 1)
                bd_free[0] = s.c["ts"]
            blk.tensor(te5)

            def ve11(v, c=c, hb=hb):
                s.w(v, "ts")
                for j in range(16):
                    h = 16 * hb + j
                    src = psS[:, 64 * j:64 * (j + 1)] if j < 8 else psS2[:, 64 * (j - 8):64 * (j - 7)]
                    s.inc(v.scalar_tensor_tensor(Scum[:, 64 * h:64 * (h + 1)], Scum[:, 64 * h:64 * (h + 1)],
                                                 exptotT[:, H * c + h:H * c + h + 1],
                                                 src, ALU.mult, ALU.add), "vs", 1)
                psS_free[0] = s.c["vs"]
            blk.vector(ve11)

        def sc4(scx, c=c, snap_od=snap_od):
            s.w(scx, "vs")
            s.w(scx, "od", snap_od)
            s.inc(scx.copy(Scum_bf[:], Scum[:]), "ss", 1)
        blk.scalar(sc4)

    c2s = s.sb("c2s", [NA, E], BF16)

    def ve12(v):
        s.w(v, "ss")
        s.inc(v.tensor_scalar(c2s[:], Scum_bf[:], msend_t[:, 0:1], None, ALU.mult), "vs", 1)
    blk.vector(ve12)

    def gp4(g):
        s.w(g, "vs")
        s.inc(g.dma_start(out=c2_in[:], in_=c2s[:]), "gd", 16)
        s.w(g, "gd")
        s.inc(g.collective_compute("AllGather", ALU.bypass, replica_groups=PAIRS,
                                   ins=[c2_in.ap().opt()], outs=[c2_out.ap().opt()]), "cc", 1)
        s.w(g, "cc")
        s.inc(g.dma_start(out=S_in[:], in_=c2_out[0:NA, :]), "gd", 16)
    blk.gpsimd(gp4)

    def ve13(v):
        s.w(v, "gd")
        s.inc(v.tensor_scalar(S_in[:], S_in[:], rrecv_t[:, 0:1], None, ALU.mult), "vs", 1)
    blk.vector(ve13)
    dbg("Scum", Scum[:, :], [NA, E], F32)
    dbg("S_in", S_in[:, :], [NA, E], BF16)
    s.pop()
    if upto == "E1":
        return fin()

    s.push()
    tril_t = s.sb("tril_t", [CH, CH], F32)
    Dbt = s.sb("Dbt", [128, E], BF16)
    Xld2 = s.sb("Xld2", [128, 32 * 128], BF16)
    zld2 = s.sb("zld2", [128, 32 * 128], BF16)
    Xtm2 = s.sb("Xtm2", [128, E], BF16)
    ztm2 = s.sb("ztm2", [128, E], BF16)
    snapc = s.sb("snapc", [NA, E], BF16)
    Ytm = s.sb("Ytm", [128, E], F32)
    ytmbf = s.sb("ytmbf", [128, E], BF16)
    ygc = s.sb("ygc", [128, 32 * 128], BF16)
    CBt = s.sb("CBt", [128, CH], F32)
    expcsT2 = s.sb("expcsT2", [128, H], F32)
    Lh_w = s.sb("Lh_w", [128, 16 * CH], F32)
    csrow_q = [s.sb(f"csrq{i}", [1, 16 * CH], F32) for i in range(2)]
    Mch_w = s.sb("Mch_w", [128, 16 * CH], BF16)
    seff_w = s.sb("seff_w", [NA, 16 * PDIM], BF16)
    psY = s.ps("psY", [128, 512], F32)
    psOf = s.ps("psOf", [128, 512], F32)
    psCB = s.ps("psCB", [128, 128], F32)
    psXPw = s.ps("psXPw", [128, 16 * CH], F32)
    pstb2 = s.ps("pstb2", [128, 1024], BF16)

    def sy5(sx):
        s.inc(sx.dma_start(out=tril_t[:], in_=tril_d[:]), "ld", 16)
    blk.sync(sy5)
    s.push()
    darow = s.sb("darow", [1, 512], BF16)
    for st in range(8):
        def gpd(g, st=st):
            if st >= 1:
                s.w(g, "ts")
            s.inc(g.dma_start(out=darow[:], in_=D_a_row[0:1, 512 * st:512 * (st + 1)]), "gd", 16)
        blk.gpsimd(gpd)

        def te6(t, st=st):
            s.w(t, "ld")
            s.w(t, "gd")
            if st >= 1:
                s.w(t, "ss")
            s.inc(t.matmul(psXPw[:, 0:512], ones1[:], darow[:], start=True, stop=True), "ts", 1)
        blk.tensor(te6)

        def sc5(scx, st=st):
            s.w(scx, "ts")
            s.inc(scx.copy(Dbt[:, 512 * st:512 * (st + 1)], psXPw[:, 0:512]), "ss", 1)
        blk.scalar(sc5)
    s.pop()

    psXP_free = [0]
    LM_free = [0]
    se_free = [0]
    psY_free = [0, 0]
    csq_free = [0, 0]
    for c in range(NCH):
        c0 = c * CH

        def gp5(g, c=c, c0=c0):
            s.w(g, "vs")
            s.w(g, "ts")
            s.inc(g.dma_start(out=Xld2[:].rearrange("p (k m) -> p k m", k=32),
                              in_=X_spill[:, c0:c0 + CH].rearrange("(k p) m -> p k m", p=128)), "gd", 16)
            s.inc(g.dma_start(out=zld2[:].rearrange("p (k m) -> p k m", k=32),
                              in_=z_spill[:, c0:c0 + CH].rearrange("(k p) m -> p k m", p=128)), "gd", 16)
            s.inc(g.dma_start(out=snapc[:], in_=snap_spill[NA * c:NA * (c + 1), :]), "gd", 16)
        blk.gpsimd(gp5)

        for gb in range(4):
            def te7(t, c=c, gb=gb):
                if gb == 0:
                    s.w(t, "gd")
                    s.w(t, "ss")
                s.w(t, "vs")
                for j in range(8):
                    s.inc(t.transpose(pstb2[:, 128 * j:128 * (j + 1)],
                                      Xld2[:, 128 * (8 * gb + j):128 * (8 * gb + j + 1)], identt[:]), "ts", 1)
            blk.tensor(te7)

            def ve15(v, gb=gb):
                s.w(v, "ts")
                s.inc(v.tensor_copy(Xtm2[:, 1024 * gb:1024 * (gb + 1)], pstb2[:]), "vs", 1)
            blk.vector(ve15)
        for gb in range(4):
            def te8(t, c=c, gb=gb):
                s.w(t, "vs")
                for j in range(8):
                    s.inc(t.transpose(pstb2[:, 128 * j:128 * (j + 1)],
                                      zld2[:, 128 * (8 * gb + j):128 * (8 * gb + j + 1)], identt[:]), "ts", 1)
            blk.tensor(te8)

            def ve16(v, gb=gb):
                s.w(v, "ts")
                s.inc(v.tensor_copy(ztm2[:, 1024 * gb:1024 * (gb + 1)], pstb2[:]), "vs", 1)
            blk.vector(ve16)

        def te9(t, c=c, c0=c0):
            s.w(t, "vs")
            s.inc(t.matmul(psCB[:], B_fm[:, c0:c0 + CH], C_fm[:, c0:c0 + CH], start=True, stop=True), "ts", 1)
        blk.tensor(te9)

        def ve17(v, c=c):
            s.w(v, "ts")
            s.w(v, "ld")
            s.inc(v.tensor_tensor(CBt[:], psCB[:], tril_t[:], ALU.mult), "vs", 1)
        blk.vector(ve17)

        def sc6(scx, c=c):
            s.w(scx, "vs")
            s.inc(scx.activation(expcsT2[:], csT[:, H * c:H * (c + 1)], AF.Exp), "ss", 1)
        blk.scalar(sc6)

        for hb in range(4):
            qi = hb % 2

            def syq(sx, c=c, c0=c0, hb=hb, qi=qi):
                s.w(sx, "ts", csq_free[qi])
                s.inc(sx.dma_start(out=csrow_q[qi].ap().rearrange("a (h l) -> a h l", h=16),
                                   in_=cs_dram[16 * hb:16 * (hb + 1), c0:c0 + CH]
                                   .rearrange("(a h) l -> a h l", a=1)), "ld", 16)
            blk.sync(syq)

            def te10(t, c=c, hb=hb, qi=qi):
                s.w(t, "vs", psXP_free[0])
                s.w(t, "ss")
                s.w(t, "ld")
                for j in range(16):
                    s.inc(t.matmul(psXPw[:, 128 * j:128 * (j + 1)], ones1f[:],
                                   csrow_q[qi][0:1, CH * j:CH * (j + 1)],
                                   start=True, stop=True), "ts", 1)
                csq_free[qi] = s.c["ts"]
            blk.tensor(te10)

            def ve19(v, c=c, hb=hb):
                s.w(v, "ts")
                s.w(v, "ss", LM_free[0])
                for j in range(16):
                    h = 16 * hb + j
                    s.inc(v.tensor_scalar(Lh_w[:, 128 * j:128 * (j + 1)], psXPw[:, 128 * j:128 * (j + 1)],
                                          csT[:, H * c + h:H * c + h + 1], 0.0,
                                          ALU.subtract, ALU.min), "vs", 1)
                psXP_free[0] = s.c["vs"]
            blk.vector(ve19)

            def sc7(scx, c=c, hb=hb):
                s.w(scx, "vs")
                s.inc(scx.activation(Lh_w[:], Lh_w[:], AF.Exp), "ss", 1)
                LM_free[0] = s.c["ss"]
            blk.scalar(sc7)

            def ve20(v, c=c, hb=hb):
                s.w(v, "ss")
                s.w(v, "ts", se_free[0])
                for j in range(16):
                    h = 16 * hb + j
                    s.inc(v.scalar_tensor_tensor(Mch_w[:, 128 * j:128 * (j + 1)], Lh_w[:, 128 * j:128 * (j + 1)],
                                                 dtT[:, H * c + h:H * c + h + 1],
                                                 CBt[:], ALU.mult, ALU.mult), "vs", 1)
                    s.inc(v.scalar_tensor_tensor(seff_w[:, 64 * j:64 * (j + 1)], S_in[:, 64 * h:64 * (h + 1)],
                                                 ecT[:, H * c + h:H * c + h + 1],
                                                 snapc[:, 64 * h:64 * (h + 1)], ALU.mult, ALU.add), "vs", 1)
            blk.vector(ve20)

            for wv in range(2):
                def te11(t, c=c, hb=hb, wv=wv, c0=c0):
                    s.w(t, "vs")
                    s.w(t, "vs", psY_free[wv])
                    for sl in range(8):
                        j = 8 * wv + sl
                        h = 16 * hb + j
                        s.inc(t.matmul(psY[:, 64 * sl:64 * (sl + 1)], Mch_w[:, 128 * j:128 * (j + 1)],
                                       Xtm2[:, 64 * h:64 * (h + 1)], start=True, stop=True), "ts", 1)
                        s.inc(t.matmul(psOf[:, 64 * sl:64 * (sl + 1)], C_fm[:, c0:c0 + CH],
                                       seff_w[:, 64 * j:64 * (j + 1)], start=True, stop=True), "ts", 1)
                    if wv == 1:
                        se_free[0] = s.c["ts"]
                blk.tensor(te11)

                def ve21(v, c=c, hb=hb, wv=wv):
                    s.w(v, "ts")
                    for sl in range(8):
                        j = 8 * wv + sl
                        h = 16 * hb + j
                        s.inc(v.tensor_scalar(Ytm[:, 64 * h:64 * (h + 1)], psOf[:, 64 * sl:64 * (sl + 1)],
                                              expcsT2[:, h:h + 1], None, ALU.mult), "vs", 1)
                        s.inc(v.tensor_tensor(Ytm[:, 64 * h:64 * (h + 1)], Ytm[:, 64 * h:64 * (h + 1)],
                                              psY[:, 64 * sl:64 * (sl + 1)], ALU.add), "vs", 1)
                        s.inc(v.scalar_tensor_tensor(Ytm[:, 64 * h:64 * (h + 1)], Xtm2[:, 64 * h:64 * (h + 1)],
                                                     Dbt[:, 64 * h:64 * h + 1], Ytm[:, 64 * h:64 * (h + 1)],
                                                     ALU.mult, ALU.add), "vs", 1)
                    psY_free[wv] = s.c["vs"]
                blk.vector(ve21)

        def ve22(v, c=c):
            s.inc(v.tensor_tensor(ytmbf[:], Ytm[:], ztm2[:], ALU.mult), "vs", 1)
        blk.vector(ve22)

        for gb in range(4):
            def te12(t, c=c, gb=gb):
                s.w(t, "vs")
                for j in range(8):
                    s.inc(t.transpose(pstb2[:, 128 * j:128 * (j + 1)],
                                      ytmbf[:, 128 * (8 * gb + j):128 * (8 * gb + j + 1)], identt[:]), "ts", 1)
            blk.tensor(te12)

            def ve23(v, gb=gb):
                s.w(v, "ts")
                s.inc(v.tensor_copy(ygc[:, 1024 * gb:1024 * (gb + 1)], pstb2[:]), "vs", 1)
            blk.vector(ve23)

        def sy6(sx, c=c, c0=c0):
            s.w(sx, "vs")
            s.inc(sx.dma_start(out=yg_spill[:, c0:c0 + CH].rearrange("(k p) m -> p k m", p=128),
                               in_=ygc[:].rearrange("p (k m) -> p k m", k=32)), "od", 16)
        blk.sync(sy6)
    s.pop()
    s.pop()
    s.pop()
    dbg("yg0", yg_spill[0:128, :], [128, TO], BF16)
    if upto == "E2":
        return fin()

    s.push()
    wbufF = s.sb("wbufF", [128, 32 * 1024], BF16)
    ygk = [s.sb(f"ygk{i}", [128, 512], BF16) for i in range(32)]
    psFb = [s.ps(f"psFb{i}", [128, 512], F32) for i in range(8)]
    yg_free = [0]
    psF_free = [0] * 8
    for dgw in range(2):
        def gpw(g, dgw=dgw):
            s.w(g, "ts")
            s.inc(g.dma_start(out=wbufF[:].rearrange("p (k m) -> p k m", k=32),
                              in_=W_outprojT[:, 1024 * dgw:1024 * (dgw + 1)]
                              .rearrange("(k p) m -> p k m", p=128)), "gd", 16)
        blk.gpsimd(gpw)
        for tt in range(2):
            def gp6(g, tt=tt, dgw=dgw):
                s.w(g, "ts", yg_free[0])
                for k in range(EG):
                    s.inc(g.dma_start(out=ygk[k][:],
                                      in_=yg_spill[128 * k:128 * (k + 1), 512 * tt:512 * (tt + 1)]), "gd", 16)
            blk.gpsimd(gp6)
            for k in range(EG):
                def te13(t, tt=tt, k=k):
                    s.w(t, "gd")
                    if k == 0:
                        for bi in range(8):
                            s.w(t, "vs", psF_free[bi])
                    for dg8 in range(8):
                        s.inc(t.matmul(psFb[dg8][:], wbufF[:, 1024 * k + 128 * dg8:1024 * k + 128 * (dg8 + 1)],
                                       ygk[k][:], start=(k == 0), stop=(k == EG - 1)), "ts", 1)
                    yg_free[0] = s.c["ts"]
                blk.tensor(te13)

            for dg8 in range(8):
                def ve24(v, dgw=dgw, tt=tt, dg8=dg8):
                    s.w(v, "ts")
                    dg = dgw * 8 + dg8
                    s.inc(v.tensor_tensor(resid[dg][:, 512 * tt:512 * (tt + 1)],
                                          resid[dg][:, 512 * tt:512 * (tt + 1)],
                                          psFb[dg8][:], ALU.add), "vs", 1)
                    psF_free[dg8] = s.c["vs"]
                blk.vector(ve24)
    s.pop()
    dbg("resid1", resid[0][:, :], [128, TO], F32)
    if upto == "F":
        return fin()

    def ffn_phase(tag, first):
        s.push()
        ln2 = [s.sb(f"{tag}ln2_{i}", [128, TO], BF16) for i in range(DG)]
        r2 = s.sb(f"{tag}r2", [1, TO], F32)
        n2 = s.sb(f"{tag}n2", [1, TO], F32)
        r2b = s.sb(f"{tag}r2b", [128, TO], F32)
        n2b = s.sb(f"{tag}n2b", [128, TO], F32)
        tmpg = s.sb(f"{tag}tmpg", [128, TO], F32)
        sgw = s.sb(f"{tag}sgw", [128, 16 * 128], BF16)
        s.push()
        ln_stats(f"{tag}l2", resid, TO, r2, n2)
        expand_f32(f"{tag}r2x", r2, TO, lambda a, w: r2b[:, a:a + w])
        expand_f32(f"{tag}n2x", n2, TO, lambda a, w: n2b[:, a:a + w])
        for pg in range(DG):
            def ve(v, pg=pg):
                if pg == 0:
                    s.w(v, "ss")
                s.inc(v.tensor_tensor(tmpg[:], resid[pg][:], r2b[:], ALU.mult), "vs", 1)
                s.inc(v.tensor_tensor(tmpg[:], tmpg[:], n2b[:], ALU.add), "vs", 1)
                s.inc(v.scalar_tensor_tensor(ln2[pg][:], tmpg[:], flnw[:, pg:pg + 1],
                                             flnb[:, pg:pg + 1].broadcast_to([128, TO]),
                                             ALU.mult, ALU.add), "vs", 1)
            blk.vector(ve)

        if first:
            amax = s.sb("amax", [128, DG], F32)
            wraw = s.sb(f"{tag}wraw", [128, 16 * 128], BF16)
            sg_od = [0]
            for mo in range(DG):
                def gp(g, mo=mo):
                    s.w(g, "vs")
                    s.w(g, "ss")
                    s.inc(g.dma_start(out=wraw[:].rearrange("p (k m) -> p k m", k=16),
                                      in_=ffn_WT[:, 128 * mo:128 * (mo + 1)]
                                      .rearrange("(k p) m -> p k m", p=128)), "gd", 16)
                blk.gpsimd(gp)

                def ve25(v, mo=mo):
                    s.w(v, "gd")
                    s.inc(v.tensor_reduce(amax[:, mo:mo + 1], wraw[:], mybir.AxisListType.X,
                                          ALU.max, apply_absolute_value=True), "vs", 1)
                blk.vector(ve25)

                def sc8(scx, mo=mo):
                    s.w(scx, "gd")
                    s.w(scx, "od", sg_od[0])
                    s.inc(scx.activation(sgw[:], wraw[:], AF.Sign), "ss", 1)
                blk.scalar(sc8)

                def sy7(sx, mo=mo):
                    s.w(sx, "ss")
                    s.inc(sx.dma_start(out=signW_spill[:, 128 * mo:128 * (mo + 1)]
                                       .rearrange("(k p) m -> p k m", p=128),
                                       in_=sgw[:].rearrange("p (k m) -> p k m", k=16)), "od", 16)
                    sg_od[0] = s.c["od"]
                blk.sync(sy7)
            am1 = s.sb("am1", [128, 1], F32)
            amaxc = s.sb("amaxc", [128, DG], F32)
            amr = s.sb("amr", [1, 128], F32)
            amrc = s.sb("amrc", [1, 128], F32)
            ams = s.sb("ams", [1, 1], F32)
            psam = s.ps("psam", [128, 128], F32)

            def sc26(scx):
                s.w(scx, "vs")
                s.inc(scx.copy(amaxc[:], amax[:]), "ss", 1)
            blk.scalar(sc26)

            def ve26(v):
                s.w(v, "ss")
                s.inc(v.tensor_reduce(am1[:], amaxc[:], mybir.AxisListType.X, ALU.max), "vs", 1)
            blk.vector(ve26)

            def te14(t):
                s.w(t, "vs")
                s.inc(t.transpose(psam[0:1, 0:128], am1[:], identf[:]), "ts", 1)
            blk.tensor(te14)

            def ve27a(v):
                s.w(v, "ts")
                s.inc(v.tensor_copy(amr[:], psam[0:1, 0:128]), "vs", 1)
            blk.vector(ve27a)

            def sc27(scx):
                s.w(scx, "vs")
                s.inc(scx.copy(amrc[:], amr[:]), "ss", 1)
            blk.scalar(sc27)

            def ve27(v):
                s.w(v, "ss")
                s.inc(v.tensor_reduce(ams[:], amrc[:], mybir.AxisListType.X, ALU.max), "vs", 1)
            blk.vector(ve27)

            def te15(t):
                s.w(t, "vs")
                s.inc(t.matmul(psam[:, 0:1], ones1f[:], ams[:], start=True, stop=True), "ts", 1)
            blk.tensor(te15)

            def ve28(v):
                s.w(v, "ts")
                s.inc(v.tensor_scalar(gscale[:], gscale[:], psam[:, 0:1], None, ALU.mult), "vs", 1)
            blk.vector(ve28)

        s.pop()
        if first:
            dbg(tag + "amax", amax[:, :], [128, DG], F32)
            dbg(tag + "am1", am1[:, :], [128, 1], F32)
            dbg(tag + "ams", ams[:, :], [1, 1], F32)
        dbg(tag + "gsc", gscale[:, :], [128, DG], F32)
        dbg(tag + "rstd", r2[0:1, :], [1, TO], F32)
        dbg(tag + "nmr", n2[0:1, :], [1, TO], F32)
        dbg(tag + "ln2", ln2[0][:, :], [128, TO], BF16)
        psG = [s.ps(f"{tag}psG{i}", [128, 512], F32) for i in range(8)]
        bank_G = [0] * 8
        sg_free = [0]
        for dg in range(DG):
            def gp7(g, dg=dg):
                s.w(g, "ts", sg_free[0])
                if dg == 0:
                    s.w(g, "od")
                    s.w(g, "ss")
                s.inc(g.dma_start(out=sgw[:].rearrange("p (k m) -> p k m", k=16),
                                  in_=signW_spill[:, 128 * dg:128 * (dg + 1)]
                                  .rearrange("(k p) m -> p k m", p=128)), "gd", 16)
            blk.gpsimd(gp7)
            bids = [(2 * dg) % 8, (2 * dg + 1) % 8]

            def te16(t, dg=dg, bids=bids):
                s.w(t, "gd")
                if dg == 0:
                    s.w(t, "vs")
                for bi in bids:
                    s.w(t, "vs", bank_G[bi])
                for k in range(DG):
                    for ti in range(2):
                        s.inc(t.matmul(psG[bids[ti]][:], sgw[:, 128 * k:128 * (k + 1)],
                                       ln2[k][:, 512 * ti:512 * (ti + 1)],
                                       start=(k == 0), stop=(k == DG - 1)), "ts", 1)
                sg_free[0] = s.c["ts"]
            blk.tensor(te16)

            def ve29(v, dg=dg, bids=bids):
                s.w(v, "ts")
                for ti in range(2):
                    s.inc(v.scalar_tensor_tensor(tmpg[:, 0:512], psG[bids[ti]][:], gscale[:, dg:dg + 1],
                                                 fbeta[:, dg:dg + 1].broadcast_to([128, 512]),
                                                 ALU.mult, ALU.add), "vs", 1)
                    s.inc(v.tensor_tensor(resid[dg][:, 512 * ti:512 * (ti + 1)],
                                          resid[dg][:, 512 * ti:512 * (ti + 1)],
                                          tmpg[:, 0:512], ALU.add), "vs", 1)
                for bi in bids:
                    bank_G[bi] = s.c["vs"]
            blk.vector(ve29)
        s.pop()

    ffn_phase("g", True)
    dbg("resid2", resid[0][:, :], [128, TO], F32)
    if upto == "G":
        return fin()

    s.push()
    rstd3_bt = s.sb("rstd3_bt", [128, TO], F32)
    nmr3_bt = s.sb("nmr3_bt", [128, TO], F32)
    xcb = [s.sb(f"xcb{eg}", [128, TO], BF16) for eg in range(EGL)]

    s.push()
    rn = [s.sb(f"rn{i}", [128, TO], BF16) for i in range(DG)]
    rn_warm = [s.sb(f"rnw{i}", [128, WARM], BF16) for i in range(DG)]
    s.push()
    r3 = s.sb("r3", [1, TO], F32)
    n3 = s.sb("n3", [1, TO], F32)
    ln_stats("l3", resid, TO, r3, n3)
    expand_f32("r3x", r3, TO, lambda a, w: rstd3_bt[:, a:a + w])
    expand_f32("n3x", n3, TO, lambda a, w: nmr3_bt[:, a:a + w])
    ln3t = s.sb("ln3t", [128, TO], F32)
    rr3 = s.sb("rr3", [1, TO], F32)
    rr3b = s.sb("rr3b", [128, TO], F32)
    sq3 = s.sb("sq3", [128, 512], F32)
    ps3 = s.ps("ps3", [1, 512], F32)
    for st in range(2):
        c0 = st * 512
        for pg in range(DG):
            def ve(v, pg=pg, st=st, c0=c0):
                s.w(v, "ss")
                s.inc(v.tensor_tensor(ln3t[:, 0:512], resid[pg][:, c0:c0 + 512],
                                      rstd3_bt[:, c0:c0 + 512], ALU.mult), "vs", 1)
                s.inc(v.tensor_tensor(ln3t[:, 0:512], ln3t[:, 0:512],
                                      nmr3_bt[:, c0:c0 + 512], ALU.add), "vs", 1)
            blk.vector(ve)

            def scq(scx, pg=pg):
                s.w(scx, "vs")
                if pg > 0:
                    s.w(scx, "ts")
                s.inc(scx.activation(sq3[:], ln3t[:, 0:512], AF.Square), "ss", 1)
            blk.scalar(scq)

            def te(t, pg=pg):
                s.w(t, "ss")
                s.inc(t.matmul(ps3[:], onescol[:], sq3[:], start=(pg == 0), stop=(pg == DG - 1)), "ts", 1)
            blk.tensor(te)

        def ve30(v, c0=c0):
            s.w(v, "ts")
            s.inc(v.tensor_scalar(rr3[:, c0:c0 + 512], ps3[:], 1.0 / D, None, ALU.mult), "vs", 1)
        blk.vector(ve30)

    def sc9(scx):
        s.w(scx, "vs")
        s.inc(scx.activation(rr3[:], rr3[:], AF.Ln, bias=eps_t[0:1, 0:1]), "ss", 1)
        s.inc(scx.activation(rr3[:], rr3[:], AF.Exp, scale=-0.5), "ss", 1)
    blk.scalar(sc9)
    expand_f32("rr3x", rr3, TO, lambda a, w: rr3b[:, a:a + w])
    for pg in range(DG):
        def ve(v, pg=pg):
            if pg == 0:
                s.w(v, "ss")
            s.inc(v.tensor_tensor(ln3t[:], resid[pg][:], rstd3_bt[:], ALU.mult), "vs", 1)
            s.inc(v.tensor_tensor(ln3t[:], ln3t[:], nmr3_bt[:], ALU.add), "vs", 1)
            s.inc(v.tensor_tensor(ln3t[:], ln3t[:], rr3b[:], ALU.mult), "vs", 1)
            s.inc(v.tensor_scalar(rn[pg][:], ln3t[:], nrmw[:, pg:pg + 1], None, ALU.mult), "vs", 1)
        blk.vector(ve)
    dbg("hr3", r3[0:1, :], [1, TO], F32)
    dbg("hn3", n3[0:1, :], [1, TO], F32)
    dbg("hrr3", rr3[0:1, :], [1, TO], F32)
    s.pop()
    dbg("rn", rn[0][:, :], [128, TO], BF16)

    def gp8(g):
        s.w(g, "vs")
        for i in range(DG):
            s.inc(g.dma_start(out=c1_in[128 * i:128 * (i + 1), :], in_=rn[i][:]), "gd", 16)
        s.w(g, "gd")
        s.inc(g.collective_compute("AllGather", ALU.bypass, replica_groups=PAIRS,
                                   ins=[c1_in.ap().opt()], outs=[c1_out.ap().opt()]), "cc", 1)
        s.w(g, "cc")
        for i in range(DG):
            s.inc(g.dma_start(out=rn_warm[i][:], in_=c1_out[128 * i:128 * (i + 1), TO - WARM:TO]), "gd", 16)
    blk.gpsimd(gp8)

    def ve31(v):
        s.w(v, "gd")
        for i in range(DG):
            s.inc(v.tensor_scalar(rn_warm[i][:], rn_warm[i][:], rrecv_t[:, 0:1], None, ALU.mult), "vs", 1)
    blk.vector(ve31)

    s.push()
    rnh = [s.sb(f"rnh{i}", [128, TO], BF16) for i in range(DG)]
    wbufH = [s.sb(f"wbufH{i}", [128, 16 * 128], BF16) for i in range(2)]
    sktile = s.sb("sktile", [128, TO], BF16)
    psH = [s.ps(f"psH{i}", [128, 512], F32) for i in range(8)]
    bank_H = [0] * 8
    w_H = [0, 0]
    sk_od = [0]
    for thalf in range(2):
        def gph(g, thalf=thalf):
            s.w(g, "cc")
            if thalf > 0:
                s.w(g, "ts")
            for i in range(DG):
                s.inc(g.dma_start(out=rnh[i][:],
                                  in_=c1_out[thalf * D + 128 * i:thalf * D + 128 * (i + 1), :]), "gd", 16)
        blk.gpsimd(gph)
        for eg in range(EGL):
            wb = wbufH[eg % 2]

            def gp(g, eg=eg, wb=wb, thalf=thalf):
                s.w(g, "ts", w_H[eg % 2])
                s.inc(g.dma_start(out=wb[:].rearrange("p (k m) -> p k m", k=16),
                                  in_=W_skip[:, 128 * eg:128 * (eg + 1)]
                                  .rearrange("(k p) m -> p k m", p=128)), "gd", 16)
            blk.gpsimd(gp)
            bids = [(2 * eg) % 8, (2 * eg + 1) % 8]

            def te(t, eg=eg, wb=wb, bids=bids):
                s.w(t, "gd")
                for bi in bids:
                    s.w(t, "ss", bank_H[bi])
                for k in range(DG):
                    for ti in range(2):
                        s.inc(t.matmul(psH[bids[ti]][:], wb[:, 128 * k:128 * (k + 1)],
                                       rnh[k][:, 512 * ti:512 * (ti + 1)],
                                       start=(k == 0), stop=(k == DG - 1)), "ts", 1)
                w_H[eg % 2] = s.c["ts"]
            blk.tensor(te)

            def sc(scx, eg=eg, bids=bids):
                s.w(scx, "ts")
                s.w(scx, "od", sk_od[0])
                for ti in range(2):
                    s.inc(scx.activation(sktile[:, 512 * ti:512 * (ti + 1)], psH[bids[ti]][:], AF.Silu), "ss", 1)
                for bi in bids:
                    bank_H[bi] = s.c["ss"]
            blk.scalar(sc)

            def sy(sx, eg=eg, thalf=thalf):
                s.w(sx, "ss")
                s.inc(sx.dma_start(out=skip_spill[128 * eg:128 * (eg + 1), thalf * TO:(thalf + 1) * TO],
                                   in_=sktile[:]), "od", 16)
                sk_od[0] = s.c["od"]
            blk.sync(sy)
    s.pop()

    s.push()
    s.push()
    xcbh_t = [s.sb(f"xcbht{i}", [128, TO], BF16) for i in range(2)]
    wbufI = [s.sb(f"wbufI{i}", [128, 16 * 128], BF16) for i in range(2)]
    xin_eg = [s.sb(f"xin{i}", [128, WARM + TO], BF16) for i in range(2)]
    cacc2 = s.sb("cacc2", [128, TO], F32)
    psI = [s.ps(f"psI{i}", [128, 512], F32) for i in range(6)]
    bank_I = [0] * 6
    w_I = [0, 0]
    xin_I = [0, 0]
    xcbh_od = [0, 0]
    for eg in range(EG):
        wb = wbufI[eg % 2]
        xie = xin_eg[eg % 2]

        def gp(g, eg=eg, wb=wb):
            s.w(g, "ts", w_I[eg % 2])
            s.inc(g.dma_start(out=wb[:].rearrange("p (k m) -> p k m", k=16),
                              in_=W_in[:, 128 * eg:128 * (eg + 1)]
                              .rearrange("(k p) m -> p k m", p=128)), "gd", 16)
        blk.gpsimd(gp)
        bids = [(3 * eg) % 6, (3 * eg + 1) % 6, (3 * eg + 2) % 6]

        def te(t, eg=eg, wb=wb, bids=bids):
            s.w(t, "gd")
            for bi in bids:
                s.w(t, "ss", bank_I[bi])
            for k in range(DG):
                s.inc(t.matmul(psI[bids[0]][:, 0:WARM], wb[:, 128 * k:128 * (k + 1)], rn_warm[k][:],
                               start=(k == 0), stop=(k == DG - 1)), "ts", 1)
                for ti in range(2):
                    s.inc(t.matmul(psI[bids[1 + ti]][:], wb[:, 128 * k:128 * (k + 1)],
                                   rn[k][:, 512 * ti:512 * (ti + 1)],
                                   start=(k == 0), stop=(k == DG - 1)), "ts", 1)
            w_I[eg % 2] = s.c["ts"]
        blk.tensor(te)

        def sc(scx, eg=eg, bids=bids, xie=xie):
            s.w(scx, "ts")
            s.w(scx, "vs", xin_I[eg % 2])
            s.inc(scx.copy(xie[:, 0:WARM], psI[bids[0]][:, 0:WARM]), "ss", 1)
            for ti in range(2):
                s.inc(scx.copy(xie[:, WARM + 512 * ti:WARM + 512 * (ti + 1)], psI[bids[1 + ti]][:]), "ss", 1)
            for bi in bids:
                bank_I[bi] = s.c["ss"]
        blk.scalar(sc)

        def ve(v, eg=eg, xie=xie):
            s.w(v, "ss")
            s.inc(v.tensor_scalar(cacc2[:], xie[:, WARM - 3:WARM - 3 + TO],
                                  cbw[:, 4 * eg:4 * eg + 1], None, ALU.mult), "vs", 1)
            for k in range(1, 4):
                s.inc(v.scalar_tensor_tensor(cacc2[:], xie[:, WARM - 3 + k:WARM - 3 + k + TO],
                                             cbw[:, 4 * eg + k:4 * eg + k + 1], cacc2[:],
                                             ALU.mult, ALU.add), "vs", 1)
            xin_I[eg % 2] = s.c["vs"]
        blk.vector(ve)

        if eg < EGL:
            def sc2(scx, eg=eg):
                s.w(scx, "vs")
                s.inc(scx.activation(xcb[eg][:], cacc2[:], AF.Silu, bias=cbb[:, eg:eg + 1]), "ss", 1)
            blk.scalar(sc2)
        else:
            def sc2(scx, eg=eg):
                s.w(scx, "vs")
                s.w(scx, "od", xcbh_od[eg % 2])
                s.inc(scx.activation(xcbh_t[eg % 2][:], cacc2[:], AF.Silu, bias=cbb[:, eg:eg + 1]), "ss", 1)
            blk.scalar(sc2)

            def syh(sx, eg=eg):
                s.w(sx, "ss")
                s.inc(sx.dma_start(out=c3b_in[128 * (eg - EGL):128 * (eg - EGL + 1), :],
                                   in_=xcbh_t[eg % 2][:]), "od", 16)
                xcbh_od[eg % 2] = s.c["od"]
            blk.sync(syh)
    s.pop()
    dbg("xcb0", xcb[0][:, :], [128, TO], BF16)

    wd1 = s.sb("wd1", [128, 32 * 128], BF16)
    wBt = s.sb("wBt", [128, 32 * 16], BF16)
    wCt = s.sb("wCt", [128, 32 * 16], BF16)
    d1t = s.sb("d1t", [128, TO], BF16)
    B2t = s.sb("B2t", [NB, TO], F32)
    C2tt = s.sb("C2tt", [NB, TO], F32)
    psJ = [s.ps(f"psJ{i}", [128, 512], F32) for i in range(6)]

    def gp9(g):
        s.inc(g.dma_start(out=wd1[:].rearrange("p (k m) -> p k m", k=32),
                          in_=W_d1.ap().rearrange("(k p) m -> p k m", p=128)), "gd", 16)
        s.inc(g.dma_start(out=wBt[:].rearrange("p (k m) -> p k m", k=32),
                          in_=W_Bp.ap().rearrange("(k p) m -> p k m", p=128)), "gd", 16)
        s.inc(g.dma_start(out=wCt[:].rearrange("p (k m) -> p k m", k=32),
                          in_=W_Cp.ap().rearrange("(k p) m -> p k m", p=128)), "gd", 16)
    blk.gpsimd(gp9)

    xhk = [s.sb(f"xhk{i}", [128, TO], BF16) for i in range(2)]
    xhk_free = [0, 0]
    for ti in range(2):
        for k in range(EG):
            if k >= EGL:
                def gph2(g, k=k, ti=ti):
                    s.w(g, "ts", xhk_free[k % 2])
                    if k == EGL and ti == 0:
                        s.w(g, "od")
                    s.inc(g.dma_start(out=xhk[k % 2][:],
                                      in_=c3b_in[128 * (k - EGL):128 * (k - EGL + 1), :]), "gd", 16)
                blk.gpsimd(gph2)

            def te17(t, k=k, ti=ti):
                s.w(t, "gd")
                if k == 0 and ti == 0:
                    s.w(t, "ss")
                rhs = xcb[k][:, 512 * ti:512 * (ti + 1)] if k < EGL \
                    else xhk[k % 2][:, 512 * ti:512 * (ti + 1)]
                s.inc(t.matmul(psJ[ti][:], wd1[:, 128 * k:128 * (k + 1)], rhs,
                               start=(k == 0), stop=(k == EG - 1)), "ts", 1)
                s.inc(t.matmul(psJ[2 + ti][0:16, :], wBt[:, 16 * k:16 * (k + 1)], rhs,
                               start=(k == 0), stop=(k == EG - 1)), "ts", 1)
                s.inc(t.matmul(psJ[4 + ti][0:16, :], wCt[:, 16 * k:16 * (k + 1)], rhs,
                               start=(k == 0), stop=(k == EG - 1)), "ts", 1)
                if k >= EGL:
                    xhk_free[k % 2] = s.c["ts"]
            blk.tensor(te17)

    def sc10(scx):
        s.w(scx, "ts")
        for ti in range(2):
            s.inc(scx.copy(d1t[:, 512 * ti:512 * (ti + 1)], psJ[ti][:]), "ss", 1)
            s.inc(scx.copy(B2t[:, 512 * ti:512 * (ti + 1)], psJ[2 + ti][0:16, :]), "ss", 1)
            s.inc(scx.copy(C2tt[:, 512 * ti:512 * (ti + 1)], psJ[4 + ti][0:16, :]), "ss", 1)
    blk.scalar(sc10)

    wd2 = s.sb("wd2", [128, 32 * 128], BF16)
    bdc = s.sb("bdc", [128, EG], F32)
    dtile = [s.sb(f"dtile{i}", [128, TO], F32) for i in range(2)]

    def gp10(g):
        s.w(g, "ss")
        s.inc(g.dma_start(out=wd2[:], in_=W_d2[:]), "gd", 16)
    blk.gpsimd(gp10)

    def sy9(sx):
        for eg in range(EG):
            s.inc(sx.dma_start(out=bdc[:, eg:eg + 1], in_=b_d2[128 * eg:128 * (eg + 1), :]), "ld", 16)
    blk.sync(sy9)
    dt_free = [0, 0]
    psJ_free = [0] * 6
    for mo in range(EG):
        dte = dtile[mo % 2]
        bids = [(2 * mo) % 6, (2 * mo + 1) % 6]

        def te18(t, mo=mo, bids=bids):
            if mo == 0:
                s.w(t, "gd")
                s.w(t, "ld")
            for bi in bids:
                s.w(t, "ss", psJ_free[bi])
            for ti in range(2):
                s.inc(t.matmul(psJ[bids[ti]][:], wd2[:, 128 * mo:128 * (mo + 1)],
                               d1t[:, 512 * ti:512 * (ti + 1)], start=True, stop=True), "ts", 1)
        blk.tensor(te18)

        def sc11(scx, mo=mo, bids=bids, dte=dte):
            s.w(scx, "ts")
            s.w(scx, "od", dt_free[mo % 2])
            for ti in range(2):
                s.inc(scx.activation(dte[:, 512 * ti:512 * (ti + 1)], psJ[bids[ti]][:],
                                     AF.Exp, bias=bdc[:, mo:mo + 1]), "ss", 1)
                s.inc(scx.activation(dte[:, 512 * ti:512 * (ti + 1)], dte[:, 512 * ti:512 * (ti + 1)],
                                     AF.Ln, bias=one_t[:, 0:1]), "ss", 1)
            for bi in bids:
                psJ_free[bi] = s.c["ss"]
        blk.scalar(sc11)

        def sy10(sx, mo=mo, dte=dte):
            s.w(sx, "ss")
            s.inc(sx.dma_start(out=delta_spill[128 * mo:128 * (mo + 1), :], in_=dte[:]), "od", 16)
            dt_free[mo % 2] = s.c["od"]
        blk.sync(sy10)

    def gp11(g):
        s.w(g, "od")
        s.inc(g.dma_start(out=c3f_in[0:EL, :], in_=delta_spill[EL:E, :]), "gd", 16)
    blk.gpsimd(gp11)

    def sy11(sx):
        s.w(sx, "ss")
        s.inc(sx.dma_start(out=c3f_in[EL:EL + NB, :], in_=B2t[:]), "od", 16)
        s.inc(sx.dma_start(out=c3f_in[EL + NB:EL + 2 * NB, :], in_=C2tt[:]), "od", 16)
    blk.sync(sy11)

    def gp12(g):
        s.w(g, "od")
        s.w(g, "gd")
        s.inc(g.collective_compute("AllGather", ALU.bypass, replica_groups=PAIRS,
                                   ins=[c3f_in.ap().opt()], outs=[c3f_out.ap().opt()]), "cc", 1)
        s.w(g, "cc")
        s.inc(g.collective_compute("AllGather", ALU.bypass, replica_groups=PAIRS,
                                   ins=[c3b_in.ap().opt()], outs=[c3b_out.ap().opt()]), "cc", 1)
    blk.gpsimd(gp12)

    B_full = s.sb("B_full", [NB, T], BF16)
    C_full = s.sb("C_full", [NB, T], BF16)

    def gp13(g):
        s.w(g, "cc")
        s.inc(g.dma_start(out=B_full[:, 0:TO], in_=c3f_out[EL:EL + NB, :]), "gd", 16)
        s.inc(g.dma_start(out=B_full[:, TO:T], in_=c3f_out[NF3 + EL:NF3 + EL + NB, :]), "gd", 16)
        s.inc(g.dma_start(out=C_full[:, 0:TO], in_=c3f_out[EL + NB:EL + 2 * NB, :]), "gd", 16)
        s.inc(g.dma_start(out=C_full[:, TO:T], in_=c3f_out[NF3 + EL + NB:NF3 + EL + 2 * NB, :]), "gd", 16)
    blk.gpsimd(gp13)

    def sy12(sx):
        s.w(sx, "gd")
        s.inc(sx.dma_start(out=Brow_dram[:], in_=B_full[:]), "od", 16)
        s.inc(sx.dma_start(out=Crow_dram[:], in_=C_full[:]), "od", 16)
    blk.sync(sy12)
    s.pop()
    s.pop()

    s.push()
    Brow_n = [s.sb(f"Brow_n{i}", [1, T], BF16) for i in range(2)]
    Crow_n = [s.sb(f"Crow_n{i}", [1, T], BF16) for i in range(2)]
    brow_free = [0, 0]
    dsc = s.sb("dsc", [128, T], F32)
    xsc = s.sb("xsc", [128, T], BF16)
    wsc = s.sb("wsc", [128, T], F32)
    d_loc = s.sb("d_loc", [128, TO], F32)
    d_s0 = s.sb("d_s0", [128, TO], F32)
    d_s1 = s.sb("d_s1", [128, TO], F32)
    x_s0 = s.sb("x_s0", [128, TO], BF16)
    x_s1 = s.sb("x_s1", [128, TO], BF16)
    a_n = s.sb("a_n", [128, T], F32)
    u_n = s.sb("u_n", [128, T], F32)
    h_n = s.sb("h_n", [128, T], F32)
    y_eg = s.sb("y_eg", [128, T], F32)
    sk_eg = s.sb("sk_eg", [128, T], BF16)
    ysk_eg = s.sb("ysk_eg", [128, T], BF16)
    psBt = s.ps("psBt", [128, 2048], F32)
    psCt = s.ps("psCt", [128, 2048], F32)
    ysk_od2 = [0]
    for eg in range(EGL):
        def gp(g, eg=eg):
            s.w(g, "vs")
            s.inc(g.dma_start(out=d_loc[:], in_=delta_spill[128 * eg:128 * (eg + 1), :]), "gd", 16)
            s.inc(g.dma_start(out=d_s0[:], in_=c3f_out[128 * eg:128 * (eg + 1), :]), "gd", 16)
            s.inc(g.dma_start(out=d_s1[:], in_=c3f_out[NF3 + 128 * eg:NF3 + 128 * (eg + 1), :]), "gd", 16)
            s.inc(g.dma_start(out=x_s0[:], in_=c3b_out[128 * eg:128 * (eg + 1), :]), "gd", 16)
            s.inc(g.dma_start(out=x_s1[:], in_=c3b_out[EL + 128 * eg:EL + 128 * (eg + 1), :]), "gd", 16)
            s.inc(g.dma_start(out=sk_eg[:], in_=skip_spill[128 * eg:128 * (eg + 1), :]), "gd", 16)
        blk.gpsimd(gp)

        def ve(v, eg=eg):
            s.w(v, "gd")
            mask = selr0_t[:, 0:1].broadcast_to([128, TO])
            s.inc(v.select(dsc[:, 0:TO], mask, d_loc[:], d_s0[:]), "vs", 1)
            s.inc(v.select(dsc[:, TO:T], mask, d_s1[:], d_loc[:]), "vs", 1)
            s.inc(v.select(xsc[:, 0:TO], mask, xcb[eg][:], x_s0[:]), "vs", 1)
            s.inc(v.select(xsc[:, TO:T], mask, x_s1[:], xcb[eg][:]), "vs", 1)
            s.inc(v.tensor_tensor(wsc[:], dsc[:], xsc[:], ALU.mult), "vs", 1)
        blk.vector(ve)
        for n in range(NB):
            n2i = n % 2

            def gpn(g, eg=eg, n=n, n2i=n2i):
                s.w(g, "ts", brow_free[n2i])
                if eg == 0 and n == 0:
                    s.w(g, "od")
                s.inc(g.dma_start(out=Brow_n[n2i][:], in_=Brow_dram[n:n + 1, :]), "gd", 16)
                s.inc(g.dma_start(out=Crow_n[n2i][:], in_=Crow_dram[n:n + 1, :]), "gd", 16)
            blk.gpsimd(gpn)

            def te19(t, eg=eg, n=n, n2i=n2i):
                s.w(t, "gd")
                s.w(t, "vs")
                for q4 in range(4):
                    s.inc(t.matmul(psBt[:, 512 * q4:512 * (q4 + 1)], ones1[:],
                                   Brow_n[n2i][0:1, 512 * q4:512 * (q4 + 1)],
                                   start=True, stop=True), "ts", 1)
                    s.inc(t.matmul(psCt[:, 512 * q4:512 * (q4 + 1)], ones1[:],
                                   Crow_n[n2i][0:1, 512 * q4:512 * (q4 + 1)],
                                   start=True, stop=True), "ts", 1)
                brow_free[n2i] = s.c["ts"]
            blk.tensor(te19)

            def sc12(scx, eg=eg, n=n):
                s.w(scx, "vs")
                s.inc(scx.activation(a_n[:], dsc[:], AF.Exp,
                                     scale=Ab_t[:, NB * eg + n:NB * eg + n + 1]), "ss", 1)
            blk.scalar(sc12)

            def ve32(v, eg=eg, n=n):
                s.w(v, "ts")
                s.w(v, "ss")
                s.inc(v.tensor_tensor(u_n[:], wsc[:], psBt[:], ALU.mult), "vs", 1)
                s.inc(v.tensor_tensor_scan(h_n[:], a_n[:], u_n[:], 0.0, ALU.mult, ALU.add), "vs", 1)
                if n == 0:
                    s.inc(v.tensor_tensor(y_eg[:], h_n[:], psCt[:], ALU.mult), "vs", 1)
                else:
                    s.inc(v.tensor_tensor(h_n[:], h_n[:], psCt[:], ALU.mult), "vs", 1)
                    s.inc(v.tensor_tensor(y_eg[:], y_eg[:], h_n[:], ALU.add), "vs", 1)
            blk.vector(ve32)

        def ve33(v, eg=eg):
            s.w(v, "od", ysk_od2[0])
            s.inc(v.scalar_tensor_tensor(y_eg[:], xsc[:], wdcol[:, eg:eg + 1], y_eg[:],
                                         ALU.mult, ALU.add), "vs", 1)
            s.inc(v.tensor_tensor(ysk_eg[:], y_eg[:], sk_eg[:], ALU.mult), "vs", 1)
        blk.vector(ve33)

        def sy13(sx, eg=eg):
            s.w(sx, "vs")
            s.inc(sx.dma_start(out=ysk_spill[128 * eg:128 * (eg + 1), :], in_=ysk_eg[:]), "od", 16)
            ysk_od2[0] = s.c["od"]
        blk.sync(sy13)
    s.pop()
    dbg("delta0", delta_spill[0:128, :], [128, TO], F32)
    dbg("ysk5", ysk_spill[128 * 5:128 * 6, :], [128, T], BF16)
    dbg("ysk15", ysk_spill[128 * 15:128 * 16, :], [128, T], BF16)
    dbg("skip0", skip_spill[0:128, :], [128, T], BF16)
    dbg("ysk0", ysk_spill[0:128, :], [128, T], BF16)
    dbg("ds0", c3f_out[0:128, :], [128, TO], F32)
    dbg("ds1", c3f_out[NF3:NF3 + 128, :], [128, TO], F32)

    s.push()
    wbufO = s.sb("wbufO", [128, 16 * 8 * 128], BF16)
    yskt = [s.sb(f"yskt{i}", [128, 512], BF16) for i in range(16)]
    otile = s.sb("otile", [128, 512], F32)
    dbg_ps = s.sb("dbg_ps", [128, 512], F32)
    dbg_yt = s.sb("dbg_yt", [128, 512], BF16)
    dbg_wo = s.sb("dbg_wo", [128, 128], BF16)
    psO2 = [s.ps(f"psO2_{i}", [128, 512], F32) for i in range(8)]
    ysk_free = [0, 0]
    psO2_free = [0] * 8
    for dgw in range(2):
        def gpw2(g, dgw=dgw):
            s.w(g, "ts")
            s.inc(g.dma_start(out=wbufO[:].rearrange("p (k m) -> p k m", k=16),
                              in_=W_out_in[0:EL, 1024 * dgw:1024 * (dgw + 1)]
                              .rearrange("(k p) m -> p k m", p=128)), "gd", 16)
        blk.gpsimd(gpw2)
        for tt in range(4):
            def gp15(g, tt=tt, dgw=dgw):
                s.w(g, "ts", ysk_free[0])
                for k in range(EGL):
                    s.inc(g.dma_start(out=yskt[k][:],
                                      in_=ysk_spill[128 * k:128 * (k + 1), 512 * tt:512 * (tt + 1)]), "gd", 16)
            blk.gpsimd(gp15)
            for k in range(EGL):
                def te20(t, tt=tt, k=k, dgw=dgw):
                    s.w(t, "gd")
                    if k == 0:
                        for bi in range(8):
                            s.w(t, "vs", psO2_free[bi])
                    for dg8 in range(8):
                        s.inc(t.matmul(psO2[dg8][:], wbufO[:, 1024 * k + 128 * dg8:1024 * k + 128 * (dg8 + 1)],
                                       yskt[k][:], start=(k == 0), stop=(k == EGL - 1)), "ts", 1)
                    ysk_free[0] = s.c["ts"]
                blk.tensor(te20)
                if dgw == 0 and tt == 0 and k == 0:
                    def vdbg(v):
                        s.w(v, "ts")
                        s.inc(v.tensor_copy(dbg_yt[:], yskt[0][:]), "vs", 1)
                        s.inc(v.tensor_copy(dbg_wo[:], wbufO[:, 0:128]), "vs", 1)
                    blk.vector(vdbg)
            for dg8 in range(8):
                def ve34(v, dgw=dgw, tt=tt, dg8=dg8):
                    s.w(v, "ts")
                    s.w(v, "od")
                    s.inc(v.tensor_copy(otile[:], psO2[dg8][:]), "vs", 1)
                    if dgw == 0 and tt == 0 and dg8 == 0:
                        s.inc(v.tensor_copy(dbg_ps[:], psO2[0][:]), "vs", 1)
                    psO2_free[dg8] = s.c["vs"]
                blk.vector(ve34)

                def sy14(sx, dgw=dgw, tt=tt, dg8=dg8):
                    s.w(sx, "vs")
                    dg = dgw * 8 + dg8
                    j = tt % 2
                    half = 0 if tt < 2 else 1
                    s.inc(sx.dma_start(out=c4_in[j][half * D + 128 * dg:half * D + 128 * (dg + 1), :],
                                       in_=otile[:]), "od", 16)
                blk.sync(sy14)
    dbg("h6ps", dbg_ps[:, :], [128, 512], F32)
    dbg("h6yt", dbg_yt[:, :], [128, 512], BF16)
    dbg("h6wo", dbg_wo[:, :], [128, 128], BF16)
    dbg("c4pre", c4_in[0][0:128, :], [128, 512], F32)
    dbg("c4preb", c4_in[0][D:D + 128, :], [128, 512], F32)
    for j in range(2):
        def gp16(g, j=j):
            s.w(g, "od")
            s.w(g, "cc")
            s.inc(g.collective_compute("ReduceScatter", ALU.add, replica_groups=PAIRS,
                                       ins=[c4_in[j].ap().opt()], outs=[c4_out[j].ap().opt()]), "cc", 1)
        blk.gpsimd(gp16)

    rs_t = s.sb("rs_t", [128, 512], F32)
    ln3b = s.sb("ln3b", [128, 512], F32)
    rs_free = [0]
    for j in range(2):
        for dg in range(DG):
            def gp17(g, j=j, dg=dg):
                s.w(g, "cc")
                s.w(g, "vs", rs_free[0])
                s.inc(g.dma_start(out=rs_t[:], in_=c4_out[j][128 * dg:128 * (dg + 1), :]), "gd", 16)
            blk.gpsimd(gp17)

            def ve35(v, j=j, dg=dg):
                s.w(v, "gd")
                s.inc(v.tensor_tensor(ln3b[:], resid[dg][:, 512 * j:512 * (j + 1)],
                                      rstd3_bt[:, 512 * j:512 * (j + 1)], ALU.mult), "vs", 1)
                s.inc(v.tensor_tensor(ln3b[:], ln3b[:], nmr3_bt[:, 512 * j:512 * (j + 1)], ALU.add), "vs", 1)
                s.inc(v.tensor_tensor(ln3b[:], ln3b[:], rs_t[:], ALU.add), "vs", 1)
                s.inc(v.tensor_tensor(resid[dg][:, 512 * j:512 * (j + 1)],
                                      resid[dg][:, 512 * j:512 * (j + 1)], ln3b[:], ALU.add), "vs", 1)
                rs_free[0] = s.c["vs"]
            blk.vector(ve35)
    s.pop()
    s.pop()
    dbg("resid3", resid[0][:, :], [128, TO], F32)
    dbg("c4o0", c4_out[0][0:128, :], [128, 512], F32)
    dbg("c4i0a", c4_in[0][0:128, :], [128, 512], F32)
    dbg("c4i0b", c4_in[0][D:D + 128, :], [128, 512], F32)
    if upto == "H":
        return fin()

    ffn_phase("i", False)
    return fin()


def _prep_core_inputs(core, inp):
    b, r = core // 2, core % 2
    t0 = r * TO
    f32 = np.float32

    def c(a):
        return np.ascontiguousarray(np.asarray(a), dtype=f32)

    x = np.asarray(inp["L0_attn_a_x"])
    xT = c(x[b, t0:t0 + TO, :].T)
    if r == 0:
        xwT = np.zeros((D, WARM), f32)
    else:
        xwT = c(x[b, t0 - WARM:t0, :].T)

    def roll(a, ax):
        return np.ascontiguousarray(np.roll(a, -r * EL, axis=ax))

    return {
        "xT": xT, "xwT": xwT,
        "W_inprojT": c(np.asarray(inp["L0_attn_a_in_proj_w"]).T),
        "W_xprojT": c(np.asarray(inp["L0_attn_a_x_proj_w"]).T),
        "conv_a_w": c(inp["L0_attn_a_conv_w"]),
        "conv_a_b": c(inp["L0_attn_a_conv_b"]).reshape(E, 1),
        "W_dtT": c(np.asarray(inp["L0_attn_a_dt_proj_w"]).T),
        "dt_b_col": c(inp["L0_attn_a_dt_proj_b"]).reshape(H, 1),
        "Alog_a": c(inp["L0_attn_a_A_log"]).reshape(H, 1),
        "D_a_row": np.repeat(c(inp["L0_attn_a_D"]), PDIM).reshape(1, E),
        "W_outprojT": c(np.asarray(inp["L0_attn_a_out_proj_w"]).T),
        "ffn_WT": c(np.asarray(inp["L0_ffn_weight"]).T),
        "ffn_gamma": c(inp["L0_ffn_gamma"]).reshape(D, 1),
        "ffn_beta": c(inp["L0_ffn_beta"]).reshape(D, 1),
        "ffn_lnw": c(inp["L0_ffn_ln_weight"]).reshape(D, 1),
        "ffn_lnb": c(inp["L0_ffn_ln_bias"]).reshape(D, 1),
        "norm_w": c(inp["L0_attn_b_norm_w"]).reshape(D, 1),
        "W_skip": roll(c(inp["L0_attn_b_W_skip"]), 1),
        "W_in": roll(c(inp["L0_attn_b_W_in"]), 1),
        "conv_b_w": roll(c(np.asarray(inp["L0_attn_b_conv_w"]).reshape(4, E).T), 0),
        "conv_b_b": roll(c(inp["L0_attn_b_conv_b"]), 0).reshape(E, 1),
        "W_d1": roll(c(inp["L0_attn_b_W_delta_1"]), 0),
        "W_d2": roll(c(inp["L0_attn_b_W_delta_2"]), 1),
        "b_d2": roll(c(inp["L0_attn_b_b_delta_2"]), 0).reshape(E, 1),
        "W_Bp": roll(c(inp["L0_attn_b_W_B"]), 0),
        "W_Cp": roll(c(inp["L0_attn_b_W_C"]), 0),
        "Alog_b": roll(c(inp["L0_attn_b_A_log"]), 0),
        "W_D_col": roll(c(inp["L0_attn_b_W_D"]), 0).reshape(E, 1),
        "W_out_in": roll(c(inp["L0_attn_b_W_out"]), 0),
        "m_send": np.full((128, 1), 1.0 - r, f32),
        "r_recv": np.full((128, 1), float(r), f32),
        "sel_r0": np.full((128, 1), 1 - r, np.uint8),
    }


def kernel(**inputs):
    nc = build_nc()
    in_maps = [_prep_core_inputs(cc, inputs) for cc in range(8)]
    res = run_bass_kernel_spmd(nc, in_maps, core_ids=list(range(8)))
    o = np.zeros((4, T, D), np.float32)
    for cc in range(8):
        b, r = cc // 2, cc % 2
        o[b, r * TO:(r + 1) * TO, :] = np.asarray(res.results[cc]["out"]).T
    return o


# version 04
# speedup vs baseline: 1.2945x; 1.0796x over previous
import numpy as np
import ml_dtypes
import concourse.bass as bass
import concourse.mybir as mybir
from concourse.bass_utils import run_bass_kernel_spmd

F32 = mybir.dt.float32
BF16 = mybir.dt.bfloat16
U8 = mybir.dt.uint8
AF = mybir.ActivationFunctionType
ALU = mybir.AluOpType

PAIRS = [[0, 1], [2, 3], [4, 5], [6, 7]]
D = 2048
E = 4096
EL = 2048
H = 64
PDIM = 64
NA = 128
NB = 16
T = 2048
TO = 1024
CH = 128
NCH = TO // CH
WARM = 128
DG = D // 128
EG = E // 128
EGL = EL // 128

bfd = ml_dtypes.bfloat16
SEMS = ["ld", "gd", "ts", "ss", "vs", "cc", "od"]


class Bld:
    def __init__(self, nc, blk):
        self.nc = nc
        self.blk = blk
        self.h = {n: nc.alloc_semaphore(n) for n in SEMS}
        self.c = {n: 0 for n in SEMS}
        self.scopes = []
        self.names = set()

    def inc(self, instr, name, n):
        instr.then_inc(self.h[name], n)
        self.c[name] += n
        return self.c[name]

    def w(self, eng, name, v=None):
        v = self.c[name] if v is None else v
        if v > 0:
            eng.wait_ge(self.h[name], v)

    def push(self):
        self.scopes.append([])

    def pop(self):
        self.join_all()
        for cm in reversed(self.scopes.pop()):
            cm.__exit__(None, None, None)

    def _alloc(self, cm):
        t = cm.__enter__()
        if self.scopes:
            self.scopes[-1].append(cm)
        return t

    def sb(self, name, shape, dt):
        assert name not in self.names, name
        self.names.add(name)
        return self._alloc(self.nc.sbuf_tensor(name, shape, dt))

    def ps(self, name, shape, dt=F32):
        assert name not in self.names, name
        self.names.add(name)
        return self._alloc(self.nc.psum_tensor(name, shape, dt))

    def join_all(self):
        snap = dict(self.c)
        for engname in ["sync", "gpsimd", "tensor", "scalar", "vector"]:
            def body(e, snap=snap):
                for n in SEMS:
                    if snap[n] > 0:
                        e.wait_ge(self.h[n], snap[n])
            getattr(self.blk, engname)(body)


def build_nc(upto=None, debug_keys=()):
    nc = bass.Bass("TRN2", target_bir_lowering=False)

    def di(name, shape, dt=F32):
        return nc.dram_tensor(name, shape, dt, kind="ExternalInput")

    xT = di("xT", [D, TO])
    xwT = di("xwT", [D, WARM])
    W_inprojT = di("W_inprojT", [D, 2 * E])
    W_xprojT = di("W_xprojT", [E, 4416])
    conv_a_w = di("conv_a_w", [E, 4])
    conv_a_b = di("conv_a_b", [E, 1])
    W_dtT = di("W_dtT", [H, H])
    dt_b_col = di("dt_b_col", [H, 1])
    Alog_a = di("Alog_a", [H, 1])
    D_a_row = di("D_a_row", [1, E])
    W_outprojT = di("W_outprojT", [E, D])
    ffn_WT = di("ffn_WT", [D, D])
    ffn_gamma = di("ffn_gamma", [D, 1])
    ffn_beta = di("ffn_beta", [D, 1])
    ffn_lnw = di("ffn_lnw", [D, 1])
    ffn_lnb = di("ffn_lnb", [D, 1])
    norm_w = di("norm_w", [D, 1])
    W_skip = di("W_skip", [D, E])
    W_in = di("W_in", [D, E])
    conv_b_w = di("conv_b_w", [E, 4])
    conv_b_b = di("conv_b_b", [E, 1])
    W_d1 = di("W_d1", [E, 128])
    W_d2 = di("W_d2", [128, E])
    b_d2 = di("b_d2", [E, 1])
    W_Bp = di("W_Bp", [E, NB])
    W_Cp = di("W_Cp", [E, NB])
    Alog_b = di("Alog_b", [E, NB])
    W_D_col = di("W_D_col", [E, 1])
    W_out_in = di("W_out_in", [E, D])
    m_send = di("m_send", [128, 1])
    r_recv = di("r_recv", [128, 1])
    sel_r0 = di("sel_r0", [128, 1], U8)

    out = nc.dram_tensor("out", [D, TO], F32, kind="ExternalOutput")

    def drm(name, shape, dt):
        return nc.dram_tensor(name, shape, dt)

    z_spill = drm("z_spill", [E, TO], BF16)
    X_spill = drm("X_spill", [E, TO], BF16)
    yg_spill = drm("yg_spill", [E, TO], BF16)
    signW_spill = drm("signW_spill", [D, D], BF16)
    skip_spill = drm("skip_spill", [EL, T], BF16)
    delta_spill = drm("delta_spill", [E, TO], F32)
    snap_spill = drm("snap_spill", [NCH * NA, E], BF16)
    cs_dram = drm("cs_dram", [H, TO], F32)
    ysk_spill = drm("ysk_spill", [EL, T], BF16)
    Brow_dram = drm("Brow_dram", [NB, T], BF16)
    Crow_dram = drm("Crow_dram", [NB, T], BF16)

    c2_in = drm("c2_in", [NA, E], BF16)
    c2_out = drm("c2_out", [2 * NA, E], BF16)
    c1_in = drm("c1_in", [D, TO], BF16)
    c1_out = drm("c1_out", [2 * D, TO], BF16)
    NF3 = EL + 2 * NB
    c3f_in = drm("c3f_in", [NF3, TO], F32)
    c3f_out = drm("c3f_out", [2 * NF3, TO], F32)
    c3b_in = drm("c3b_in", [EL, TO], BF16)
    c3b_out = drm("c3b_out", [2 * EL, TO], BF16)
    c4_in = [drm(f"c4_in{j}", [2 * D, 512], F32) for j in range(2)]
    c4_out = [drm(f"c4_out{j}", [D, 512], F32) for j in range(2)]

    ident_bf_d = nc.inline_tensor(np.eye(128, dtype=bfd), "ident_bf")
    ident_f_d = nc.inline_tensor(np.eye(128, dtype=np.float32), "ident_f")
    rst_np = np.ones((H, TO), np.float32)
    rst_np[:, ::CH] = 0.0
    rst_d = nc.inline_tensor(rst_np, "rst_a")
    tril_d = nc.inline_tensor(np.triu(np.ones((CH, CH), np.float32)), "tril")

    blk_cm = nc.Block()
    blk = blk_cm.__enter__()
    s = Bld(nc, blk)

    def dbg(key, ap, shape, dt=F32):
        if key in debug_keys:
            t = nc.dram_tensor("dbg_" + key, shape, dt, kind="ExternalOutput")

            def em(sy, t=t, ap=ap):
                for n in SEMS:
                    s.w(sy, n)
                s.inc(sy.dma_start(out=t[:], in_=ap), "od", 16)
            blk.sync(em)

    def fin():
        def sy(sx):
            s.w(sx, "vs")
            s.w(sx, "ss")
            s.w(sx, "ts")
            for i in range(DG):
                s.inc(sx.dma_start(out=out[128 * i:128 * (i + 1), :], in_=resid[i][:]), "od", 16)
            s.w(sx, "od")
        blk.sync(sy)
        blk_cm.__exit__(None, None, None)
        return nc

    resid = [s.sb(f"resid{i}", [128, TO], F32) for i in range(DG)]
    identt = s.sb("identt", [128, 128], BF16)
    identf = s.sb("identf", [128, 128], F32)
    ones1 = s.sb("ones1", [1, 128], BF16)
    ones1f = s.sb("ones1f", [1, 128], F32)
    onescol = s.sb("onescol", [128, 1], F32)
    eps_t = s.sb("eps_t", [128, 1], F32)
    one_t = s.sb("one_t", [128, 1], F32)
    cab = s.sb("cab", [128, EG], F32)
    cbb = s.sb("cbb", [128, EG], F32)
    caw = s.sb("caw", [128, 4 * EG], F32)
    cbw = s.sb("cbw", [128, 4 * EG], F32)
    Aa_t = s.sb("Aa_t", [H, 1], F32)
    gscale = s.sb("gscale", [128, DG], F32)
    fbeta = s.sb("fbeta", [128, DG], F32)
    flnw = s.sb("flnw", [128, DG], F32)
    flnb = s.sb("flnb", [128, DG], F32)
    nrmw = s.sb("nrmw", [128, DG], F32)
    wdcol = s.sb("wdcol", [128, EGL], F32)
    Ab_t = s.sb("Ab_t", [128, NB * EGL], F32)
    msend_t = s.sb("msend_t", [128, 1], F32)
    rrecv_t = s.sb("rrecv_t", [128, 1], F32)
    selr0_t = s.sb("selr0_t", [128, 1], U8)
    onescol_bf = s.sb("onescol_bf", [128, 1], BF16)

    def p0_ld(sy):
        for i in range(DG):
            s.inc(sy.dma_start(out=resid[i][:], in_=xT[128 * i:128 * (i + 1), :]), "ld", 16)
        s.inc(sy.dma_start(out=identt[:], in_=ident_bf_d[:]), "ld", 16)
        s.inc(sy.dma_start(out=identf[:], in_=ident_f_d[:]), "ld", 16)
        s.inc(sy.dma_start(out=Aa_t[:], in_=Alog_a[:]), "ld", 16)
        s.inc(sy.dma_start(out=msend_t[:], in_=m_send[:]), "ld", 16)
        s.inc(sy.dma_start(out=rrecv_t[:], in_=r_recv[:]), "ld", 16)
        s.inc(sy.dma_start(out=selr0_t[:], in_=sel_r0[:]), "ld", 16)
        for eg in range(EG):
            s.inc(sy.dma_start(out=caw[:, 4 * eg:4 * eg + 4], in_=conv_a_w[128 * eg:128 * (eg + 1), :]), "ld", 16)
            s.inc(sy.dma_start(out=cbw[:, 4 * eg:4 * eg + 4], in_=conv_b_w[128 * eg:128 * (eg + 1), :]), "ld", 16)
            s.inc(sy.dma_start(out=cab[:, eg:eg + 1], in_=conv_a_b[128 * eg:128 * (eg + 1), :]), "ld", 16)
            s.inc(sy.dma_start(out=cbb[:, eg:eg + 1], in_=conv_b_b[128 * eg:128 * (eg + 1), :]), "ld", 16)
        for i in range(DG):
            s.inc(sy.dma_start(out=gscale[:, i:i + 1], in_=ffn_gamma[128 * i:128 * (i + 1), :]), "ld", 16)
            s.inc(sy.dma_start(out=fbeta[:, i:i + 1], in_=ffn_beta[128 * i:128 * (i + 1), :]), "ld", 16)
            s.inc(sy.dma_start(out=flnw[:, i:i + 1], in_=ffn_lnw[128 * i:128 * (i + 1), :]), "ld", 16)
            s.inc(sy.dma_start(out=flnb[:, i:i + 1], in_=ffn_lnb[128 * i:128 * (i + 1), :]), "ld", 16)
            s.inc(sy.dma_start(out=nrmw[:, i:i + 1], in_=norm_w[128 * i:128 * (i + 1), :]), "ld", 16)
        for eg in range(EGL):
            s.inc(sy.dma_start(out=wdcol[:, eg:eg + 1], in_=W_D_col[128 * eg:128 * (eg + 1), :]), "ld", 16)
            s.inc(sy.dma_start(out=Ab_t[:, NB * eg:NB * (eg + 1)], in_=Alog_b[128 * eg:128 * (eg + 1), :]), "ld", 16)
    blk.sync(p0_ld)

    def p0_ve(v):
        s.w(v, "ld")
        s.inc(v.memset(ones1[:], 1.0), "vs", 1)
        s.inc(v.memset(ones1f[:], 1.0), "vs", 1)
        s.inc(v.memset(onescol[:], 1.0), "vs", 1)
        s.inc(v.memset(onescol_bf[:], 1.0), "vs", 1)
        s.inc(v.memset(eps_t[:], 1e-5), "vs", 1)
        s.inc(v.memset(one_t[:], 1.0), "vs", 1)
    blk.vector(p0_ve)

    def p0_sc(sc):
        s.w(sc, "ld")
        s.inc(sc.activation(Aa_t[:], Aa_t[:], AF.Exp), "ss", 1)
        s.inc(sc.activation(Ab_t[:], Ab_t[:], AF.Exp), "ss", 1)
    blk.scalar(p0_sc)

    def p0_ve2(v):
        s.w(v, "ss")
        s.inc(v.tensor_scalar(Aa_t[:], Aa_t[:], -1.0, None, ALU.mult), "vs", 1)
        s.inc(v.tensor_scalar(Ab_t[:], Ab_t[:], -1.0, None, ALU.mult), "vs", 1)
    blk.vector(p0_ve2)

    def expand_f32(tag, row_ap, wd, out_ap_fn):
        pse = s.ps(f"{tag}_pse", [128, 512], F32)
        nst = (wd + 511) // 512
        for st in range(nst):
            a = st * 512
            bw = min(wd, a + 512) - a

            def te(t, a=a, bw=bw):
                s.w(t, "vs")
                s.w(t, "ss")
                s.inc(t.matmul(pse[:, 0:bw], ones1f[:], row_ap[0:1, a:a + bw], start=True, stop=True), "ts", 1)
            blk.tensor(te)

            def scv(sc, a=a, bw=bw):
                s.w(sc, "ts")
                s.inc(sc.copy(out_ap_fn(a, bw), pse[:, 0:bw]), "ss", 1)
            blk.scalar(scv)

    def ln_stats(tag, tiles, tc, rstd_row, nmr_row, bf=False):
        srow = s.sb(f"{tag}_srow", [1, tc], F32)
        sqt = s.sb(f"{tag}_sqt", [128, 512], F32)
        ps_s = s.ps(f"{tag}_pss", [1, 512], F32)
        ps_q = s.ps(f"{tag}_psq", [1, 512], F32)
        oc = onescol_bf if bf else onescol
        nst = (tc + 511) // 512
        for st in range(nst):
            c0 = st * 512
            wd = min(tc, c0 + 512) - c0
            for pg in range(DG):
                def scq(sc, pg=pg, c0=c0, wd=wd):
                    s.w(sc, "vs")
                    s.w(sc, "ld")
                    if pg > 0:
                        s.w(sc, "ts")
                    s.inc(sc.activation(sqt[:, 0:wd], tiles[pg][:, c0:c0 + wd], AF.Square), "ss", 1)
                blk.scalar(scq)

                def te(t, pg=pg, c0=c0, wd=wd):
                    s.w(t, "ss")
                    s.inc(t.matmul(ps_s[:, 0:wd], oc[:], tiles[pg][:, c0:c0 + wd],
                                   start=(pg == 0), stop=(pg == DG - 1)), "ts", 1)
                    s.inc(t.matmul(ps_q[:, 0:wd], onescol[:], sqt[:, 0:wd],
                                   start=(pg == 0), stop=(pg == DG - 1)), "ts", 1)
                blk.tensor(te)

            def ve(v, c0=c0, wd=wd):
                s.w(v, "ts")
                s.inc(v.tensor_scalar(srow[:, c0:c0 + wd], ps_s[:, 0:wd], 1.0 / D, None, ALU.mult), "vs", 1)
                s.inc(v.tensor_scalar(rstd_row[:, c0:c0 + wd], ps_q[:, 0:wd], 1.0 / D, None, ALU.mult), "vs", 1)
            blk.vector(ve)

        def ve3(v):
            s.inc(v.tensor_tensor(nmr_row[:, 0:tc], srow[:], srow[:], ALU.mult), "vs", 1)
            s.inc(v.tensor_tensor(rstd_row[:, 0:tc], rstd_row[:, 0:tc], nmr_row[:, 0:tc], ALU.subtract), "vs", 1)
        blk.vector(ve3)

        def sc2(sc):
            s.w(sc, "vs")
            s.inc(sc.activation(rstd_row[:, 0:tc], rstd_row[:, 0:tc], AF.Ln, bias=eps_t[0:1, 0:1]), "ss", 1)
            s.inc(sc.activation(rstd_row[:, 0:tc], rstd_row[:, 0:tc], AF.Exp, scale=-0.5), "ss", 1)
        blk.scalar(sc2)

        def ve4(v):
            s.w(v, "ss")
            s.inc(v.tensor_tensor(nmr_row[:, 0:tc], srow[:], rstd_row[:, 0:tc], ALU.mult), "vs", 1)
            s.inc(v.tensor_scalar(nmr_row[:, 0:tc], nmr_row[:, 0:tc], -1.0, None, ALU.mult), "vs", 1)
        blk.vector(ve4)

    s.push()
    dt2 = s.sb("dt2", [H, TO], F32)
    cs = s.sb("cs", [H, TO], F32)
    B_fm = s.sb("B_fm", [NA, TO], BF16)
    C_fm = s.sb("C_fm", [NA, TO], BF16)
    s.push()
    xc = [s.sb(f"xc{eg}", [128, TO], BF16) for eg in range(EG)]
    s.push()
    ln1 = [s.sb(f"ln1_{i}", [128, WARM + TO], BF16) for i in range(DG)]
    s.push()
    xw = [s.sb(f"xw{i}", [128, WARM], BF16) for i in range(DG)]

    def ldxw(g):
        for i in range(DG):
            s.inc(g.dma_start(out=xw[i][:], in_=xwT[128 * i:128 * (i + 1), :]), "gd", 16)
    blk.gpsimd(ldxw)
    r1 = s.sb("r1", [1, TO], F32)
    n1 = s.sb("n1", [1, TO], F32)
    r1w = s.sb("r1w", [1, WARM], F32)
    n1w = s.sb("n1w", [1, WARM], F32)

    def wtg(sc):
        s.w(sc, "gd")
    blk.scalar(wtg)
    ln_stats("l1o", resid, TO, r1, n1)
    ln_stats("l1w", xw, WARM, r1w, n1w, bf=True)
    tmpa = s.sb("tmpa", [128, 512], F32)
    psEr = s.ps("psEr", [128, 512], F32)
    psEn = s.ps("psEn", [128, 512], F32)
    for st in range(2):
        c0 = st * 512

        def tex(t, c0=c0, st=st):
            s.w(t, "vs")
            s.w(t, "ss")
            if st > 0:
                s.w(t, "vs")
            s.inc(t.matmul(psEr[:], ones1f[:], r1[0:1, c0:c0 + 512], start=True, stop=True), "ts", 1)
            s.inc(t.matmul(psEn[:], ones1f[:], n1[0:1, c0:c0 + 512], start=True, stop=True), "ts", 1)
        blk.tensor(tex)
        for pg in range(DG):
            def ve(v, pg=pg, c0=c0):
                s.w(v, "ts")
                s.inc(v.tensor_tensor(tmpa[:], resid[pg][:, c0:c0 + 512], psEr[:], ALU.mult), "vs", 1)
                s.inc(v.tensor_tensor(ln1[pg][:, WARM + c0:WARM + c0 + 512], tmpa[:], psEn[:], ALU.add), "vs", 1)
            blk.vector(ve)

    def tex2(t):
        s.w(t, "vs")
        s.inc(t.matmul(psEr[:, 0:WARM], ones1f[:], r1w[:], start=True, stop=True), "ts", 1)
        s.inc(t.matmul(psEn[:, 0:WARM], ones1f[:], n1w[:], start=True, stop=True), "ts", 1)
    blk.tensor(tex2)
    for pg in range(DG):
        def ve(v, pg=pg):
            s.w(v, "ts")
            s.inc(v.tensor_tensor(tmpa[:, 0:WARM], xw[pg][:], psEr[:, 0:WARM], ALU.mult), "vs", 1)
            s.inc(v.tensor_tensor(ln1[pg][:, 0:WARM], tmpa[:, 0:WARM], psEn[:, 0:WARM], ALU.add), "vs", 1)
        blk.vector(ve)
    s.pop()
    dbg("ln1", ln1[0][:, :], [128, WARM + TO], BF16)

    s.push()
    wbufB = [s.sb(f"wbufB{i}", [128, 16 * 128], BF16) for i in range(2)]
    ztile = [s.sb(f"ztile{i}", [128, TO], BF16) for i in range(2)]
    psB = [s.ps(f"psB{i}", [128, 512], F32) for i in range(4)]
    bankB = [0] * 4
    wfree = [0, 0]
    zfree = [0, 0]
    for mz in range(32):
        mo = 32 + mz
        wb = wbufB[mz % 2]
        zt = ztile[mz % 2]

        def gp(g, mo=mo, wb=wb, mz=mz):
            s.w(g, "ts", wfree[mz % 2])
            s.inc(g.dma_start(out=wb[:].rearrange("p (k m) -> p k m", k=16),
                              in_=W_inprojT[:, 128 * mo:128 * (mo + 1)]
                              .rearrange("(k p) m -> p k m", p=128)), "gd", 16)
        blk.gpsimd(gp)
        bids = [(2 * mz) % 4, (2 * mz + 1) % 4]

        def te(t, mz=mz, wb=wb, bids=bids):
            s.w(t, "gd")
            if mz == 0:
                s.w(t, "vs")
            for bi in bids:
                s.w(t, "ss", bankB[bi])
            for k in range(DG):
                for ti in range(2):
                    s.inc(t.matmul(psB[bids[ti]][:], wb[:, 128 * k:128 * (k + 1)],
                                   ln1[k][:, WARM + 512 * ti:WARM + 512 * (ti + 1)],
                                   start=(k == 0), stop=(k == DG - 1)), "ts", 1)
            wfree[mz % 2] = s.c["ts"]
        blk.tensor(te)

        def sc(scx, mz=mz, bids=bids, zt=zt):
            s.w(scx, "ts")
            s.w(scx, "od", zfree[mz % 2])
            for ti in range(2):
                s.inc(scx.activation(zt[:, 512 * ti:512 * (ti + 1)], psB[bids[ti]][:], AF.Silu), "ss", 1)
            for bi in bids:
                bankB[bi] = s.c["ss"]
        blk.scalar(sc)

        def sy(sx, mz=mz, zt=zt):
            s.w(sx, "ss")
            s.inc(sx.dma_start(out=z_spill[128 * mz:128 * (mz + 1), :], in_=zt[:]), "od", 16)
            zfree[mz % 2] = s.c["od"]
        blk.sync(sy)
    s.pop()

    s.push()
    wbufX = [s.sb(f"wbufX{i}", [128, 16 * 128], BF16) for i in range(2)]
    xbt = [s.sb(f"xbt{i}", [128, WARM + TO], BF16) for i in range(2)]
    cacc = s.sb("cacc", [128, TO], F32)
    psXx = [s.ps(f"psXx{i}", [128, 512], F32) for i in range(6)]
    bankX = [0] * 6
    wfreeX = [0, 0]
    xbfree = [0, 0]
    tilesB2 = [(0, WARM), (WARM, 512), (WARM + 512, 512)]
    for eg in range(EG):
        wb = wbufX[eg % 2]
        xbe = xbt[eg % 2]

        def gp(g, eg=eg, wb=wb):
            s.w(g, "ts", wfreeX[eg % 2])
            s.inc(g.dma_start(out=wb[:].rearrange("p (k m) -> p k m", k=16),
                              in_=W_inprojT[:, 128 * eg:128 * (eg + 1)]
                              .rearrange("(k p) m -> p k m", p=128)), "gd", 16)
        blk.gpsimd(gp)
        bids = [(3 * eg) % 6, (3 * eg + 1) % 6, (3 * eg + 2) % 6]

        def te(t, eg=eg, wb=wb, bids=bids):
            s.w(t, "gd")
            for bi in bids:
                s.w(t, "ss", bankX[bi])
            for k in range(DG):
                for ti, (c0, wd) in enumerate(tilesB2):
                    s.inc(t.matmul(psXx[bids[ti]][:, 0:wd], wb[:, 128 * k:128 * (k + 1)],
                                   ln1[k][:, c0:c0 + wd],
                                   start=(k == 0), stop=(k == DG - 1)), "ts", 1)
            wfreeX[eg % 2] = s.c["ts"]
        blk.tensor(te)

        def sc(scx, eg=eg, bids=bids, xbe=xbe):
            s.w(scx, "ts")
            s.w(scx, "vs", xbfree[eg % 2])
            for ti, (c0, wd) in enumerate(tilesB2):
                s.inc(scx.copy(xbe[:, c0:c0 + wd], psXx[bids[ti]][:, 0:wd]), "ss", 1)
            for bi in bids:
                bankX[bi] = s.c["ss"]
        blk.scalar(sc)

        def ve(v, eg=eg, xbe=xbe):
            s.w(v, "ss")
            s.inc(v.tensor_scalar(cacc[:], xbe[:, WARM - 3:WARM - 3 + TO],
                                  caw[:, 4 * eg:4 * eg + 1], None, ALU.mult), "vs", 1)
            for k in range(1, 4):
                s.inc(v.scalar_tensor_tensor(cacc[:], xbe[:, WARM - 3 + k:WARM - 3 + k + TO],
                                             caw[:, 4 * eg + k:4 * eg + k + 1], cacc[:],
                                             ALU.mult, ALU.add), "vs", 1)
            xbfree[eg % 2] = s.c["vs"]
        blk.vector(ve)

        def sc2(scx, eg=eg):
            s.w(scx, "vs")
            s.inc(scx.activation(xc[eg][:], cacc[:], AF.Silu, bias=cab[:, eg:eg + 1]), "ss", 1)
        blk.scalar(sc2)
    s.pop()
    s.pop()
    dbg("xc0", xc[0][:, :], [128, TO], BF16)
    if upto == "B":
        return fin()

    s.push()
    wbufD = [s.sb(f"wbufD{i}", [128, 32 * 128], BF16) for i in range(2)]
    xtile = [s.sb(f"xtile{i}", [128, TO], BF16) for i in range(2)]
    s.push()
    psD = [s.ps(f"psD{i}", [128, 512], F32) for i in range(4)]
    bankD = [0] * 4
    wfreeD = [0, 0]
    xtfree = [0, 0]
    for mo in range(32):
        wb = wbufD[mo % 2]
        xte = xtile[mo % 2]

        def gp(g, mo=mo, wb=wb):
            s.w(g, "ts", wfreeD[mo % 2])
            s.inc(g.dma_start(out=wb[:].rearrange("p (k m) -> p k m", k=32),
                              in_=W_xprojT[:, 128 * mo:128 * (mo + 1)]
                              .rearrange("(k p) m -> p k m", p=128)), "gd", 16)
        blk.gpsimd(gp)
        bids = [(2 * mo) % 4, (2 * mo + 1) % 4]

        def te(t, mo=mo, wb=wb, bids=bids):
            s.w(t, "gd")
            if mo == 0:
                s.w(t, "ss")
            for bi in bids:
                s.w(t, "ss", bankD[bi])
            for k in range(EG):
                for ti in range(2):
                    s.inc(t.matmul(psD[bids[ti]][:], wb[:, 128 * k:128 * (k + 1)],
                                   xc[k][:, 512 * ti:512 * (ti + 1)],
                                   start=(k == 0), stop=(k == EG - 1)), "ts", 1)
            wfreeD[mo % 2] = s.c["ts"]
        blk.tensor(te)

        def sc(scx, mo=mo, bids=bids, xte=xte):
            s.w(scx, "ts")
            s.w(scx, "od", xtfree[mo % 2])
            for ti in range(2):
                s.inc(scx.copy(xte[:, 512 * ti:512 * (ti + 1)], psD[bids[ti]][:]), "ss", 1)
            for bi in bids:
                bankD[bi] = s.c["ss"]
        blk.scalar(sc)

        def sy(sx, mo=mo, xte=xte):
            s.w(sx, "ss")
            s.inc(sx.dma_start(out=X_spill[128 * mo:128 * (mo + 1), :], in_=xte[:]), "od", 16)
            xtfree[mo % 2] = s.c["od"]
        blk.sync(sy)

    s.pop()
    wdt = s.sb("wdt", [128, 32 * 64], BF16)
    wBx = s.sb("wBx", [128, 32 * 128], BF16)
    wCx = s.sb("wCx", [128, 32 * 128], BF16)
    dtp = s.sb("dtp", [H, TO], BF16)

    def gp2(g):
        s.w(g, "ts", wfreeD[0])
        s.w(g, "ts", wfreeD[1])
        s.inc(g.dma_start(out=wdt[:].rearrange("p (k m) -> p k m", k=32),
                          in_=W_xprojT[:, 4096:4160].rearrange("(k p) m -> p k m", p=128)), "gd", 16)
        s.inc(g.dma_start(out=wBx[:].rearrange("p (k m) -> p k m", k=32),
                          in_=W_xprojT[:, 4160:4288].rearrange("(k p) m -> p k m", p=128)), "gd", 16)
        s.inc(g.dma_start(out=wCx[:].rearrange("p (k m) -> p k m", k=32),
                          in_=W_xprojT[:, 4288:4416].rearrange("(k p) m -> p k m", p=128)), "gd", 16)
    blk.gpsimd(gp2)
    s.push()
    psdt = s.ps("psdt", [64, 512], F32)
    psBb = s.ps("psBb", [128, 512], F32)
    psCc = s.ps("psCc", [128, 512], F32)
    for ti in range(2):
        def te(t, ti=ti):
            s.w(t, "gd")
            s.w(t, "ss")
            for k in range(EG):
                s.inc(t.matmul(psdt[:], wdt[:, 64 * k:64 * (k + 1)], xc[k][:, 512 * ti:512 * (ti + 1)],
                               start=(k == 0), stop=(k == EG - 1)), "ts", 1)
            for k in range(EG):
                s.inc(t.matmul(psBb[:], wBx[:, 128 * k:128 * (k + 1)], xc[k][:, 512 * ti:512 * (ti + 1)],
                               start=(k == 0), stop=(k == EG - 1)), "ts", 1)
            for k in range(EG):
                s.inc(t.matmul(psCc[:], wCx[:, 128 * k:128 * (k + 1)], xc[k][:, 512 * ti:512 * (ti + 1)],
                               start=(k == 0), stop=(k == EG - 1)), "ts", 1)
        blk.tensor(te)

        def sc(scx, ti=ti):
            s.w(scx, "ts")
            s.inc(scx.copy(dtp[:, 512 * ti:512 * (ti + 1)], psdt[:]), "ss", 1)
            s.inc(scx.copy(B_fm[:, 512 * ti:512 * (ti + 1)], psBb[:]), "ss", 1)
            s.inc(scx.copy(C_fm[:, 512 * ti:512 * (ti + 1)], psCc[:]), "ss", 1)
        blk.scalar(sc)

    s.pop()
    wdtp = s.sb("wdtp", [H, H], BF16)
    dtbcol = s.sb("dtbcol", [H, 1], F32)

    def gp3(g):
        s.inc(g.dma_start(out=wdtp[:], in_=W_dtT[:]), "gd", 16)
    blk.gpsimd(gp3)

    def sy2(sx):
        s.inc(sx.dma_start(out=dtbcol[:], in_=dt_b_col[:]), "ld", 16)
    blk.sync(sy2)
    psd2 = s.ps("psd2", [64, 512], F32)
    for ti in range(2):
        def te(t, ti=ti):
            s.w(t, "gd")
            s.w(t, "ss")
            if ti > 0:
                s.w(t, "vs")
            s.inc(t.matmul(psd2[:], wdtp[:], dtp[:, 512 * ti:512 * (ti + 1)], start=True, stop=True), "ts", 1)
        blk.tensor(te)

        def ve(v, ti=ti):
            s.w(v, "ts")
            s.w(v, "ld")
            s.inc(v.tensor_scalar(dt2[:, 512 * ti:512 * (ti + 1)], psd2[:], dtbcol[:, 0:1],
                                  None, ALU.add), "vs", 1)
        blk.vector(ve)

        def sc(scx, ti=ti):
            s.w(scx, "vs")
            s.inc(scx.activation(dt2[:, 512 * ti:512 * (ti + 1)], dt2[:, 512 * ti:512 * (ti + 1)], AF.Exp), "ss", 1)
            s.inc(scx.activation(dt2[:, 512 * ti:512 * (ti + 1)], dt2[:, 512 * ti:512 * (ti + 1)],
                                 AF.Ln, bias=one_t[0:64, 0:1]), "ss", 1)
        blk.scalar(sc)

    A_sc = s.sb("A_sc", [H, TO], F32)
    rstt = s.sb("rstt", [H, TO], F32)

    def sy3(sx):
        s.inc(sx.dma_start(out=rstt[:], in_=rst_d[:]), "ld", 16)
    blk.sync(sy3)

    def ve5(v):
        s.w(v, "ss")
        s.w(v, "ld")
        s.inc(v.tensor_scalar(A_sc[:], dt2[:], Aa_t[:, 0:1], None, ALU.mult), "vs", 1)
        s.inc(v.tensor_tensor_scan(cs[:], rstt[:], A_sc[:], 0.0, ALU.mult, ALU.add), "vs", 1)
    blk.vector(ve5)

    def sycs(sx):
        s.w(sx, "vs")
        s.inc(sx.dma_start(out=cs_dram[:], in_=cs[:]), "od", 16)
    blk.sync(sycs)
    s.pop()
    s.pop()
    dbg("dt2", dt2[:, :], [H, TO], F32)
    dbg("cs", cs[:, :], [H, TO], F32)
    dbg("B_fm", B_fm[:, :], [NA, TO], BF16)
    if upto == "D":
        return fin()

    s.push()
    csT = s.sb("csT", [128, H * NCH], F32)
    dtT = s.sb("dtT", [128, H * NCH], F32)
    ecT = s.sb("ecT", [128, H * NCH], F32)
    S_in = s.sb("S_in", [NA, E], BF16)
    s.push()
    Scum = s.sb("Scum", [NA, E], F32)
    Scum_bf = s.sb("Scum_bf", [NA, E], BF16)
    exptotT = s.sb("exptotT", [128, H * NCH], F32)
    totT = s.sb("totT", [128, H], F32)
    wcol_all = s.sb("wcol_all", [128, H], F32)
    ecrow = s.sb("ecrow", [1, H], F32)
    etrow = s.sb("etrow", [1, H], F32)
    totrow = s.sb("totrow", [1, H], F32)
    ecrow_exp = s.sb("ecrow_exp", [1, H], F32)
    Xld = s.sb("Xld", [128, 32 * 128], BF16)
    Xtm = s.sb("Xtm", [128, E], BF16)
    Btm = s.sb("Btm", [128, NA], BF16)
    Bd_w = s.sb("Bd_w", [128, 16 * 128], BF16)
    pstf = s.ps("pstf", [128, 512], F32)
    pstb = s.ps("pstb", [128, 1024], BF16)
    psS = s.ps("psS", [128, 512], F32)
    psS2 = s.ps("psS2", [128, 512], F32)
    pse2 = s.ps("pse2", [128, 192], F32)

    def ve6(v):
        s.inc(v.memset(Scum[:], 0.0), "vs", 1)
        s.inc(v.memset(Scum_bf[:], 0.0), "vs", 1)
        s.inc(v.memset(ecrow[:], 0.0), "vs", 1)
    blk.vector(ve6)

    bd_free = [0]
    psS_free = [0]
    for c in range(NCH):
        c0 = c * CH

        def te(t, c=c, c0=c0):
            s.w(t, "vs")
            s.w(t, "ss")
            s.inc(t.transpose(pstf[:, 0:64], cs[0:H, c0:c0 + CH], identf[0:64, 0:64]), "ts", 1)
            s.inc(t.transpose(pstf[:, 64:128], dt2[0:H, c0:c0 + CH], identf[0:64, 0:64]), "ts", 1)
        blk.tensor(te)

        def syt(sx, c=c, c0=c0):
            s.w(sx, "vs")
            if c == 0:
                s.w(sx, "od")
            s.inc(sx.dma_start(out=totrow[:],
                               in_=cs_dram.ap().rearrange("h t -> t h")[c0 + CH - 1:c0 + CH, 0:H],
                               allow_slow_non_contiguous=True), "ld", 16)
        blk.sync(syt)

        def sc(scx, c=c):
            s.w(scx, "ts")
            s.w(scx, "vs")
            s.w(scx, "ld")
            s.inc(scx.copy(csT[:, H * c:H * (c + 1)], pstf[:, 0:64]), "ss", 1)
            s.inc(scx.copy(dtT[:, H * c:H * (c + 1)], pstf[:, 64:128]), "ss", 1)
            s.inc(scx.activation(etrow[:], totrow[:], AF.Exp), "ss", 1)
            s.inc(scx.activation(ecrow_exp[:], ecrow[:], AF.Exp), "ss", 1)
        blk.scalar(sc)

        def te2(t, c=c):
            s.w(t, "ss")
            s.inc(t.matmul(pse2[:, 0:64], ones1f[:], etrow[:], start=True, stop=True), "ts", 1)
            s.inc(t.matmul(pse2[:, 64:128], ones1f[:], ecrow_exp[:], start=True, stop=True), "ts", 1)
            s.inc(t.matmul(pse2[:, 128:192], ones1f[:], totrow[:], start=True, stop=True), "ts", 1)
        blk.tensor(te2)

        def sc3(scx, c=c):
            s.w(scx, "ts")
            s.inc(scx.copy(exptotT[:, H * c:H * (c + 1)], pse2[:, 0:64]), "ss", 1)
            s.inc(scx.copy(ecT[:, H * c:H * (c + 1)], pse2[:, 64:128]), "ss", 1)
            s.inc(scx.copy(totT[:], pse2[:, 128:192]), "ss", 1)
        blk.scalar(sc3)

        def ve7(v, c=c):
            s.w(v, "ss")
            s.inc(v.tensor_tensor(ecrow[:], ecrow[:], totrow[:], ALU.add), "vs", 1)
            s.inc(v.tensor_tensor(wcol_all[:], totT[:], csT[:, H * c:H * (c + 1)], ALU.subtract), "vs", 1)
        blk.vector(ve7)

        def sc3b(scx, c=c):
            s.w(scx, "vs")
            s.inc(scx.activation(wcol_all[:], wcol_all[:], AF.Exp), "ss", 1)
        blk.scalar(sc3b)

        def ve7b(v, c=c):
            s.w(v, "ss")
            s.inc(v.tensor_tensor(wcol_all[:], wcol_all[:], dtT[:, H * c:H * (c + 1)], ALU.mult), "vs", 1)
        blk.vector(ve7b)

        def gp(g, c=c, c0=c0):
            s.w(g, "vs")
            s.inc(g.dma_start(out=Xld[:].rearrange("p (k m) -> p k m", k=32),
                              in_=X_spill[:, c0:c0 + CH].rearrange("(k p) m -> p k m", p=128)), "gd", 16)
        blk.gpsimd(gp)

        for gb in range(4):
            def te3(t, c=c, gb=gb):
                if gb == 0:
                    s.w(t, "gd")
                s.w(t, "vs")
                for j in range(8):
                    s.inc(t.transpose(pstb[:, 128 * j:128 * (j + 1)],
                                      Xld[:, 128 * (8 * gb + j):128 * (8 * gb + j + 1)], identt[:]), "ts", 1)
            blk.tensor(te3)

            def ve8(v, gb=gb):
                s.w(v, "ts")
                s.inc(v.tensor_copy(Xtm[:, 1024 * gb:1024 * (gb + 1)], pstb[:]), "vs", 1)
            blk.vector(ve8)

        def te4(t, c=c, c0=c0):
            s.w(t, "vs")
            s.inc(t.transpose(pstb[:, 0:128], B_fm[:, c0:c0 + CH], identt[:]), "ts", 1)
        blk.tensor(te4)

        def ve9(v, c=c):
            s.w(v, "ts")
            s.inc(v.tensor_copy(Btm[:], pstb[:, 0:128]), "vs", 1)
        blk.vector(ve9)

        def sy4(sx, c=c):
            s.w(sx, "vs")
            s.w(sx, "ss")
            s.inc(sx.dma_start(out=snap_spill[NA * c:NA * (c + 1), :], in_=Scum_bf[:]), "od", 16)
        blk.sync(sy4)
        snap_od = s.c["od"]

        for hb in range(4):
            def ve10(v, c=c, hb=hb):
                s.w(v, "ts", bd_free[0])
                for j in range(16):
                    h = 16 * hb + j
                    s.inc(v.tensor_scalar(Bd_w[:, 128 * j:128 * (j + 1)], Btm[:],
                                          wcol_all[:, h:h + 1], None, ALU.mult), "vs", 1)
            blk.vector(ve10)

            def te5(t, c=c, hb=hb):
                s.w(t, "vs")
                s.w(t, "vs", psS_free[0])
                for j in range(16):
                    h = 16 * hb + j
                    s.inc(t.matmul(psS[:, 64 * j:64 * (j + 1)] if j < 8 else psS2[:, 64 * (j - 8):64 * (j - 7)],
                                   Bd_w[:, 128 * j:128 * (j + 1)], Xtm[:, 64 * h:64 * (h + 1)],
                                   start=True, stop=True), "ts", 1)
                bd_free[0] = s.c["ts"]
            blk.tensor(te5)

            def ve11(v, c=c, hb=hb):
                s.w(v, "ts")
                for j in range(16):
                    h = 16 * hb + j
                    src = psS[:, 64 * j:64 * (j + 1)] if j < 8 else psS2[:, 64 * (j - 8):64 * (j - 7)]
                    s.inc(v.scalar_tensor_tensor(Scum[:, 64 * h:64 * (h + 1)], Scum[:, 64 * h:64 * (h + 1)],
                                                 exptotT[:, H * c + h:H * c + h + 1],
                                                 src, ALU.mult, ALU.add), "vs", 1)
                psS_free[0] = s.c["vs"]
            blk.vector(ve11)

        def sc4(scx, c=c, snap_od=snap_od):
            s.w(scx, "vs")
            s.w(scx, "od", snap_od)
            s.inc(scx.copy(Scum_bf[:], Scum[:]), "ss", 1)
        blk.scalar(sc4)

    c2s = s.sb("c2s", [NA, E], BF16)

    def ve12(v):
        s.w(v, "ss")
        s.inc(v.tensor_scalar(c2s[:], Scum_bf[:], msend_t[:, 0:1], None, ALU.mult), "vs", 1)
    blk.vector(ve12)

    def gp4(g):
        s.w(g, "vs")
        s.inc(g.dma_start(out=c2_in[:], in_=c2s[:]), "gd", 16)
        s.w(g, "gd")
        s.inc(g.collective_compute("AllGather", ALU.bypass, replica_groups=PAIRS,
                                   ins=[c2_in.ap().opt()], outs=[c2_out.ap().opt()]), "cc", 1)
        s.w(g, "cc")
        s.inc(g.dma_start(out=S_in[:], in_=c2_out[0:NA, :]), "gd", 16)
    blk.gpsimd(gp4)

    def ve13(v):
        s.w(v, "gd")
        s.inc(v.tensor_scalar(S_in[:], S_in[:], rrecv_t[:, 0:1], None, ALU.mult), "vs", 1)
    blk.vector(ve13)
    dbg("Scum", Scum[:, :], [NA, E], F32)
    dbg("S_in", S_in[:, :], [NA, E], BF16)
    s.pop()
    if upto == "E1":
        return fin()

    s.push()
    tril_t = s.sb("tril_t", [CH, CH], F32)
    Dbt = s.sb("Dbt", [128, E], BF16)
    Xld2 = s.sb("Xld2", [128, 32 * 128], BF16)
    zld2 = s.sb("zld2", [128, 32 * 128], BF16)
    Xtm2 = s.sb("Xtm2", [128, E], BF16)
    ztm2 = s.sb("ztm2", [128, E], BF16)
    snapc = s.sb("snapc", [NA, E], BF16)
    Ytm = s.sb("Ytm", [128, E], F32)
    ytmbf = s.sb("ytmbf", [128, E], BF16)
    ygc = s.sb("ygc", [128, 32 * 128], BF16)
    CBt = s.sb("CBt", [128, CH], F32)
    expcsT2 = s.sb("expcsT2", [128, H], F32)
    Lh_w = s.sb("Lh_w", [128, 16 * CH], F32)
    csrow_q = [s.sb(f"csrq{i}", [1, 16 * CH], F32) for i in range(2)]
    Mch_w = s.sb("Mch_w", [128, 16 * CH], BF16)
    seff_w = s.sb("seff_w", [NA, 16 * PDIM], BF16)
    psY = s.ps("psY", [128, 512], F32)
    psOf = s.ps("psOf", [128, 512], F32)
    psCB = s.ps("psCB", [128, 128], F32)
    psXPw = s.ps("psXPw", [128, 16 * CH], F32)
    pstb2 = s.ps("pstb2", [128, 1024], BF16)

    def sy5(sx):
        s.inc(sx.dma_start(out=tril_t[:], in_=tril_d[:]), "ld", 16)
    blk.sync(sy5)
    s.push()
    darow = s.sb("darow", [1, 512], BF16)
    for st in range(8):
        def gpd(g, st=st):
            if st >= 1:
                s.w(g, "ts")
            s.inc(g.dma_start(out=darow[:], in_=D_a_row[0:1, 512 * st:512 * (st + 1)]), "gd", 16)
        blk.gpsimd(gpd)

        def te6(t, st=st):
            s.w(t, "ld")
            s.w(t, "gd")
            if st >= 1:
                s.w(t, "ss")
            s.inc(t.matmul(psXPw[:, 0:512], ones1[:], darow[:], start=True, stop=True), "ts", 1)
        blk.tensor(te6)

        def sc5(scx, st=st):
            s.w(scx, "ts")
            s.inc(scx.copy(Dbt[:, 512 * st:512 * (st + 1)], psXPw[:, 0:512]), "ss", 1)
        blk.scalar(sc5)
    s.pop()

    psXP_free = [0]
    LM_free = [0]
    se_free = [0]
    psY_free = [0, 0]
    csq_free = [0, 0]
    for c in range(NCH):
        c0 = c * CH

        def gp5(g, c=c, c0=c0):
            s.w(g, "vs")
            s.w(g, "ts")
            s.inc(g.dma_start(out=Xld2[:].rearrange("p (k m) -> p k m", k=32),
                              in_=X_spill[:, c0:c0 + CH].rearrange("(k p) m -> p k m", p=128)), "gd", 16)
            s.inc(g.dma_start(out=zld2[:].rearrange("p (k m) -> p k m", k=32),
                              in_=z_spill[:, c0:c0 + CH].rearrange("(k p) m -> p k m", p=128)), "gd", 16)
            s.inc(g.dma_start(out=snapc[:], in_=snap_spill[NA * c:NA * (c + 1), :]), "gd", 16)
        blk.gpsimd(gp5)

        for gb in range(4):
            def te7(t, c=c, gb=gb):
                if gb == 0:
                    s.w(t, "gd")
                    s.w(t, "ss")
                s.w(t, "vs")
                for j in range(8):
                    s.inc(t.transpose(pstb2[:, 128 * j:128 * (j + 1)],
                                      Xld2[:, 128 * (8 * gb + j):128 * (8 * gb + j + 1)], identt[:]), "ts", 1)
            blk.tensor(te7)

            def ve15(v, gb=gb):
                s.w(v, "ts")
                s.inc(v.tensor_copy(Xtm2[:, 1024 * gb:1024 * (gb + 1)], pstb2[:]), "vs", 1)
            blk.vector(ve15)
        for gb in range(4):
            def te8(t, c=c, gb=gb):
                s.w(t, "vs")
                for j in range(8):
                    s.inc(t.transpose(pstb2[:, 128 * j:128 * (j + 1)],
                                      zld2[:, 128 * (8 * gb + j):128 * (8 * gb + j + 1)], identt[:]), "ts", 1)
            blk.tensor(te8)

            def ve16(v, gb=gb):
                s.w(v, "ts")
                s.inc(v.tensor_copy(ztm2[:, 1024 * gb:1024 * (gb + 1)], pstb2[:]), "vs", 1)
            blk.vector(ve16)

        def te9(t, c=c, c0=c0):
            s.w(t, "vs")
            s.inc(t.matmul(psCB[:], B_fm[:, c0:c0 + CH], C_fm[:, c0:c0 + CH], start=True, stop=True), "ts", 1)
        blk.tensor(te9)

        def ve17(v, c=c):
            s.w(v, "ts")
            s.w(v, "ld")
            s.inc(v.tensor_tensor(CBt[:], psCB[:], tril_t[:], ALU.mult), "vs", 1)
        blk.vector(ve17)

        def sc6(scx, c=c):
            s.w(scx, "vs")
            s.inc(scx.activation(expcsT2[:], csT[:, H * c:H * (c + 1)], AF.Exp), "ss", 1)
        blk.scalar(sc6)

        for hb in range(4):
            qi = hb % 2

            def syq(sx, c=c, c0=c0, hb=hb, qi=qi):
                s.w(sx, "ts", csq_free[qi])
                s.inc(sx.dma_start(out=csrow_q[qi].ap().rearrange("a (h l) -> a h l", h=16),
                                   in_=cs_dram[16 * hb:16 * (hb + 1), c0:c0 + CH]
                                   .rearrange("(a h) l -> a h l", a=1)), "ld", 16)
            blk.sync(syq)

            def te10(t, c=c, hb=hb, qi=qi):
                s.w(t, "vs", psXP_free[0])
                s.w(t, "ss")
                s.w(t, "ld")
                for j in range(16):
                    s.inc(t.matmul(psXPw[:, 128 * j:128 * (j + 1)], ones1f[:],
                                   csrow_q[qi][0:1, CH * j:CH * (j + 1)],
                                   start=True, stop=True), "ts", 1)
                csq_free[qi] = s.c["ts"]
            blk.tensor(te10)

            def ve19(v, c=c, hb=hb):
                s.w(v, "ts")
                s.w(v, "ss", LM_free[0])
                for j in range(16):
                    h = 16 * hb + j
                    s.inc(v.tensor_scalar(Lh_w[:, 128 * j:128 * (j + 1)], psXPw[:, 128 * j:128 * (j + 1)],
                                          csT[:, H * c + h:H * c + h + 1], 0.0,
                                          ALU.subtract, ALU.min), "vs", 1)
                psXP_free[0] = s.c["vs"]
            blk.vector(ve19)

            def sc7(scx, c=c, hb=hb):
                s.w(scx, "vs")
                s.inc(scx.activation(Lh_w[:], Lh_w[:], AF.Exp), "ss", 1)
                LM_free[0] = s.c["ss"]
            blk.scalar(sc7)

            def ve20(v, c=c, hb=hb):
                s.w(v, "ss")
                s.w(v, "ts", se_free[0])
                for j in range(16):
                    h = 16 * hb + j
                    s.inc(v.scalar_tensor_tensor(Mch_w[:, 128 * j:128 * (j + 1)], Lh_w[:, 128 * j:128 * (j + 1)],
                                                 dtT[:, H * c + h:H * c + h + 1],
                                                 CBt[:], ALU.mult, ALU.mult), "vs", 1)
                    s.inc(v.scalar_tensor_tensor(seff_w[:, 64 * j:64 * (j + 1)], S_in[:, 64 * h:64 * (h + 1)],
                                                 ecT[:, H * c + h:H * c + h + 1],
                                                 snapc[:, 64 * h:64 * (h + 1)], ALU.mult, ALU.add), "vs", 1)
            blk.vector(ve20)

            for wv in range(2):
                def te11(t, c=c, hb=hb, wv=wv, c0=c0):
                    s.w(t, "vs")
                    s.w(t, "vs", psY_free[wv])
                    for sl in range(8):
                        j = 8 * wv + sl
                        h = 16 * hb + j
                        s.inc(t.matmul(psY[:, 64 * sl:64 * (sl + 1)], Mch_w[:, 128 * j:128 * (j + 1)],
                                       Xtm2[:, 64 * h:64 * (h + 1)], start=True, stop=True), "ts", 1)
                        s.inc(t.matmul(psOf[:, 64 * sl:64 * (sl + 1)], C_fm[:, c0:c0 + CH],
                                       seff_w[:, 64 * j:64 * (j + 1)], start=True, stop=True), "ts", 1)
                    if wv == 1:
                        se_free[0] = s.c["ts"]
                blk.tensor(te11)

                def ve21(v, c=c, hb=hb, wv=wv):
                    s.w(v, "ts")
                    for sl in range(8):
                        j = 8 * wv + sl
                        h = 16 * hb + j
                        s.inc(v.tensor_scalar(Ytm[:, 64 * h:64 * (h + 1)], psOf[:, 64 * sl:64 * (sl + 1)],
                                              expcsT2[:, h:h + 1], None, ALU.mult), "vs", 1)
                        s.inc(v.tensor_tensor(Ytm[:, 64 * h:64 * (h + 1)], Ytm[:, 64 * h:64 * (h + 1)],
                                              psY[:, 64 * sl:64 * (sl + 1)], ALU.add), "vs", 1)
                        s.inc(v.scalar_tensor_tensor(Ytm[:, 64 * h:64 * (h + 1)], Xtm2[:, 64 * h:64 * (h + 1)],
                                                     Dbt[:, 64 * h:64 * h + 1], Ytm[:, 64 * h:64 * (h + 1)],
                                                     ALU.mult, ALU.add), "vs", 1)
                    psY_free[wv] = s.c["vs"]
                blk.vector(ve21)

        def ve22(v, c=c):
            s.inc(v.tensor_tensor(ytmbf[:], Ytm[:], ztm2[:], ALU.mult), "vs", 1)
        blk.vector(ve22)

        for gb in range(4):
            def te12(t, c=c, gb=gb):
                s.w(t, "vs")
                for j in range(8):
                    s.inc(t.transpose(pstb2[:, 128 * j:128 * (j + 1)],
                                      ytmbf[:, 128 * (8 * gb + j):128 * (8 * gb + j + 1)], identt[:]), "ts", 1)
            blk.tensor(te12)

            def ve23(v, gb=gb):
                s.w(v, "ts")
                s.inc(v.tensor_copy(ygc[:, 1024 * gb:1024 * (gb + 1)], pstb2[:]), "vs", 1)
            blk.vector(ve23)

        def sy6(sx, c=c, c0=c0):
            s.w(sx, "vs")
            s.inc(sx.dma_start(out=yg_spill[:, c0:c0 + CH].rearrange("(k p) m -> p k m", p=128),
                               in_=ygc[:].rearrange("p (k m) -> p k m", k=32)), "od", 16)
        blk.sync(sy6)
    s.pop()
    s.pop()
    s.pop()
    dbg("yg0", yg_spill[0:128, :], [128, TO], BF16)
    if upto == "E2":
        return fin()

    s.push()
    wbufF = s.sb("wbufF", [128, 32 * 1024], BF16)
    ygk = [s.sb(f"ygk{i}", [128, 512], BF16) for i in range(32)]
    psFb = [s.ps(f"psFb{i}", [128, 512], F32) for i in range(8)]
    yg_free = [0]
    psF_free = [0] * 8
    for dgw in range(2):
        def gpw(g, dgw=dgw):
            s.w(g, "ts")
            s.inc(g.dma_start(out=wbufF[:].rearrange("p (k m) -> p k m", k=32),
                              in_=W_outprojT[:, 1024 * dgw:1024 * (dgw + 1)]
                              .rearrange("(k p) m -> p k m", p=128)), "gd", 16)
        blk.gpsimd(gpw)
        for tt in range(2):
            def gp6(g, tt=tt, dgw=dgw):
                s.w(g, "ts", yg_free[0])
                for k in range(EG):
                    s.inc(g.dma_start(out=ygk[k][:],
                                      in_=yg_spill[128 * k:128 * (k + 1), 512 * tt:512 * (tt + 1)]), "gd", 16)
            blk.gpsimd(gp6)
            for k in range(EG):
                def te13(t, tt=tt, k=k):
                    s.w(t, "gd")
                    if k == 0:
                        for bi in range(8):
                            s.w(t, "vs", psF_free[bi])
                    for dg8 in range(8):
                        s.inc(t.matmul(psFb[dg8][:], wbufF[:, 1024 * k + 128 * dg8:1024 * k + 128 * (dg8 + 1)],
                                       ygk[k][:], start=(k == 0), stop=(k == EG - 1)), "ts", 1)
                    yg_free[0] = s.c["ts"]
                blk.tensor(te13)

            for dg8 in range(8):
                def ve24(v, dgw=dgw, tt=tt, dg8=dg8):
                    s.w(v, "ts")
                    dg = dgw * 8 + dg8
                    s.inc(v.tensor_tensor(resid[dg][:, 512 * tt:512 * (tt + 1)],
                                          resid[dg][:, 512 * tt:512 * (tt + 1)],
                                          psFb[dg8][:], ALU.add), "vs", 1)
                    psF_free[dg8] = s.c["vs"]
                blk.vector(ve24)
    s.pop()
    dbg("resid1", resid[0][:, :], [128, TO], F32)
    if upto == "F":
        return fin()

    def ffn_phase(tag, first):
        s.push()
        ln2 = [s.sb(f"{tag}ln2_{i}", [128, TO], BF16) for i in range(DG)]
        r2 = s.sb(f"{tag}r2", [1, TO], F32)
        n2 = s.sb(f"{tag}n2", [1, TO], F32)
        r2b = s.sb(f"{tag}r2b", [128, TO], F32)
        n2b = s.sb(f"{tag}n2b", [128, TO], F32)
        tmpg = s.sb(f"{tag}tmpg", [128, TO], F32)
        sgw = s.sb(f"{tag}sgw", [128, 16 * 128], BF16)
        s.push()
        ln_stats(f"{tag}l2", resid, TO, r2, n2)
        expand_f32(f"{tag}r2x", r2, TO, lambda a, w: r2b[:, a:a + w])
        expand_f32(f"{tag}n2x", n2, TO, lambda a, w: n2b[:, a:a + w])
        for pg in range(DG):
            def ve(v, pg=pg):
                if pg == 0:
                    s.w(v, "ss")
                s.inc(v.tensor_tensor(tmpg[:], resid[pg][:], r2b[:], ALU.mult), "vs", 1)
                s.inc(v.tensor_tensor(tmpg[:], tmpg[:], n2b[:], ALU.add), "vs", 1)
                s.inc(v.scalar_tensor_tensor(ln2[pg][:], tmpg[:], flnw[:, pg:pg + 1],
                                             flnb[:, pg:pg + 1].broadcast_to([128, TO]),
                                             ALU.mult, ALU.add), "vs", 1)
            blk.vector(ve)

        if first:
            amax = s.sb("amax", [128, DG], F32)
            wraw = s.sb(f"{tag}wraw", [128, 16 * 128], BF16)
            sg_od = [0]
            for mo in range(DG):
                def gp(g, mo=mo):
                    s.w(g, "vs")
                    s.w(g, "ss")
                    s.inc(g.dma_start(out=wraw[:].rearrange("p (k m) -> p k m", k=16),
                                      in_=ffn_WT[:, 128 * mo:128 * (mo + 1)]
                                      .rearrange("(k p) m -> p k m", p=128)), "gd", 16)
                blk.gpsimd(gp)

                def ve25(v, mo=mo):
                    s.w(v, "gd")
                    s.inc(v.tensor_reduce(amax[:, mo:mo + 1], wraw[:], mybir.AxisListType.X,
                                          ALU.max, apply_absolute_value=True), "vs", 1)
                blk.vector(ve25)

                def sc8(scx, mo=mo):
                    s.w(scx, "gd")
                    s.w(scx, "od", sg_od[0])
                    s.inc(scx.activation(sgw[:], wraw[:], AF.Sign), "ss", 1)
                blk.scalar(sc8)

                def sy7(sx, mo=mo):
                    s.w(sx, "ss")
                    s.inc(sx.dma_start(out=signW_spill[:, 128 * mo:128 * (mo + 1)]
                                       .rearrange("(k p) m -> p k m", p=128),
                                       in_=sgw[:].rearrange("p (k m) -> p k m", k=16)), "od", 16)
                    sg_od[0] = s.c["od"]
                blk.sync(sy7)
            am1 = s.sb("am1", [128, 1], F32)
            amaxc = s.sb("amaxc", [128, DG], F32)
            amr = s.sb("amr", [1, 128], F32)
            amrc = s.sb("amrc", [1, 128], F32)
            ams = s.sb("ams", [1, 1], F32)
            psam = s.ps("psam", [128, 128], F32)

            def sc26(scx):
                s.w(scx, "vs")
                s.inc(scx.copy(amaxc[:], amax[:]), "ss", 1)
            blk.scalar(sc26)

            def ve26(v):
                s.w(v, "ss")
                s.inc(v.tensor_reduce(am1[:], amaxc[:], mybir.AxisListType.X, ALU.max), "vs", 1)
            blk.vector(ve26)

            def te14(t):
                s.w(t, "vs")
                s.inc(t.transpose(psam[0:1, 0:128], am1[:], identf[:]), "ts", 1)
            blk.tensor(te14)

            def ve27a(v):
                s.w(v, "ts")
                s.inc(v.tensor_copy(amr[:], psam[0:1, 0:128]), "vs", 1)
            blk.vector(ve27a)

            def sc27(scx):
                s.w(scx, "vs")
                s.inc(scx.copy(amrc[:], amr[:]), "ss", 1)
            blk.scalar(sc27)

            def ve27(v):
                s.w(v, "ss")
                s.inc(v.tensor_reduce(ams[:], amrc[:], mybir.AxisListType.X, ALU.max), "vs", 1)
            blk.vector(ve27)

            def te15(t):
                s.w(t, "vs")
                s.inc(t.matmul(psam[:, 0:1], ones1f[:], ams[:], start=True, stop=True), "ts", 1)
            blk.tensor(te15)

            def ve28(v):
                s.w(v, "ts")
                s.inc(v.tensor_scalar(gscale[:], gscale[:], psam[:, 0:1], None, ALU.mult), "vs", 1)
            blk.vector(ve28)

        s.pop()
        if first:
            dbg(tag + "amax", amax[:, :], [128, DG], F32)
            dbg(tag + "am1", am1[:, :], [128, 1], F32)
            dbg(tag + "ams", ams[:, :], [1, 1], F32)
        dbg(tag + "gsc", gscale[:, :], [128, DG], F32)
        dbg(tag + "rstd", r2[0:1, :], [1, TO], F32)
        dbg(tag + "nmr", n2[0:1, :], [1, TO], F32)
        dbg(tag + "ln2", ln2[0][:, :], [128, TO], BF16)
        psG = [s.ps(f"{tag}psG{i}", [128, 512], F32) for i in range(8)]
        bank_G = [0] * 8
        sg_free = [0]
        for dg in range(DG):
            def gp7(g, dg=dg):
                s.w(g, "ts", sg_free[0])
                if dg == 0:
                    s.w(g, "od")
                    s.w(g, "ss")
                s.inc(g.dma_start(out=sgw[:].rearrange("p (k m) -> p k m", k=16),
                                  in_=signW_spill[:, 128 * dg:128 * (dg + 1)]
                                  .rearrange("(k p) m -> p k m", p=128)), "gd", 16)
            blk.gpsimd(gp7)
            bids = [(2 * dg) % 8, (2 * dg + 1) % 8]

            def te16(t, dg=dg, bids=bids):
                s.w(t, "gd")
                if dg == 0:
                    s.w(t, "vs")
                for bi in bids:
                    s.w(t, "vs", bank_G[bi])
                for k in range(DG):
                    for ti in range(2):
                        s.inc(t.matmul(psG[bids[ti]][:], sgw[:, 128 * k:128 * (k + 1)],
                                       ln2[k][:, 512 * ti:512 * (ti + 1)],
                                       start=(k == 0), stop=(k == DG - 1)), "ts", 1)
                sg_free[0] = s.c["ts"]
            blk.tensor(te16)

            def ve29(v, dg=dg, bids=bids):
                s.w(v, "ts")
                for ti in range(2):
                    s.inc(v.scalar_tensor_tensor(tmpg[:, 0:512], psG[bids[ti]][:], gscale[:, dg:dg + 1],
                                                 fbeta[:, dg:dg + 1].broadcast_to([128, 512]),
                                                 ALU.mult, ALU.add), "vs", 1)
                    s.inc(v.tensor_tensor(resid[dg][:, 512 * ti:512 * (ti + 1)],
                                          resid[dg][:, 512 * ti:512 * (ti + 1)],
                                          tmpg[:, 0:512], ALU.add), "vs", 1)
                for bi in bids:
                    bank_G[bi] = s.c["vs"]
            blk.vector(ve29)
        s.pop()

    ffn_phase("g", True)
    dbg("resid2", resid[0][:, :], [128, TO], F32)
    if upto == "G":
        return fin()

    s.push()
    rstd3_bt = s.sb("rstd3_bt", [128, TO], F32)
    nmr3_bt = s.sb("nmr3_bt", [128, TO], F32)
    xcb = [s.sb(f"xcb{eg}", [128, TO], BF16) for eg in range(EGL)]

    s.push()
    rn = [s.sb(f"rn{i}", [128, TO], BF16) for i in range(DG)]
    rn_warm = [s.sb(f"rnw{i}", [128, WARM], BF16) for i in range(DG)]
    s.push()
    r3 = s.sb("r3", [1, TO], F32)
    n3 = s.sb("n3", [1, TO], F32)
    ln_stats("l3", resid, TO, r3, n3)
    expand_f32("r3x", r3, TO, lambda a, w: rstd3_bt[:, a:a + w])
    expand_f32("n3x", n3, TO, lambda a, w: nmr3_bt[:, a:a + w])
    ln3t = s.sb("ln3t", [128, TO], F32)
    rr3 = s.sb("rr3", [1, TO], F32)
    rr3b = s.sb("rr3b", [128, TO], F32)
    sq3 = s.sb("sq3", [128, 512], F32)
    ps3 = s.ps("ps3", [1, 512], F32)
    for st in range(2):
        c0 = st * 512
        for pg in range(DG):
            def ve(v, pg=pg, st=st, c0=c0):
                s.w(v, "ss")
                s.inc(v.tensor_tensor(ln3t[:, 0:512], resid[pg][:, c0:c0 + 512],
                                      rstd3_bt[:, c0:c0 + 512], ALU.mult), "vs", 1)
                s.inc(v.tensor_tensor(ln3t[:, 0:512], ln3t[:, 0:512],
                                      nmr3_bt[:, c0:c0 + 512], ALU.add), "vs", 1)
            blk.vector(ve)

            def scq(scx, pg=pg):
                s.w(scx, "vs")
                if pg > 0:
                    s.w(scx, "ts")
                s.inc(scx.activation(sq3[:], ln3t[:, 0:512], AF.Square), "ss", 1)
            blk.scalar(scq)

            def te(t, pg=pg):
                s.w(t, "ss")
                s.inc(t.matmul(ps3[:], onescol[:], sq3[:], start=(pg == 0), stop=(pg == DG - 1)), "ts", 1)
            blk.tensor(te)

        def ve30(v, c0=c0):
            s.w(v, "ts")
            s.inc(v.tensor_scalar(rr3[:, c0:c0 + 512], ps3[:], 1.0 / D, None, ALU.mult), "vs", 1)
        blk.vector(ve30)

    def sc9(scx):
        s.w(scx, "vs")
        s.inc(scx.activation(rr3[:], rr3[:], AF.Ln, bias=eps_t[0:1, 0:1]), "ss", 1)
        s.inc(scx.activation(rr3[:], rr3[:], AF.Exp, scale=-0.5), "ss", 1)
    blk.scalar(sc9)
    expand_f32("rr3x", rr3, TO, lambda a, w: rr3b[:, a:a + w])
    for pg in range(DG):
        def ve(v, pg=pg):
            if pg == 0:
                s.w(v, "ss")
            s.inc(v.tensor_tensor(ln3t[:], resid[pg][:], rstd3_bt[:], ALU.mult), "vs", 1)
            s.inc(v.tensor_tensor(ln3t[:], ln3t[:], nmr3_bt[:], ALU.add), "vs", 1)
            s.inc(v.tensor_tensor(ln3t[:], ln3t[:], rr3b[:], ALU.mult), "vs", 1)
            s.inc(v.tensor_scalar(rn[pg][:], ln3t[:], nrmw[:, pg:pg + 1], None, ALU.mult), "vs", 1)
        blk.vector(ve)
    dbg("hr3", r3[0:1, :], [1, TO], F32)
    dbg("hn3", n3[0:1, :], [1, TO], F32)
    dbg("hrr3", rr3[0:1, :], [1, TO], F32)
    s.pop()
    dbg("rn", rn[0][:, :], [128, TO], BF16)

    def gp8(g):
        s.w(g, "vs")
        for i in range(DG):
            s.inc(g.dma_start(out=c1_in[128 * i:128 * (i + 1), :], in_=rn[i][:]), "gd", 16)
        s.w(g, "gd")
        s.inc(g.collective_compute("AllGather", ALU.bypass, replica_groups=PAIRS,
                                   ins=[c1_in.ap().opt()], outs=[c1_out.ap().opt()]), "cc", 1)
        s.w(g, "cc")
        for i in range(DG):
            s.inc(g.dma_start(out=rn_warm[i][:], in_=c1_out[128 * i:128 * (i + 1), TO - WARM:TO]), "gd", 16)
    blk.gpsimd(gp8)

    def ve31(v):
        s.w(v, "gd")
        for i in range(DG):
            s.inc(v.tensor_scalar(rn_warm[i][:], rn_warm[i][:], rrecv_t[:, 0:1], None, ALU.mult), "vs", 1)
    blk.vector(ve31)

    s.push()
    rnh = [s.sb(f"rnh{i}", [128, TO], BF16) for i in range(DG)]
    wbufH = [s.sb(f"wbufH{i}", [128, 16 * 128], BF16) for i in range(2)]
    sktile = s.sb("sktile", [128, TO], BF16)
    psH = [s.ps(f"psH{i}", [128, 512], F32) for i in range(8)]
    bank_H = [0] * 8
    w_H = [0, 0]
    sk_od = [0]
    for thalf in range(2):
        def gph(g, thalf=thalf):
            s.w(g, "cc")
            if thalf > 0:
                s.w(g, "ts")
            for i in range(DG):
                s.inc(g.dma_start(out=rnh[i][:],
                                  in_=c1_out[thalf * D + 128 * i:thalf * D + 128 * (i + 1), :]), "gd", 16)
        blk.gpsimd(gph)
        for eg in range(EGL):
            wb = wbufH[eg % 2]

            def gp(g, eg=eg, wb=wb, thalf=thalf):
                s.w(g, "ts", w_H[eg % 2])
                s.inc(g.dma_start(out=wb[:].rearrange("p (k m) -> p k m", k=16),
                                  in_=W_skip[:, 128 * eg:128 * (eg + 1)]
                                  .rearrange("(k p) m -> p k m", p=128)), "gd", 16)
            blk.gpsimd(gp)
            bids = [(2 * eg) % 8, (2 * eg + 1) % 8]

            def te(t, eg=eg, wb=wb, bids=bids):
                s.w(t, "gd")
                for bi in bids:
                    s.w(t, "ss", bank_H[bi])
                for k in range(DG):
                    for ti in range(2):
                        s.inc(t.matmul(psH[bids[ti]][:], wb[:, 128 * k:128 * (k + 1)],
                                       rnh[k][:, 512 * ti:512 * (ti + 1)],
                                       start=(k == 0), stop=(k == DG - 1)), "ts", 1)
                w_H[eg % 2] = s.c["ts"]
            blk.tensor(te)

            def sc(scx, eg=eg, bids=bids):
                s.w(scx, "ts")
                s.w(scx, "od", sk_od[0])
                for ti in range(2):
                    s.inc(scx.activation(sktile[:, 512 * ti:512 * (ti + 1)], psH[bids[ti]][:], AF.Silu), "ss", 1)
                for bi in bids:
                    bank_H[bi] = s.c["ss"]
            blk.scalar(sc)

            def sy(sx, eg=eg, thalf=thalf):
                s.w(sx, "ss")
                s.inc(sx.dma_start(out=skip_spill[128 * eg:128 * (eg + 1), thalf * TO:(thalf + 1) * TO],
                                   in_=sktile[:]), "od", 16)
                sk_od[0] = s.c["od"]
            blk.sync(sy)
    s.pop()

    s.push()
    s.push()
    xcbh_t = [s.sb(f"xcbht{i}", [128, TO], BF16) for i in range(2)]
    wbufI = [s.sb(f"wbufI{i}", [128, 16 * 128], BF16) for i in range(2)]
    xin_eg = [s.sb(f"xin{i}", [128, WARM + TO], BF16) for i in range(2)]
    cacc2 = s.sb("cacc2", [128, TO], F32)
    psI = [s.ps(f"psI{i}", [128, 512], F32) for i in range(6)]
    bank_I = [0] * 6
    w_I = [0, 0]
    xin_I = [0, 0]
    xcbh_od = [0, 0]
    for eg in range(EG):
        wb = wbufI[eg % 2]
        xie = xin_eg[eg % 2]

        def gp(g, eg=eg, wb=wb):
            s.w(g, "ts", w_I[eg % 2])
            s.inc(g.dma_start(out=wb[:].rearrange("p (k m) -> p k m", k=16),
                              in_=W_in[:, 128 * eg:128 * (eg + 1)]
                              .rearrange("(k p) m -> p k m", p=128)), "gd", 16)
        blk.gpsimd(gp)
        bids = [(3 * eg) % 6, (3 * eg + 1) % 6, (3 * eg + 2) % 6]

        def te(t, eg=eg, wb=wb, bids=bids):
            s.w(t, "gd")
            for bi in bids:
                s.w(t, "ss", bank_I[bi])
            for k in range(DG):
                s.inc(t.matmul(psI[bids[0]][:, 0:WARM], wb[:, 128 * k:128 * (k + 1)], rn_warm[k][:],
                               start=(k == 0), stop=(k == DG - 1)), "ts", 1)
                for ti in range(2):
                    s.inc(t.matmul(psI[bids[1 + ti]][:], wb[:, 128 * k:128 * (k + 1)],
                                   rn[k][:, 512 * ti:512 * (ti + 1)],
                                   start=(k == 0), stop=(k == DG - 1)), "ts", 1)
            w_I[eg % 2] = s.c["ts"]
        blk.tensor(te)

        def sc(scx, eg=eg, bids=bids, xie=xie):
            s.w(scx, "ts")
            s.w(scx, "vs", xin_I[eg % 2])
            s.inc(scx.copy(xie[:, 0:WARM], psI[bids[0]][:, 0:WARM]), "ss", 1)
            for ti in range(2):
                s.inc(scx.copy(xie[:, WARM + 512 * ti:WARM + 512 * (ti + 1)], psI[bids[1 + ti]][:]), "ss", 1)
            for bi in bids:
                bank_I[bi] = s.c["ss"]
        blk.scalar(sc)

        def ve(v, eg=eg, xie=xie):
            s.w(v, "ss")
            s.inc(v.tensor_scalar(cacc2[:], xie[:, WARM - 3:WARM - 3 + TO],
                                  cbw[:, 4 * eg:4 * eg + 1], None, ALU.mult), "vs", 1)
            for k in range(1, 4):
                s.inc(v.scalar_tensor_tensor(cacc2[:], xie[:, WARM - 3 + k:WARM - 3 + k + TO],
                                             cbw[:, 4 * eg + k:4 * eg + k + 1], cacc2[:],
                                             ALU.mult, ALU.add), "vs", 1)
            xin_I[eg % 2] = s.c["vs"]
        blk.vector(ve)

        if eg < EGL:
            def sc2(scx, eg=eg):
                s.w(scx, "vs")
                s.inc(scx.activation(xcb[eg][:], cacc2[:], AF.Silu, bias=cbb[:, eg:eg + 1]), "ss", 1)
            blk.scalar(sc2)
        else:
            def sc2(scx, eg=eg):
                s.w(scx, "vs")
                s.w(scx, "od", xcbh_od[eg % 2])
                s.inc(scx.activation(xcbh_t[eg % 2][:], cacc2[:], AF.Silu, bias=cbb[:, eg:eg + 1]), "ss", 1)
            blk.scalar(sc2)

            def syh(sx, eg=eg):
                s.w(sx, "ss")
                s.inc(sx.dma_start(out=c3b_in[128 * (eg - EGL):128 * (eg - EGL + 1), :],
                                   in_=xcbh_t[eg % 2][:]), "od", 16)
                xcbh_od[eg % 2] = s.c["od"]
            blk.sync(syh)
    s.pop()
    dbg("xcb0", xcb[0][:, :], [128, TO], BF16)

    wd1 = s.sb("wd1", [128, 32 * 128], BF16)
    wBt = s.sb("wBt", [128, 32 * 16], BF16)
    wCt = s.sb("wCt", [128, 32 * 16], BF16)
    d1t = s.sb("d1t", [128, TO], BF16)
    B2t = s.sb("B2t", [NB, TO], F32)
    C2tt = s.sb("C2tt", [NB, TO], F32)
    psJ = [s.ps(f"psJ{i}", [128, 512], F32) for i in range(6)]

    def gp9(g):
        s.inc(g.dma_start(out=wd1[:].rearrange("p (k m) -> p k m", k=32),
                          in_=W_d1.ap().rearrange("(k p) m -> p k m", p=128)), "gd", 16)
        s.inc(g.dma_start(out=wBt[:].rearrange("p (k m) -> p k m", k=32),
                          in_=W_Bp.ap().rearrange("(k p) m -> p k m", p=128)), "gd", 16)
        s.inc(g.dma_start(out=wCt[:].rearrange("p (k m) -> p k m", k=32),
                          in_=W_Cp.ap().rearrange("(k p) m -> p k m", p=128)), "gd", 16)
    blk.gpsimd(gp9)

    xhk = [s.sb(f"xhk{i}", [128, TO], BF16) for i in range(2)]
    xhk_free = [0, 0]
    for ti in range(2):
        for k in range(EG):
            if k >= EGL:
                def gph2(g, k=k, ti=ti):
                    s.w(g, "ts", xhk_free[k % 2])
                    if k == EGL and ti == 0:
                        s.w(g, "od")
                    s.inc(g.dma_start(out=xhk[k % 2][:],
                                      in_=c3b_in[128 * (k - EGL):128 * (k - EGL + 1), :]), "gd", 16)
                blk.gpsimd(gph2)

            def te17(t, k=k, ti=ti):
                s.w(t, "gd")
                if k == 0 and ti == 0:
                    s.w(t, "ss")
                rhs = xcb[k][:, 512 * ti:512 * (ti + 1)] if k < EGL \
                    else xhk[k % 2][:, 512 * ti:512 * (ti + 1)]
                s.inc(t.matmul(psJ[ti][:], wd1[:, 128 * k:128 * (k + 1)], rhs,
                               start=(k == 0), stop=(k == EG - 1)), "ts", 1)
                s.inc(t.matmul(psJ[2 + ti][0:16, :], wBt[:, 16 * k:16 * (k + 1)], rhs,
                               start=(k == 0), stop=(k == EG - 1)), "ts", 1)
                s.inc(t.matmul(psJ[4 + ti][0:16, :], wCt[:, 16 * k:16 * (k + 1)], rhs,
                               start=(k == 0), stop=(k == EG - 1)), "ts", 1)
                if k >= EGL:
                    xhk_free[k % 2] = s.c["ts"]
            blk.tensor(te17)

    def sc10(scx):
        s.w(scx, "ts")
        for ti in range(2):
            s.inc(scx.copy(d1t[:, 512 * ti:512 * (ti + 1)], psJ[ti][:]), "ss", 1)
            s.inc(scx.copy(B2t[:, 512 * ti:512 * (ti + 1)], psJ[2 + ti][0:16, :]), "ss", 1)
            s.inc(scx.copy(C2tt[:, 512 * ti:512 * (ti + 1)], psJ[4 + ti][0:16, :]), "ss", 1)
    blk.scalar(sc10)

    wd2 = s.sb("wd2", [128, 32 * 128], BF16)
    bdc = s.sb("bdc", [128, EG], F32)
    dtile = [s.sb(f"dtile{i}", [128, TO], F32) for i in range(2)]

    def gp10(g):
        s.w(g, "ss")
        s.inc(g.dma_start(out=wd2[:], in_=W_d2[:]), "gd", 16)
    blk.gpsimd(gp10)

    def sy9(sx):
        for eg in range(EG):
            s.inc(sx.dma_start(out=bdc[:, eg:eg + 1], in_=b_d2[128 * eg:128 * (eg + 1), :]), "ld", 16)
    blk.sync(sy9)
    dt_free = [0, 0]
    psJ_free = [0] * 6
    for mo in range(EG):
        dte = dtile[mo % 2]
        bids = [(2 * mo) % 6, (2 * mo + 1) % 6]

        def te18(t, mo=mo, bids=bids):
            if mo == 0:
                s.w(t, "gd")
                s.w(t, "ld")
            for bi in bids:
                s.w(t, "ss", psJ_free[bi])
            for ti in range(2):
                s.inc(t.matmul(psJ[bids[ti]][:], wd2[:, 128 * mo:128 * (mo + 1)],
                               d1t[:, 512 * ti:512 * (ti + 1)], start=True, stop=True), "ts", 1)
        blk.tensor(te18)

        def sc11(scx, mo=mo, bids=bids, dte=dte):
            s.w(scx, "ts")
            s.w(scx, "od", dt_free[mo % 2])
            for ti in range(2):
                s.inc(scx.activation(dte[:, 512 * ti:512 * (ti + 1)], psJ[bids[ti]][:],
                                     AF.Exp, bias=bdc[:, mo:mo + 1]), "ss", 1)
                s.inc(scx.activation(dte[:, 512 * ti:512 * (ti + 1)], dte[:, 512 * ti:512 * (ti + 1)],
                                     AF.Ln, bias=one_t[:, 0:1]), "ss", 1)
            for bi in bids:
                psJ_free[bi] = s.c["ss"]
        blk.scalar(sc11)

        def sy10(sx, mo=mo, dte=dte):
            s.w(sx, "ss")
            s.inc(sx.dma_start(out=delta_spill[128 * mo:128 * (mo + 1), :], in_=dte[:]), "od", 16)
            dt_free[mo % 2] = s.c["od"]
        blk.sync(sy10)

    def gp11(g):
        s.w(g, "od")
        s.inc(g.dma_start(out=c3f_in[0:EL, :], in_=delta_spill[EL:E, :]), "gd", 16)
    blk.gpsimd(gp11)

    def sy11(sx):
        s.w(sx, "ss")
        s.inc(sx.dma_start(out=c3f_in[EL:EL + NB, :], in_=B2t[:]), "od", 16)
        s.inc(sx.dma_start(out=c3f_in[EL + NB:EL + 2 * NB, :], in_=C2tt[:]), "od", 16)
    blk.sync(sy11)

    def gp12(g):
        s.w(g, "od")
        s.w(g, "gd")
        s.inc(g.collective_compute("AllGather", ALU.bypass, replica_groups=PAIRS,
                                   ins=[c3f_in.ap().opt()], outs=[c3f_out.ap().opt()]), "cc", 1)
        s.w(g, "cc")
        s.inc(g.collective_compute("AllGather", ALU.bypass, replica_groups=PAIRS,
                                   ins=[c3b_in.ap().opt()], outs=[c3b_out.ap().opt()]), "cc", 1)
    blk.gpsimd(gp12)

    B_full = s.sb("B_full", [NB, T], BF16)
    C_full = s.sb("C_full", [NB, T], BF16)

    def gp13(g):
        s.w(g, "cc")
        s.inc(g.dma_start(out=B_full[:, 0:TO], in_=c3f_out[EL:EL + NB, :]), "gd", 16)
        s.inc(g.dma_start(out=B_full[:, TO:T], in_=c3f_out[NF3 + EL:NF3 + EL + NB, :]), "gd", 16)
        s.inc(g.dma_start(out=C_full[:, 0:TO], in_=c3f_out[EL + NB:EL + 2 * NB, :]), "gd", 16)
        s.inc(g.dma_start(out=C_full[:, TO:T], in_=c3f_out[NF3 + EL + NB:NF3 + EL + 2 * NB, :]), "gd", 16)
    blk.gpsimd(gp13)

    def sy12(sx):
        s.w(sx, "gd")
        s.inc(sx.dma_start(out=Brow_dram[:], in_=B_full[:]), "od", 16)
        s.inc(sx.dma_start(out=Crow_dram[:], in_=C_full[:]), "od", 16)
    blk.sync(sy12)
    s.pop()
    s.pop()

    s.push()
    Brow_n = [s.sb(f"Brow_n{i}", [1, T], BF16) for i in range(2)]
    Crow_n = [s.sb(f"Crow_n{i}", [1, T], BF16) for i in range(2)]
    brow_free = [0, 0]
    dsc = s.sb("dsc", [128, T], F32)
    xsc = s.sb("xsc", [128, T], BF16)
    wsc = s.sb("wsc", [128, T], F32)
    d_loc = s.sb("d_loc", [128, TO], F32)
    d_s0 = s.sb("d_s0", [128, TO], F32)
    d_s1 = s.sb("d_s1", [128, TO], F32)
    x_s0 = s.sb("x_s0", [128, TO], BF16)
    x_s1 = s.sb("x_s1", [128, TO], BF16)
    a_n = [s.sb(f"a_n{i}", [128, T], F32) for i in range(2)]
    u_n = s.sb("u_n", [128, T], F32)
    h_n = s.sb("h_n", [128, T], F32)
    y_eg = s.sb("y_eg", [128, T], F32)
    sk_eg = s.sb("sk_eg", [128, T], BF16)
    psBt = s.ps("psBt", [128, 2048], F32)
    psCt = s.ps("psCt", [128, 2048], F32)
    ysk_od2 = [0]
    u_done = [0]
    q_done = [0]
    scan_done = [0, 0]
    a_done = [0, 0]
    teB_done = [0]
    for eg in range(EGL):
        def gp(g, eg=eg):
            s.w(g, "vs")
            s.w(g, "od", ysk_od2[0])
            s.inc(g.dma_start(out=d_loc[:], in_=delta_spill[128 * eg:128 * (eg + 1), :]), "gd", 16)
            s.inc(g.dma_start(out=d_s0[:], in_=c3f_out[128 * eg:128 * (eg + 1), :]), "gd", 16)
            s.inc(g.dma_start(out=d_s1[:], in_=c3f_out[NF3 + 128 * eg:NF3 + 128 * (eg + 1), :]), "gd", 16)
            s.inc(g.dma_start(out=x_s0[:], in_=c3b_out[128 * eg:128 * (eg + 1), :]), "gd", 16)
            s.inc(g.dma_start(out=x_s1[:], in_=c3b_out[EL + 128 * eg:EL + 128 * (eg + 1), :]), "gd", 16)
            s.inc(g.dma_start(out=sk_eg[:], in_=skip_spill[128 * eg:128 * (eg + 1), :]), "gd", 16)
        blk.gpsimd(gp)

        def ve(v, eg=eg):
            s.w(v, "gd")
            mask = selr0_t[:, 0:1].broadcast_to([128, TO])
            s.inc(v.select(dsc[:, 0:TO], mask, d_loc[:], d_s0[:]), "vs", 1)
            s.inc(v.select(dsc[:, TO:T], mask, d_s1[:], d_loc[:]), "vs", 1)
            s.inc(v.select(xsc[:, 0:TO], mask, xcb[eg][:], x_s0[:]), "vs", 1)
            s.inc(v.select(xsc[:, TO:T], mask, x_s1[:], xcb[eg][:]), "vs", 1)
            s.inc(v.tensor_tensor(wsc[:], dsc[:], xsc[:], ALU.mult), "vs", 1)
        blk.vector(ve)
        for n in range(NB):
            n2i = n % 2

            def gpn(g, eg=eg, n=n, n2i=n2i):
                s.w(g, "ts", brow_free[n2i])
                if eg == 0 and n == 0:
                    s.w(g, "od")
                s.inc(g.dma_start(out=Brow_n[n2i][:], in_=Brow_dram[n:n + 1, :]), "gd", 16)
                s.inc(g.dma_start(out=Crow_n[n2i][:], in_=Crow_dram[n:n + 1, :]), "gd", 16)
            blk.gpsimd(gpn)

            def teB(t, eg=eg, n=n, n2i=n2i):
                s.w(t, "gd")
                s.w(t, "vs", u_done[0])
                for q4 in range(4):
                    s.inc(t.matmul(psBt[:, 512 * q4:512 * (q4 + 1)], ones1[:],
                                   Brow_n[n2i][0:1, 512 * q4:512 * (q4 + 1)],
                                   start=True, stop=True), "ts", 1)
                teB_done[0] = s.c["ts"]
            blk.tensor(teB)

            def sc12(scx, eg=eg, n=n, n2i=n2i):
                s.w(scx, "vs", scan_done[n2i])
                if n == 0:
                    s.w(scx, "vs")
                s.inc(scx.activation(a_n[n2i][:], dsc[:], AF.Exp,
                                     scale=Ab_t[:, NB * eg + n:NB * eg + n + 1]), "ss", 1)
                a_done[n2i] = s.c["ss"]
            blk.scalar(sc12)

            def teC(t, eg=eg, n=n, n2i=n2i):
                s.w(t, "vs", q_done[0])
                for q4 in range(4):
                    s.inc(t.matmul(psCt[:, 512 * q4:512 * (q4 + 1)], ones1[:],
                                   Crow_n[n2i][0:1, 512 * q4:512 * (q4 + 1)],
                                   start=True, stop=True), "ts", 1)
                brow_free[n2i] = s.c["ts"]
            blk.tensor(teC)

            def ve32(v, eg=eg, n=n, n2i=n2i):
                s.w(v, "ts", teB_done[0])
                s.w(v, "ss", a_done[n2i])
                s.inc(v.tensor_tensor(u_n[:], wsc[:], psBt[:], ALU.mult), "vs", 1)
                u_done[0] = s.c["vs"]
                s.inc(v.tensor_tensor_scan(h_n[:], a_n[n2i][:], u_n[:], 0.0, ALU.mult, ALU.add), "vs", 1)
                scan_done[n2i] = s.c["vs"]
                s.w(v, "ts")
                if n == 0:
                    s.inc(v.tensor_tensor(y_eg[:], h_n[:], psCt[:], ALU.mult), "vs", 1)
                else:
                    s.inc(v.tensor_tensor(h_n[:], h_n[:], psCt[:], ALU.mult), "vs", 1)
                    s.inc(v.tensor_tensor(y_eg[:], y_eg[:], h_n[:], ALU.add), "vs", 1)
                q_done[0] = s.c["vs"]
            blk.vector(ve32)

        def ve33(v, eg=eg):
            s.w(v, "od", ysk_od2[0])
            s.inc(v.scalar_tensor_tensor(y_eg[:], xsc[:], wdcol[:, eg:eg + 1], y_eg[:],
                                         ALU.mult, ALU.add), "vs", 1)
            s.inc(v.tensor_tensor(sk_eg[:], y_eg[:], sk_eg[:], ALU.mult), "vs", 1)
        blk.vector(ve33)

        def sy13(sx, eg=eg):
            s.w(sx, "vs")
            s.inc(sx.dma_start(out=ysk_spill[128 * eg:128 * (eg + 1), :], in_=sk_eg[:]), "od", 16)
            ysk_od2[0] = s.c["od"]
        blk.sync(sy13)
    s.pop()
    dbg("delta0", delta_spill[0:128, :], [128, TO], F32)
    dbg("ysk5", ysk_spill[128 * 5:128 * 6, :], [128, T], BF16)
    dbg("ysk15", ysk_spill[128 * 15:128 * 16, :], [128, T], BF16)
    dbg("skip0", skip_spill[0:128, :], [128, T], BF16)
    dbg("ysk0", ysk_spill[0:128, :], [128, T], BF16)
    dbg("ds0", c3f_out[0:128, :], [128, TO], F32)
    dbg("ds1", c3f_out[NF3:NF3 + 128, :], [128, TO], F32)

    s.push()
    wbufO = s.sb("wbufO", [128, 16 * 8 * 128], BF16)
    yskt = [s.sb(f"yskt{i}", [128, 512], BF16) for i in range(16)]
    otile = s.sb("otile", [128, 512], F32)
    dbg_ps = s.sb("dbg_ps", [128, 512], F32)
    dbg_yt = s.sb("dbg_yt", [128, 512], BF16)
    dbg_wo = s.sb("dbg_wo", [128, 128], BF16)
    psO2 = [s.ps(f"psO2_{i}", [128, 512], F32) for i in range(8)]
    ysk_free = [0, 0]
    psO2_free = [0] * 8
    for dgw in range(2):
        def gpw2(g, dgw=dgw):
            s.w(g, "ts")
            s.inc(g.dma_start(out=wbufO[:].rearrange("p (k m) -> p k m", k=16),
                              in_=W_out_in[0:EL, 1024 * dgw:1024 * (dgw + 1)]
                              .rearrange("(k p) m -> p k m", p=128)), "gd", 16)
        blk.gpsimd(gpw2)
        for tt in range(4):
            def gp15(g, tt=tt, dgw=dgw):
                s.w(g, "ts", ysk_free[0])
                for k in range(EGL):
                    s.inc(g.dma_start(out=yskt[k][:],
                                      in_=ysk_spill[128 * k:128 * (k + 1), 512 * tt:512 * (tt + 1)]), "gd", 16)
            blk.gpsimd(gp15)
            for k in range(EGL):
                def te20(t, tt=tt, k=k, dgw=dgw):
                    s.w(t, "gd")
                    if k == 0:
                        for bi in range(8):
                            s.w(t, "vs", psO2_free[bi])
                    for dg8 in range(8):
                        s.inc(t.matmul(psO2[dg8][:], wbufO[:, 1024 * k + 128 * dg8:1024 * k + 128 * (dg8 + 1)],
                                       yskt[k][:], start=(k == 0), stop=(k == EGL - 1)), "ts", 1)
                    ysk_free[0] = s.c["ts"]
                blk.tensor(te20)
                if dgw == 0 and tt == 0 and k == 0:
                    def vdbg(v):
                        s.w(v, "ts")
                        s.inc(v.tensor_copy(dbg_yt[:], yskt[0][:]), "vs", 1)
                        s.inc(v.tensor_copy(dbg_wo[:], wbufO[:, 0:128]), "vs", 1)
                    blk.vector(vdbg)
            for dg8 in range(8):
                def ve34(v, dgw=dgw, tt=tt, dg8=dg8):
                    s.w(v, "ts")
                    s.w(v, "od")
                    s.inc(v.tensor_copy(otile[:], psO2[dg8][:]), "vs", 1)
                    if dgw == 0 and tt == 0 and dg8 == 0:
                        s.inc(v.tensor_copy(dbg_ps[:], psO2[0][:]), "vs", 1)
                    psO2_free[dg8] = s.c["vs"]
                blk.vector(ve34)

                def sy14(sx, dgw=dgw, tt=tt, dg8=dg8):
                    s.w(sx, "vs")
                    dg = dgw * 8 + dg8
                    j = tt % 2
                    half = 0 if tt < 2 else 1
                    s.inc(sx.dma_start(out=c4_in[j][half * D + 128 * dg:half * D + 128 * (dg + 1), :],
                                       in_=otile[:]), "od", 16)
                blk.sync(sy14)
    dbg("h6ps", dbg_ps[:, :], [128, 512], F32)
    dbg("h6yt", dbg_yt[:, :], [128, 512], BF16)
    dbg("h6wo", dbg_wo[:, :], [128, 128], BF16)
    dbg("c4pre", c4_in[0][0:128, :], [128, 512], F32)
    dbg("c4preb", c4_in[0][D:D + 128, :], [128, 512], F32)
    for j in range(2):
        def gp16(g, j=j):
            s.w(g, "od")
            s.w(g, "cc")
            s.inc(g.collective_compute("ReduceScatter", ALU.add, replica_groups=PAIRS,
                                       ins=[c4_in[j].ap().opt()], outs=[c4_out[j].ap().opt()]), "cc", 1)
        blk.gpsimd(gp16)

    rs_t = s.sb("rs_t", [128, 512], F32)
    ln3b = s.sb("ln3b", [128, 512], F32)
    rs_free = [0]
    for j in range(2):
        for dg in range(DG):
            def gp17(g, j=j, dg=dg):
                s.w(g, "cc")
                s.w(g, "vs", rs_free[0])
                s.inc(g.dma_start(out=rs_t[:], in_=c4_out[j][128 * dg:128 * (dg + 1), :]), "gd", 16)
            blk.gpsimd(gp17)

            def ve35(v, j=j, dg=dg):
                s.w(v, "gd")
                s.inc(v.tensor_tensor(ln3b[:], resid[dg][:, 512 * j:512 * (j + 1)],
                                      rstd3_bt[:, 512 * j:512 * (j + 1)], ALU.mult), "vs", 1)
                s.inc(v.tensor_tensor(ln3b[:], ln3b[:], nmr3_bt[:, 512 * j:512 * (j + 1)], ALU.add), "vs", 1)
                s.inc(v.tensor_tensor(ln3b[:], ln3b[:], rs_t[:], ALU.add), "vs", 1)
                s.inc(v.tensor_tensor(resid[dg][:, 512 * j:512 * (j + 1)],
                                      resid[dg][:, 512 * j:512 * (j + 1)], ln3b[:], ALU.add), "vs", 1)
                rs_free[0] = s.c["vs"]
            blk.vector(ve35)
    s.pop()
    s.pop()
    dbg("resid3", resid[0][:, :], [128, TO], F32)
    dbg("c4o0", c4_out[0][0:128, :], [128, 512], F32)
    dbg("c4i0a", c4_in[0][0:128, :], [128, 512], F32)
    dbg("c4i0b", c4_in[0][D:D + 128, :], [128, 512], F32)
    if upto == "H":
        return fin()

    ffn_phase("i", False)
    return fin()


def _prep_core_inputs(core, inp):
    b, r = core // 2, core % 2
    t0 = r * TO
    f32 = np.float32

    def c(a):
        return np.ascontiguousarray(np.asarray(a), dtype=f32)

    x = np.asarray(inp["L0_attn_a_x"])
    xT = c(x[b, t0:t0 + TO, :].T)
    if r == 0:
        xwT = np.zeros((D, WARM), f32)
    else:
        xwT = c(x[b, t0 - WARM:t0, :].T)

    def roll(a, ax):
        return np.ascontiguousarray(np.roll(a, -r * EL, axis=ax))

    return {
        "xT": xT, "xwT": xwT,
        "W_inprojT": c(np.asarray(inp["L0_attn_a_in_proj_w"]).T),
        "W_xprojT": c(np.asarray(inp["L0_attn_a_x_proj_w"]).T),
        "conv_a_w": c(inp["L0_attn_a_conv_w"]),
        "conv_a_b": c(inp["L0_attn_a_conv_b"]).reshape(E, 1),
        "W_dtT": c(np.asarray(inp["L0_attn_a_dt_proj_w"]).T),
        "dt_b_col": c(inp["L0_attn_a_dt_proj_b"]).reshape(H, 1),
        "Alog_a": c(inp["L0_attn_a_A_log"]).reshape(H, 1),
        "D_a_row": np.repeat(c(inp["L0_attn_a_D"]), PDIM).reshape(1, E),
        "W_outprojT": c(np.asarray(inp["L0_attn_a_out_proj_w"]).T),
        "ffn_WT": c(np.asarray(inp["L0_ffn_weight"]).T),
        "ffn_gamma": c(inp["L0_ffn_gamma"]).reshape(D, 1),
        "ffn_beta": c(inp["L0_ffn_beta"]).reshape(D, 1),
        "ffn_lnw": c(inp["L0_ffn_ln_weight"]).reshape(D, 1),
        "ffn_lnb": c(inp["L0_ffn_ln_bias"]).reshape(D, 1),
        "norm_w": c(inp["L0_attn_b_norm_w"]).reshape(D, 1),
        "W_skip": roll(c(inp["L0_attn_b_W_skip"]), 1),
        "W_in": roll(c(inp["L0_attn_b_W_in"]), 1),
        "conv_b_w": roll(c(np.asarray(inp["L0_attn_b_conv_w"]).reshape(4, E).T), 0),
        "conv_b_b": roll(c(inp["L0_attn_b_conv_b"]), 0).reshape(E, 1),
        "W_d1": roll(c(inp["L0_attn_b_W_delta_1"]), 0),
        "W_d2": roll(c(inp["L0_attn_b_W_delta_2"]), 1),
        "b_d2": roll(c(inp["L0_attn_b_b_delta_2"]), 0).reshape(E, 1),
        "W_Bp": roll(c(inp["L0_attn_b_W_B"]), 0),
        "W_Cp": roll(c(inp["L0_attn_b_W_C"]), 0),
        "Alog_b": roll(c(inp["L0_attn_b_A_log"]), 0),
        "W_D_col": roll(c(inp["L0_attn_b_W_D"]), 0).reshape(E, 1),
        "W_out_in": roll(c(inp["L0_attn_b_W_out"]), 0),
        "m_send": np.full((128, 1), 1.0 - r, f32),
        "r_recv": np.full((128, 1), float(r), f32),
        "sel_r0": np.full((128, 1), 1 - r, np.uint8),
    }


def kernel(**inputs):
    nc = build_nc()
    in_maps = [_prep_core_inputs(cc, inputs) for cc in range(8)]
    res = run_bass_kernel_spmd(nc, in_maps, core_ids=list(range(8)))
    o = np.zeros((4, T, D), np.float32)
    for cc in range(8):
        b, r = cc // 2, cc % 2
        o[b, r * TO:(r + 1) * TO, :] = np.asarray(res.results[cc]["out"]).T
    return o


# version 06
# speedup vs baseline: 1.2993x; 1.0037x over previous
import numpy as np
import ml_dtypes
import concourse.bass as bass
import concourse.mybir as mybir
from concourse.bass_utils import run_bass_kernel_spmd

F32 = mybir.dt.float32
BF16 = mybir.dt.bfloat16
U8 = mybir.dt.uint8
AF = mybir.ActivationFunctionType
ALU = mybir.AluOpType

PAIRS = [[0, 1], [2, 3], [4, 5], [6, 7]]
D = 2048
E = 4096
EL = 2048
H = 64
PDIM = 64
NA = 128
NB = 16
T = 2048
TO = 1024
CH = 128
NCH = TO // CH
WARM = 128
DG = D // 128
EG = E // 128
EGL = EL // 128

bfd = ml_dtypes.bfloat16
SEMS = ["ld", "gd", "ts", "ss", "vs", "cc", "od", "gs"]


class Bld:
    def __init__(self, nc, blk):
        self.nc = nc
        self.blk = blk
        self.h = {n: nc.alloc_semaphore(n) for n in SEMS}
        self.c = {n: 0 for n in SEMS}
        self.scopes = []
        self.names = set()

    def inc(self, instr, name, n):
        instr.then_inc(self.h[name], n)
        self.c[name] += n
        return self.c[name]

    def w(self, eng, name, v=None):
        v = self.c[name] if v is None else v
        if v > 0:
            eng.wait_ge(self.h[name], v)

    def push(self):
        self.scopes.append([])

    def pop(self):
        self.join_all()
        for cm in reversed(self.scopes.pop()):
            cm.__exit__(None, None, None)

    def _alloc(self, cm):
        t = cm.__enter__()
        if self.scopes:
            self.scopes[-1].append(cm)
        return t

    def sb(self, name, shape, dt):
        assert name not in self.names, name
        self.names.add(name)
        return self._alloc(self.nc.sbuf_tensor(name, shape, dt))

    def ps(self, name, shape, dt=F32):
        assert name not in self.names, name
        self.names.add(name)
        return self._alloc(self.nc.psum_tensor(name, shape, dt))

    def join_all(self):
        snap = dict(self.c)
        for engname in ["sync", "gpsimd", "tensor", "scalar", "vector"]:
            def body(e, snap=snap):
                for n in SEMS:
                    if snap[n] > 0:
                        e.wait_ge(self.h[n], snap[n])
            getattr(self.blk, engname)(body)


def build_nc(upto=None, debug_keys=()):
    nc = bass.Bass("TRN2", target_bir_lowering=False)

    def di(name, shape, dt=F32):
        return nc.dram_tensor(name, shape, dt, kind="ExternalInput")

    xT = di("xT", [D, TO])
    xwT = di("xwT", [D, WARM])
    W_inprojT = di("W_inprojT", [64 * 128, 16 * 128])
    W_xprojT = di("W_xprojT", [32 * 128, 32 * 128])
    W_xptail = di("W_xptail", [E, 320])
    conv_a_w = di("conv_a_w", [E, 4])
    conv_a_b = di("conv_a_b", [E, 1])
    W_dtT = di("W_dtT", [H, H])
    dt_b_col = di("dt_b_col", [H, 1])
    Alog_a = di("Alog_a", [H, 1])
    D_a_row = di("D_a_row", [1, E])
    W_outprojT = di("W_outprojT", [2 * 128, 32 * 1024])
    ffn_WT = di("ffn_WT", [16 * 128, 16 * 128])
    ffn_gamma = di("ffn_gamma", [D, 1])
    ffn_beta = di("ffn_beta", [D, 1])
    ffn_lnw = di("ffn_lnw", [D, 1])
    ffn_lnb = di("ffn_lnb", [D, 1])
    norm_w = di("norm_w", [D, 1])
    W_skip = di("W_skip", [16 * 128, 16 * 128])
    W_in = di("W_in", [32 * 128, 16 * 128])
    conv_b_w = di("conv_b_w", [E, 4])
    conv_b_b = di("conv_b_b", [E, 1])
    W_d1 = di("W_d1", [E, 128])
    W_d2 = di("W_d2", [128, E])
    b_d2 = di("b_d2", [E, 1])
    W_Bp = di("W_Bp", [E, NB])
    W_Cp = di("W_Cp", [E, NB])
    Alog_b = di("Alog_b", [E, NB])
    W_D_col = di("W_D_col", [E, 1])
    W_out_in = di("W_out_in", [2 * 128, 16 * 1024])
    m_send = di("m_send", [128, 1])
    r_recv = di("r_recv", [128, 1])
    sel_r0 = di("sel_r0", [128, 1], U8)

    out = nc.dram_tensor("out", [D, TO], F32, kind="ExternalOutput")

    def drm(name, shape, dt):
        return nc.dram_tensor(name, shape, dt)

    z_spill = drm("z_spill", [E, TO], BF16)
    X_spill = drm("X_spill", [E, TO], BF16)
    yg_spill = drm("yg_spill", [E, TO], BF16)
    signW_spill = drm("signW_spill", [128, 16 * 2048], BF16)
    skip_spill = drm("skip_spill", [EL, T], BF16)
    delta_spill = drm("delta_spill", [E, TO], F32)
    snap_spill = drm("snap_spill", [NCH * NA, E], BF16)
    cs_dram = drm("cs_dram", [H, TO], F32)
    ysk_spill = drm("ysk_spill", [EL, T], BF16)
    Brow_dram = drm("Brow_dram", [NB, T], BF16)
    Crow_dram = drm("Crow_dram", [NB, T], BF16)

    c2_in = drm("c2_in", [NA, E], BF16)
    c2_out = drm("c2_out", [2 * NA, E], BF16)
    c1_in = drm("c1_in", [D, TO], BF16)
    c1_out = drm("c1_out", [2 * D, TO], BF16)
    NF3 = EL + 2 * NB
    c3f_in = drm("c3f_in", [NF3, TO], F32)
    c3f_out = drm("c3f_out", [2 * NF3, TO], F32)
    c3b_in = drm("c3b_in", [EL, TO], BF16)
    c3b_out = drm("c3b_out", [2 * EL, TO], BF16)
    c4_in = [drm(f"c4_in{j}", [2 * D, 512], F32) for j in range(2)]
    c4_out = [drm(f"c4_out{j}", [D, 512], F32) for j in range(2)]

    ident_bf_d = nc.inline_tensor(np.eye(128, dtype=bfd), "ident_bf")
    ident_f_d = nc.inline_tensor(np.eye(128, dtype=np.float32), "ident_f")
    rst_np = np.ones((H, TO), np.float32)
    rst_np[:, ::CH] = 0.0
    rst_d = nc.inline_tensor(rst_np, "rst_a")
    tril_d = nc.inline_tensor(np.triu(np.ones((CH, CH), np.float32)), "tril")

    blk_cm = nc.Block()
    blk = blk_cm.__enter__()
    s = Bld(nc, blk)

    def dbg(key, ap, shape, dt=F32):
        if key in debug_keys:
            t = nc.dram_tensor("dbg_" + key, shape, dt, kind="ExternalOutput")

            def em(sy, t=t, ap=ap):
                for n in SEMS:
                    s.w(sy, n)
                s.inc(sy.dma_start(out=t[:], in_=ap), "od", 16)
            blk.sync(em)

    def fin():
        def sy(sx):
            s.w(sx, "vs")
            s.w(sx, "ss")
            s.w(sx, "ts")
            for i in range(DG):
                s.inc(sx.dma_start(out=out[128 * i:128 * (i + 1), :], in_=resid[i][:]), "od", 16)
            s.w(sx, "od")
        blk.sync(sy)
        blk_cm.__exit__(None, None, None)
        return nc

    resid = [s.sb(f"resid{i}", [128, TO], F32) for i in range(DG)]
    identt = s.sb("identt", [128, 128], BF16)
    identf = s.sb("identf", [128, 128], F32)
    ones1 = s.sb("ones1", [1, 128], BF16)
    ones1f = s.sb("ones1f", [1, 128], F32)
    onescol = s.sb("onescol", [128, 1], F32)
    eps_t = s.sb("eps_t", [128, 1], F32)
    one_t = s.sb("one_t", [128, 1], F32)
    cab = s.sb("cab", [128, EG], F32)
    cbb = s.sb("cbb", [128, EG], F32)
    caw = s.sb("caw", [128, 4 * EG], F32)
    cbw = s.sb("cbw", [128, 4 * EG], F32)
    Aa_t = s.sb("Aa_t", [H, 1], F32)
    gscale = s.sb("gscale", [128, DG], F32)
    fbeta = s.sb("fbeta", [128, DG], F32)
    flnw = s.sb("flnw", [128, DG], F32)
    flnb = s.sb("flnb", [128, DG], F32)
    nrmw = s.sb("nrmw", [128, DG], F32)
    wdcol = s.sb("wdcol", [128, EGL], F32)
    Ab_t = s.sb("Ab_t", [128, NB * EGL], F32)
    msend_t = s.sb("msend_t", [128, 1], F32)
    rrecv_t = s.sb("rrecv_t", [128, 1], F32)
    selr0_t = s.sb("selr0_t", [128, 1], U8)
    onescol_bf = s.sb("onescol_bf", [128, 1], BF16)

    def p0_ld(sy):
        for i in range(DG):
            s.inc(sy.dma_start(out=resid[i][:], in_=xT[128 * i:128 * (i + 1), :]), "ld", 16)
        s.inc(sy.dma_start(out=identt[:], in_=ident_bf_d[:]), "ld", 16)
        s.inc(sy.dma_start(out=identf[:], in_=ident_f_d[:]), "ld", 16)
        s.inc(sy.dma_start(out=Aa_t[:], in_=Alog_a[:]), "ld", 16)
        s.inc(sy.dma_start(out=msend_t[:], in_=m_send[:]), "ld", 16)
        s.inc(sy.dma_start(out=rrecv_t[:], in_=r_recv[:]), "ld", 16)
        s.inc(sy.dma_start(out=selr0_t[:], in_=sel_r0[:]), "ld", 16)
        for eg in range(EG):
            s.inc(sy.dma_start(out=caw[:, 4 * eg:4 * eg + 4], in_=conv_a_w[128 * eg:128 * (eg + 1), :]), "ld", 16)
            s.inc(sy.dma_start(out=cbw[:, 4 * eg:4 * eg + 4], in_=conv_b_w[128 * eg:128 * (eg + 1), :]), "ld", 16)
            s.inc(sy.dma_start(out=cab[:, eg:eg + 1], in_=conv_a_b[128 * eg:128 * (eg + 1), :]), "ld", 16)
            s.inc(sy.dma_start(out=cbb[:, eg:eg + 1], in_=conv_b_b[128 * eg:128 * (eg + 1), :]), "ld", 16)
        for i in range(DG):
            s.inc(sy.dma_start(out=gscale[:, i:i + 1], in_=ffn_gamma[128 * i:128 * (i + 1), :]), "ld", 16)
            s.inc(sy.dma_start(out=fbeta[:, i:i + 1], in_=ffn_beta[128 * i:128 * (i + 1), :]), "ld", 16)
            s.inc(sy.dma_start(out=flnw[:, i:i + 1], in_=ffn_lnw[128 * i:128 * (i + 1), :]), "ld", 16)
            s.inc(sy.dma_start(out=flnb[:, i:i + 1], in_=ffn_lnb[128 * i:128 * (i + 1), :]), "ld", 16)
            s.inc(sy.dma_start(out=nrmw[:, i:i + 1], in_=norm_w[128 * i:128 * (i + 1), :]), "ld", 16)
        for eg in range(EGL):
            s.inc(sy.dma_start(out=wdcol[:, eg:eg + 1], in_=W_D_col[128 * eg:128 * (eg + 1), :]), "ld", 16)
            s.inc(sy.dma_start(out=Ab_t[:, NB * eg:NB * (eg + 1)], in_=Alog_b[128 * eg:128 * (eg + 1), :]), "ld", 16)
    blk.sync(p0_ld)

    def p0_ve(v):
        s.w(v, "ld")
        s.inc(v.memset(ones1[:], 1.0), "vs", 1)
        s.inc(v.memset(ones1f[:], 1.0), "vs", 1)
        s.inc(v.memset(onescol[:], 1.0), "vs", 1)
        s.inc(v.memset(onescol_bf[:], 1.0), "vs", 1)
        s.inc(v.memset(eps_t[:], 1e-5), "vs", 1)
        s.inc(v.memset(one_t[:], 1.0), "vs", 1)
    blk.vector(p0_ve)

    def p0_sc(sc):
        s.w(sc, "ld")
        s.inc(sc.activation(Aa_t[:], Aa_t[:], AF.Exp), "ss", 1)
        s.inc(sc.activation(Ab_t[:], Ab_t[:], AF.Exp), "ss", 1)
    blk.scalar(p0_sc)

    def p0_ve2(v):
        s.w(v, "ss")
        s.inc(v.tensor_scalar(Aa_t[:], Aa_t[:], -1.0, None, ALU.mult), "vs", 1)
        s.inc(v.tensor_scalar(Ab_t[:], Ab_t[:], -1.0, None, ALU.mult), "vs", 1)
    blk.vector(p0_ve2)

    def expand_f32(tag, row_ap, wd, out_ap_fn):
        pse = s.ps(f"{tag}_pse", [128, 512], F32)
        nst = (wd + 511) // 512
        for st in range(nst):
            a = st * 512
            bw = min(wd, a + 512) - a

            def te(t, a=a, bw=bw):
                s.w(t, "vs")
                s.w(t, "ss")
                s.inc(t.matmul(pse[:, 0:bw], ones1f[:], row_ap[0:1, a:a + bw], start=True, stop=True), "ts", 1)
            blk.tensor(te)

            def scv(sc, a=a, bw=bw):
                s.w(sc, "ts")
                s.inc(sc.copy(out_ap_fn(a, bw), pse[:, 0:bw]), "ss", 1)
            blk.scalar(scv)

    def ln_stats(tag, tiles, tc, rstd_row, nmr_row, bf=False):
        srow = s.sb(f"{tag}_srow", [1, tc], F32)
        sqt = s.sb(f"{tag}_sqt", [128, 512], F32)
        ps_s = s.ps(f"{tag}_pss", [1, 512], F32)
        ps_q = s.ps(f"{tag}_psq", [1, 512], F32)
        oc = onescol_bf if bf else onescol
        nst = (tc + 511) // 512
        for st in range(nst):
            c0 = st * 512
            wd = min(tc, c0 + 512) - c0
            for pg in range(DG):
                def scq(sc, pg=pg, c0=c0, wd=wd):
                    s.w(sc, "vs")
                    s.w(sc, "ld")
                    if pg > 0:
                        s.w(sc, "ts")
                    s.inc(sc.activation(sqt[:, 0:wd], tiles[pg][:, c0:c0 + wd], AF.Square), "ss", 1)
                blk.scalar(scq)

                def te(t, pg=pg, c0=c0, wd=wd):
                    s.w(t, "ss")
                    s.inc(t.matmul(ps_s[:, 0:wd], oc[:], tiles[pg][:, c0:c0 + wd],
                                   start=(pg == 0), stop=(pg == DG - 1)), "ts", 1)
                    s.inc(t.matmul(ps_q[:, 0:wd], onescol[:], sqt[:, 0:wd],
                                   start=(pg == 0), stop=(pg == DG - 1)), "ts", 1)
                blk.tensor(te)

            def ve(v, c0=c0, wd=wd):
                s.w(v, "ts")
                s.inc(v.tensor_scalar(srow[:, c0:c0 + wd], ps_s[:, 0:wd], 1.0 / D, None, ALU.mult), "vs", 1)
                s.inc(v.tensor_scalar(rstd_row[:, c0:c0 + wd], ps_q[:, 0:wd], 1.0 / D, None, ALU.mult), "vs", 1)
            blk.vector(ve)

        def ve3(v):
            s.inc(v.tensor_tensor(nmr_row[:, 0:tc], srow[:], srow[:], ALU.mult), "vs", 1)
            s.inc(v.tensor_tensor(rstd_row[:, 0:tc], rstd_row[:, 0:tc], nmr_row[:, 0:tc], ALU.subtract), "vs", 1)
        blk.vector(ve3)

        def sc2(sc):
            s.w(sc, "vs")
            s.inc(sc.activation(rstd_row[:, 0:tc], rstd_row[:, 0:tc], AF.Ln, bias=eps_t[0:1, 0:1]), "ss", 1)
            s.inc(sc.activation(rstd_row[:, 0:tc], rstd_row[:, 0:tc], AF.Exp, scale=-0.5), "ss", 1)
        blk.scalar(sc2)

        def ve4(v):
            s.w(v, "ss")
            s.inc(v.tensor_tensor(nmr_row[:, 0:tc], srow[:], rstd_row[:, 0:tc], ALU.mult), "vs", 1)
            s.inc(v.tensor_scalar(nmr_row[:, 0:tc], nmr_row[:, 0:tc], -1.0, None, ALU.mult), "vs", 1)
        blk.vector(ve4)

    s.push()
    dt2 = s.sb("dt2", [H, TO], F32)
    cs = s.sb("cs", [H, TO], F32)
    B_fm = s.sb("B_fm", [NA, TO], BF16)
    C_fm = s.sb("C_fm", [NA, TO], BF16)
    s.push()
    xc = [s.sb(f"xc{eg}", [128, TO], BF16) for eg in range(EG)]
    s.push()
    ln1 = [s.sb(f"ln1_{i}", [128, WARM + TO], BF16) for i in range(DG)]
    s.push()
    xw = [s.sb(f"xw{i}", [128, WARM], BF16) for i in range(DG)]

    def ldxw(g):
        for i in range(DG):
            s.inc(g.dma_start(out=xw[i][:], in_=xwT[128 * i:128 * (i + 1), :]), "gd", 16)
    blk.gpsimd(ldxw)
    r1 = s.sb("r1", [1, TO], F32)
    n1 = s.sb("n1", [1, TO], F32)
    r1w = s.sb("r1w", [1, WARM], F32)
    n1w = s.sb("n1w", [1, WARM], F32)

    def wtg(sc):
        s.w(sc, "gd")
    blk.scalar(wtg)
    ln_stats("l1o", resid, TO, r1, n1)
    ln_stats("l1w", xw, WARM, r1w, n1w, bf=True)
    tmpa = s.sb("tmpa", [128, 512], F32)
    psEr = s.ps("psEr", [128, 512], F32)
    psEn = s.ps("psEn", [128, 512], F32)
    for st in range(2):
        c0 = st * 512

        def tex(t, c0=c0, st=st):
            s.w(t, "vs")
            s.w(t, "ss")
            if st > 0:
                s.w(t, "vs")
            s.inc(t.matmul(psEr[:], ones1f[:], r1[0:1, c0:c0 + 512], start=True, stop=True), "ts", 1)
            s.inc(t.matmul(psEn[:], ones1f[:], n1[0:1, c0:c0 + 512], start=True, stop=True), "ts", 1)
        blk.tensor(tex)
        for pg in range(DG):
            def ve(v, pg=pg, c0=c0):
                s.w(v, "ts")
                s.inc(v.tensor_tensor(tmpa[:], resid[pg][:, c0:c0 + 512], psEr[:], ALU.mult), "vs", 1)
                s.inc(v.tensor_tensor(ln1[pg][:, WARM + c0:WARM + c0 + 512], tmpa[:], psEn[:], ALU.add), "vs", 1)
            blk.vector(ve)

    def tex2(t):
        s.w(t, "vs")
        s.inc(t.matmul(psEr[:, 0:WARM], ones1f[:], r1w[:], start=True, stop=True), "ts", 1)
        s.inc(t.matmul(psEn[:, 0:WARM], ones1f[:], n1w[:], start=True, stop=True), "ts", 1)
    blk.tensor(tex2)
    for pg in range(DG):
        def ve(v, pg=pg):
            s.w(v, "ts")
            s.inc(v.tensor_tensor(tmpa[:, 0:WARM], xw[pg][:], psEr[:, 0:WARM], ALU.mult), "vs", 1)
            s.inc(v.tensor_tensor(ln1[pg][:, 0:WARM], tmpa[:, 0:WARM], psEn[:, 0:WARM], ALU.add), "vs", 1)
        blk.vector(ve)
    s.pop()
    dbg("ln1", ln1[0][:, :], [128, WARM + TO], BF16)

    s.push()
    wbufB = [s.sb(f"wbufB{i}", [128, 16 * 128], BF16) for i in range(2)]
    ztile = [s.sb(f"ztile{i}", [128, TO], BF16) for i in range(2)]
    psB = [s.ps(f"psB{i}", [128, 512], F32) for i in range(4)]
    bankB = [0] * 4
    wfree = [0, 0]
    zfree = [0, 0]
    for mz in range(32):
        mo = 32 + mz
        wb = wbufB[mz % 2]
        zt = ztile[mz % 2]

        def gp(g, mo=mo, wb=wb, mz=mz):
            s.w(g, "ts", wfree[mz % 2])
            s.inc(g.dma_start(out=wb[:], in_=W_inprojT[128 * mo:128 * (mo + 1), :]), "gd", 16)
        blk.gpsimd(gp)
        bids = [(2 * mz) % 4, (2 * mz + 1) % 4]

        def te(t, mz=mz, wb=wb, bids=bids):
            s.w(t, "gd")
            if mz == 0:
                s.w(t, "vs")
            for bi in bids:
                s.w(t, "ss", bankB[bi])
            for k in range(DG):
                for ti in range(2):
                    s.inc(t.matmul(psB[bids[ti]][:], wb[:, 128 * k:128 * (k + 1)],
                                   ln1[k][:, WARM + 512 * ti:WARM + 512 * (ti + 1)],
                                   start=(k == 0), stop=(k == DG - 1)), "ts", 1)
            wfree[mz % 2] = s.c["ts"]
        blk.tensor(te)

        def sc(scx, mz=mz, bids=bids, zt=zt):
            s.w(scx, "ts")
            s.w(scx, "od", zfree[mz % 2])
            for ti in range(2):
                s.inc(scx.activation(zt[:, 512 * ti:512 * (ti + 1)], psB[bids[ti]][:], AF.Silu), "ss", 1)
            for bi in bids:
                bankB[bi] = s.c["ss"]
        blk.scalar(sc)

        def sy(sx, mz=mz, zt=zt):
            s.w(sx, "ss")
            s.inc(sx.dma_start(out=z_spill[128 * mz:128 * (mz + 1), :], in_=zt[:]), "od", 16)
            zfree[mz % 2] = s.c["od"]
        blk.sync(sy)
    s.pop()

    s.push()
    wbufX = [s.sb(f"wbufX{i}", [128, 16 * 128], BF16) for i in range(2)]
    xbt = [s.sb(f"xbt{i}", [128, WARM + TO], BF16) for i in range(2)]
    cacc = s.sb("cacc", [128, TO], F32)
    psXx = [s.ps(f"psXx{i}", [128, 512], F32) for i in range(6)]
    bankX = [0] * 6
    wfreeX = [0, 0]
    xbfree = [0, 0]
    tilesB2 = [(0, WARM), (WARM, 512), (WARM + 512, 512)]
    for eg in range(EG):
        wb = wbufX[eg % 2]
        xbe = xbt[eg % 2]

        def gp(g, eg=eg, wb=wb):
            s.w(g, "ts", wfreeX[eg % 2])
            s.inc(g.dma_start(out=wb[:], in_=W_inprojT[128 * eg:128 * (eg + 1), :]), "gd", 16)
        blk.gpsimd(gp)
        bids = [(3 * eg) % 6, (3 * eg + 1) % 6, (3 * eg + 2) % 6]

        def te(t, eg=eg, wb=wb, bids=bids):
            s.w(t, "gd")
            for bi in bids:
                s.w(t, "ss", bankX[bi])
            for k in range(DG):
                for ti, (c0, wd) in enumerate(tilesB2):
                    s.inc(t.matmul(psXx[bids[ti]][:, 0:wd], wb[:, 128 * k:128 * (k + 1)],
                                   ln1[k][:, c0:c0 + wd],
                                   start=(k == 0), stop=(k == DG - 1)), "ts", 1)
            wfreeX[eg % 2] = s.c["ts"]
        blk.tensor(te)

        def sc(scx, eg=eg, bids=bids, xbe=xbe):
            s.w(scx, "ts")
            s.w(scx, "vs", xbfree[eg % 2])
            for ti, (c0, wd) in enumerate(tilesB2):
                s.inc(scx.copy(xbe[:, c0:c0 + wd], psXx[bids[ti]][:, 0:wd]), "ss", 1)
            for bi in bids:
                bankX[bi] = s.c["ss"]
        blk.scalar(sc)

        def ve(v, eg=eg, xbe=xbe):
            s.w(v, "ss")
            s.inc(v.tensor_scalar(cacc[:], xbe[:, WARM - 3:WARM - 3 + TO],
                                  caw[:, 4 * eg:4 * eg + 1], None, ALU.mult), "vs", 1)
            for k in range(1, 4):
                s.inc(v.scalar_tensor_tensor(cacc[:], xbe[:, WARM - 3 + k:WARM - 3 + k + TO],
                                             caw[:, 4 * eg + k:4 * eg + k + 1], cacc[:],
                                             ALU.mult, ALU.add), "vs", 1)
            xbfree[eg % 2] = s.c["vs"]
        blk.vector(ve)

        def sc2(scx, eg=eg):
            s.w(scx, "vs")
            s.inc(scx.activation(xc[eg][:], cacc[:], AF.Silu, bias=cab[:, eg:eg + 1]), "ss", 1)
        blk.scalar(sc2)
    s.pop()
    s.pop()
    dbg("xc0", xc[0][:, :], [128, TO], BF16)
    if upto == "B":
        return fin()

    s.push()
    wbufD = [s.sb(f"wbufD{i}", [128, 32 * 128], BF16) for i in range(2)]
    xtile = [s.sb(f"xtile{i}", [128, TO], BF16) for i in range(2)]
    s.push()
    psD = [s.ps(f"psD{i}", [128, 512], F32) for i in range(4)]
    bankD = [0] * 4
    wfreeD = [0, 0]
    xtfree = [0, 0]
    for mo in range(32):
        wb = wbufD[mo % 2]
        xte = xtile[mo % 2]

        def gp(g, mo=mo, wb=wb):
            s.w(g, "ts", wfreeD[mo % 2])
            s.inc(g.dma_start(out=wb[:], in_=W_xprojT[128 * mo:128 * (mo + 1), :]), "gd", 16)
        blk.gpsimd(gp)
        bids = [(2 * mo) % 4, (2 * mo + 1) % 4]

        def te(t, mo=mo, wb=wb, bids=bids):
            s.w(t, "gd")
            if mo == 0:
                s.w(t, "ss")
            for bi in bids:
                s.w(t, "ss", bankD[bi])
            for k in range(EG):
                for ti in range(2):
                    s.inc(t.matmul(psD[bids[ti]][:], wb[:, 128 * k:128 * (k + 1)],
                                   xc[k][:, 512 * ti:512 * (ti + 1)],
                                   start=(k == 0), stop=(k == EG - 1)), "ts", 1)
            wfreeD[mo % 2] = s.c["ts"]
        blk.tensor(te)

        def sc(scx, mo=mo, bids=bids, xte=xte):
            s.w(scx, "ts")
            s.w(scx, "od", xtfree[mo % 2])
            for ti in range(2):
                s.inc(scx.copy(xte[:, 512 * ti:512 * (ti + 1)], psD[bids[ti]][:]), "ss", 1)
            for bi in bids:
                bankD[bi] = s.c["ss"]
        blk.scalar(sc)

        def sy(sx, mo=mo, xte=xte):
            s.w(sx, "ss")
            s.inc(sx.dma_start(out=X_spill[128 * mo:128 * (mo + 1), :], in_=xte[:]), "od", 16)
            xtfree[mo % 2] = s.c["od"]
        blk.sync(sy)

    s.pop()
    wdt = s.sb("wdt", [128, 32 * 64], BF16)
    wBx = s.sb("wBx", [128, 32 * 128], BF16)
    wCx = s.sb("wCx", [128, 32 * 128], BF16)
    dtp = s.sb("dtp", [H, TO], BF16)

    def gp2(g):
        s.w(g, "ts", wfreeD[0])
        s.w(g, "ts", wfreeD[1])
        s.inc(g.dma_start(out=wdt[:].rearrange("p (k m) -> p k m", k=32),
                          in_=W_xptail[:, 0:64].rearrange("(k p) m -> p k m", p=128)), "gd", 16)
        s.inc(g.dma_start(out=wBx[:].rearrange("p (k m) -> p k m", k=32),
                          in_=W_xptail[:, 64:192].rearrange("(k p) m -> p k m", p=128)), "gd", 16)
        s.inc(g.dma_start(out=wCx[:].rearrange("p (k m) -> p k m", k=32),
                          in_=W_xptail[:, 192:320].rearrange("(k p) m -> p k m", p=128)), "gd", 16)
    blk.gpsimd(gp2)
    s.push()
    psdt = s.ps("psdt", [64, 512], F32)
    psBb = s.ps("psBb", [128, 512], F32)
    psCc = s.ps("psCc", [128, 512], F32)
    for ti in range(2):
        def te(t, ti=ti):
            s.w(t, "gd")
            s.w(t, "ss")
            for k in range(EG):
                s.inc(t.matmul(psdt[:], wdt[:, 64 * k:64 * (k + 1)], xc[k][:, 512 * ti:512 * (ti + 1)],
                               start=(k == 0), stop=(k == EG - 1)), "ts", 1)
            for k in range(EG):
                s.inc(t.matmul(psBb[:], wBx[:, 128 * k:128 * (k + 1)], xc[k][:, 512 * ti:512 * (ti + 1)],
                               start=(k == 0), stop=(k == EG - 1)), "ts", 1)
            for k in range(EG):
                s.inc(t.matmul(psCc[:], wCx[:, 128 * k:128 * (k + 1)], xc[k][:, 512 * ti:512 * (ti + 1)],
                               start=(k == 0), stop=(k == EG - 1)), "ts", 1)
        blk.tensor(te)

        def sc(scx, ti=ti):
            s.w(scx, "ts")
            s.inc(scx.copy(dtp[:, 512 * ti:512 * (ti + 1)], psdt[:]), "ss", 1)
            s.inc(scx.copy(B_fm[:, 512 * ti:512 * (ti + 1)], psBb[:]), "ss", 1)
            s.inc(scx.copy(C_fm[:, 512 * ti:512 * (ti + 1)], psCc[:]), "ss", 1)
        blk.scalar(sc)

    s.pop()
    wdtp = s.sb("wdtp", [H, H], BF16)
    dtbcol = s.sb("dtbcol", [H, 1], F32)

    def gp3(g):
        s.inc(g.dma_start(out=wdtp[:], in_=W_dtT[:]), "gd", 16)
    blk.gpsimd(gp3)

    def sy2(sx):
        s.inc(sx.dma_start(out=dtbcol[:], in_=dt_b_col[:]), "ld", 16)
    blk.sync(sy2)
    psd2 = s.ps("psd2", [64, 512], F32)
    for ti in range(2):
        def te(t, ti=ti):
            s.w(t, "gd")
            s.w(t, "ss")
            if ti > 0:
                s.w(t, "vs")
            s.inc(t.matmul(psd2[:], wdtp[:], dtp[:, 512 * ti:512 * (ti + 1)], start=True, stop=True), "ts", 1)
        blk.tensor(te)

        def ve(v, ti=ti):
            s.w(v, "ts")
            s.w(v, "ld")
            s.inc(v.tensor_scalar(dt2[:, 512 * ti:512 * (ti + 1)], psd2[:], dtbcol[:, 0:1],
                                  None, ALU.add), "vs", 1)
        blk.vector(ve)

        def sc(scx, ti=ti):
            s.w(scx, "vs")
            s.inc(scx.activation(dt2[:, 512 * ti:512 * (ti + 1)], dt2[:, 512 * ti:512 * (ti + 1)], AF.Exp), "ss", 1)
            s.inc(scx.activation(dt2[:, 512 * ti:512 * (ti + 1)], dt2[:, 512 * ti:512 * (ti + 1)],
                                 AF.Ln, bias=one_t[0:64, 0:1]), "ss", 1)
        blk.scalar(sc)

    A_sc = s.sb("A_sc", [H, TO], F32)
    rstt = s.sb("rstt", [H, TO], F32)

    def sy3(sx):
        s.inc(sx.dma_start(out=rstt[:], in_=rst_d[:]), "ld", 16)
    blk.sync(sy3)

    def ve5(v):
        s.w(v, "ss")
        s.w(v, "ld")
        s.inc(v.tensor_scalar(A_sc[:], dt2[:], Aa_t[:, 0:1], None, ALU.mult), "vs", 1)
        s.inc(v.tensor_tensor_scan(cs[:], rstt[:], A_sc[:], 0.0, ALU.mult, ALU.add), "vs", 1)
    blk.vector(ve5)

    def sycs(sx):
        s.w(sx, "vs")
        s.inc(sx.dma_start(out=cs_dram[:], in_=cs[:]), "od", 16)
    blk.sync(sycs)
    s.pop()
    s.pop()
    dbg("dt2", dt2[:, :], [H, TO], F32)
    dbg("cs", cs[:, :], [H, TO], F32)
    dbg("B_fm", B_fm[:, :], [NA, TO], BF16)
    if upto == "D":
        return fin()

    s.push()
    csT = s.sb("csT", [128, H * NCH], F32)
    dtT = s.sb("dtT", [128, H * NCH], F32)
    ecT = s.sb("ecT", [128, H * NCH], F32)
    S_in = s.sb("S_in", [NA, E], BF16)
    s.push()
    Scum = s.sb("Scum", [NA, E], F32)
    Scum_bf = s.sb("Scum_bf", [NA, E], BF16)
    exptotT = s.sb("exptotT", [128, H * NCH], F32)
    totT = s.sb("totT", [128, H], F32)
    wcol_all = s.sb("wcol_all", [128, H], F32)
    ecrow = s.sb("ecrow", [1, H], F32)
    etrow = s.sb("etrow", [1, H], F32)
    totrow = s.sb("totrow", [1, H], F32)
    ecrow_exp = s.sb("ecrow_exp", [1, H], F32)
    Xld = s.sb("Xld", [128, 32 * 128], BF16)
    Xtm = s.sb("Xtm", [128, E], BF16)
    Btm = s.sb("Btm", [128, NA], BF16)
    Bd_w = s.sb("Bd_w", [128, 16 * 128], BF16)
    pstf = s.ps("pstf", [128, 512], F32)
    pstb = s.ps("pstb", [128, 1024], BF16)
    psS = s.ps("psS", [128, 512], F32)
    psS2 = s.ps("psS2", [128, 512], F32)
    pse2 = s.ps("pse2", [128, 192], F32)

    def ve6(v):
        s.inc(v.memset(Scum[:], 0.0), "vs", 1)
        s.inc(v.memset(Scum_bf[:], 0.0), "vs", 1)
        s.inc(v.memset(ecrow[:], 0.0), "vs", 1)
    blk.vector(ve6)

    bd_free = [0]
    psS_free = [0]
    for c in range(NCH):
        c0 = c * CH

        def te(t, c=c, c0=c0):
            s.w(t, "vs")
            s.w(t, "ss")
            s.inc(t.transpose(pstf[:, 0:64], cs[0:H, c0:c0 + CH], identf[0:64, 0:64]), "ts", 1)
            s.inc(t.transpose(pstf[:, 64:128], dt2[0:H, c0:c0 + CH], identf[0:64, 0:64]), "ts", 1)
        blk.tensor(te)

        def syt(sx, c=c, c0=c0):
            s.w(sx, "vs")
            if c == 0:
                s.w(sx, "od")
            s.inc(sx.dma_start(out=totrow[:],
                               in_=cs_dram.ap().rearrange("h t -> t h")[c0 + CH - 1:c0 + CH, 0:H],
                               allow_slow_non_contiguous=True), "ld", 16)
        blk.sync(syt)

        def sc(scx, c=c):
            s.w(scx, "ts")
            s.w(scx, "vs")
            s.w(scx, "ld")
            s.inc(scx.copy(csT[:, H * c:H * (c + 1)], pstf[:, 0:64]), "ss", 1)
            s.inc(scx.copy(dtT[:, H * c:H * (c + 1)], pstf[:, 64:128]), "ss", 1)
            s.inc(scx.activation(etrow[:], totrow[:], AF.Exp), "ss", 1)
            s.inc(scx.activation(ecrow_exp[:], ecrow[:], AF.Exp), "ss", 1)
        blk.scalar(sc)

        def te2(t, c=c):
            s.w(t, "ss")
            s.inc(t.matmul(pse2[:, 0:64], ones1f[:], etrow[:], start=True, stop=True), "ts", 1)
            s.inc(t.matmul(pse2[:, 64:128], ones1f[:], ecrow_exp[:], start=True, stop=True), "ts", 1)
            s.inc(t.matmul(pse2[:, 128:192], ones1f[:], totrow[:], start=True, stop=True), "ts", 1)
        blk.tensor(te2)

        def sc3(scx, c=c):
            s.w(scx, "ts")
            s.inc(scx.copy(exptotT[:, H * c:H * (c + 1)], pse2[:, 0:64]), "ss", 1)
            s.inc(scx.copy(ecT[:, H * c:H * (c + 1)], pse2[:, 64:128]), "ss", 1)
            s.inc(scx.copy(totT[:], pse2[:, 128:192]), "ss", 1)
        blk.scalar(sc3)

        def ve7(v, c=c):
            s.w(v, "ss")
            s.inc(v.tensor_tensor(ecrow[:], ecrow[:], totrow[:], ALU.add), "vs", 1)
            s.inc(v.tensor_tensor(wcol_all[:], totT[:], csT[:, H * c:H * (c + 1)], ALU.subtract), "vs", 1)
        blk.vector(ve7)

        def sc3b(scx, c=c):
            s.w(scx, "vs")
            s.inc(scx.activation(wcol_all[:], wcol_all[:], AF.Exp), "ss", 1)
        blk.scalar(sc3b)

        def ve7b(v, c=c):
            s.w(v, "ss")
            s.inc(v.tensor_tensor(wcol_all[:], wcol_all[:], dtT[:, H * c:H * (c + 1)], ALU.mult), "vs", 1)
        blk.vector(ve7b)

        def gp(g, c=c, c0=c0):
            s.w(g, "vs")
            s.inc(g.dma_start(out=Xld[:].rearrange("p (k m) -> p k m", k=32),
                              in_=X_spill[:, c0:c0 + CH].rearrange("(k p) m -> p k m", p=128)), "gd", 16)
        blk.gpsimd(gp)

        for gb in range(4):
            def te3(t, c=c, gb=gb):
                if gb == 0:
                    s.w(t, "gd")
                s.w(t, "vs")
                for j in range(8):
                    s.inc(t.transpose(pstb[:, 128 * j:128 * (j + 1)],
                                      Xld[:, 128 * (8 * gb + j):128 * (8 * gb + j + 1)], identt[:]), "ts", 1)
            blk.tensor(te3)

            def ve8(v, gb=gb):
                s.w(v, "ts")
                s.inc(v.tensor_copy(Xtm[:, 1024 * gb:1024 * (gb + 1)], pstb[:]), "vs", 1)
            blk.vector(ve8)

        def te4(t, c=c, c0=c0):
            s.w(t, "vs")
            s.inc(t.transpose(pstb[:, 0:128], B_fm[:, c0:c0 + CH], identt[:]), "ts", 1)
        blk.tensor(te4)

        def ve9(v, c=c):
            s.w(v, "ts")
            s.inc(v.tensor_copy(Btm[:], pstb[:, 0:128]), "vs", 1)
        blk.vector(ve9)

        def sy4(sx, c=c):
            s.w(sx, "vs")
            s.w(sx, "ss")
            s.inc(sx.dma_start(out=snap_spill[NA * c:NA * (c + 1), :], in_=Scum_bf[:]), "od", 16)
        blk.sync(sy4)
        snap_od = s.c["od"]

        for hb in range(4):
            def ve10(v, c=c, hb=hb):
                s.w(v, "ts", bd_free[0])
                for j in range(16):
                    h = 16 * hb + j
                    s.inc(v.tensor_scalar(Bd_w[:, 128 * j:128 * (j + 1)], Btm[:],
                                          wcol_all[:, h:h + 1], None, ALU.mult), "vs", 1)
            blk.vector(ve10)

            def te5(t, c=c, hb=hb):
                s.w(t, "vs")
                s.w(t, "vs", psS_free[0])
                for j in range(16):
                    h = 16 * hb + j
                    s.inc(t.matmul(psS[:, 64 * j:64 * (j + 1)] if j < 8 else psS2[:, 64 * (j - 8):64 * (j - 7)],
                                   Bd_w[:, 128 * j:128 * (j + 1)], Xtm[:, 64 * h:64 * (h + 1)],
                                   start=True, stop=True), "ts", 1)
                bd_free[0] = s.c["ts"]
            blk.tensor(te5)

            def ve11(v, c=c, hb=hb):
                s.w(v, "ts")
                for j in range(16):
                    h = 16 * hb + j
                    src = psS[:, 64 * j:64 * (j + 1)] if j < 8 else psS2[:, 64 * (j - 8):64 * (j - 7)]
                    s.inc(v.scalar_tensor_tensor(Scum[:, 64 * h:64 * (h + 1)], Scum[:, 64 * h:64 * (h + 1)],
                                                 exptotT[:, H * c + h:H * c + h + 1],
                                                 src, ALU.mult, ALU.add), "vs", 1)
                psS_free[0] = s.c["vs"]
            blk.vector(ve11)

        def sc4(scx, c=c, snap_od=snap_od):
            s.w(scx, "vs")
            s.w(scx, "od", snap_od)
            s.inc(scx.copy(Scum_bf[:], Scum[:]), "ss", 1)
        blk.scalar(sc4)

    c2s = s.sb("c2s", [NA, E], BF16)

    def ve12(v):
        s.w(v, "ss")
        s.inc(v.tensor_scalar(c2s[:], Scum_bf[:], msend_t[:, 0:1], None, ALU.mult), "vs", 1)
    blk.vector(ve12)

    def gp4(g):
        s.w(g, "vs")
        s.inc(g.dma_start(out=c2_in[:], in_=c2s[:]), "gd", 16)
        s.w(g, "gd")
        s.inc(g.collective_compute("AllGather", ALU.bypass, replica_groups=PAIRS,
                                   ins=[c2_in.ap().opt()], outs=[c2_out.ap().opt()]), "cc", 1)
        s.w(g, "cc")
        s.inc(g.dma_start(out=S_in[:], in_=c2_out[0:NA, :]), "gd", 16)
    blk.gpsimd(gp4)

    def ve13(v):
        s.w(v, "gd")
        s.inc(v.tensor_scalar(S_in[:], S_in[:], rrecv_t[:, 0:1], None, ALU.mult), "vs", 1)
    blk.vector(ve13)
    dbg("Scum", Scum[:, :], [NA, E], F32)
    dbg("S_in", S_in[:, :], [NA, E], BF16)
    s.pop()
    if upto == "E1":
        return fin()

    s.push()
    tril_t = s.sb("tril_t", [CH, CH], F32)
    Dbt = s.sb("Dbt", [128, E], BF16)
    Xld2 = s.sb("Xld2", [128, 32 * 128], BF16)
    zld2 = s.sb("zld2", [128, 32 * 128], BF16)
    Xtm2 = s.sb("Xtm2", [128, E], BF16)
    ztm2 = s.sb("ztm2", [128, E], BF16)
    snapc = s.sb("snapc", [NA, E], BF16)
    Ytm = s.sb("Ytm", [128, E], F32)
    ytmbf = s.sb("ytmbf", [128, E], BF16)
    ygc = s.sb("ygc", [128, 32 * 128], BF16)
    CBt = s.sb("CBt", [128, CH], F32)
    expcsT2 = s.sb("expcsT2", [128, H], F32)
    Lh_w = s.sb("Lh_w", [128, 16 * CH], F32)
    csrow_q = [s.sb(f"csrq{i}", [1, 16 * CH], F32) for i in range(2)]
    Mch_w = s.sb("Mch_w", [128, 16 * CH], BF16)
    seff_w = s.sb("seff_w", [NA, 16 * PDIM], BF16)
    psY = s.ps("psY", [128, 512], F32)
    psOf = s.ps("psOf", [128, 512], F32)
    psCB = s.ps("psCB", [128, 128], F32)
    psXPw = s.ps("psXPw", [128, 16 * CH], F32)
    pstb2 = s.ps("pstb2", [128, 1024], BF16)

    def sy5(sx):
        s.inc(sx.dma_start(out=tril_t[:], in_=tril_d[:]), "ld", 16)
    blk.sync(sy5)
    s.push()
    darow = s.sb("darow", [1, 512], BF16)
    for st in range(8):
        def gpd(g, st=st):
            if st >= 1:
                s.w(g, "ts")
            s.inc(g.dma_start(out=darow[:], in_=D_a_row[0:1, 512 * st:512 * (st + 1)]), "gd", 16)
        blk.gpsimd(gpd)

        def te6(t, st=st):
            s.w(t, "ld")
            s.w(t, "gd")
            if st >= 1:
                s.w(t, "ss")
            s.inc(t.matmul(psXPw[:, 0:512], ones1[:], darow[:], start=True, stop=True), "ts", 1)
        blk.tensor(te6)

        def sc5(scx, st=st):
            s.w(scx, "ts")
            s.inc(scx.copy(Dbt[:, 512 * st:512 * (st + 1)], psXPw[:, 0:512]), "ss", 1)
        blk.scalar(sc5)
    s.pop()

    psXP_free = [0]
    LM_free = [0]
    se_free = [0]
    psY_free = [0, 0]
    csq_free = [0, 0]
    for c in range(NCH):
        c0 = c * CH

        def gp5(g, c=c, c0=c0):
            s.w(g, "vs")
            s.w(g, "ts")
            s.inc(g.dma_start(out=Xld2[:].rearrange("p (k m) -> p k m", k=32),
                              in_=X_spill[:, c0:c0 + CH].rearrange("(k p) m -> p k m", p=128)), "gd", 16)
            s.inc(g.dma_start(out=zld2[:].rearrange("p (k m) -> p k m", k=32),
                              in_=z_spill[:, c0:c0 + CH].rearrange("(k p) m -> p k m", p=128)), "gd", 16)
            s.inc(g.dma_start(out=snapc[:], in_=snap_spill[NA * c:NA * (c + 1), :]), "gd", 16)
        blk.gpsimd(gp5)

        for gb in range(4):
            def te7(t, c=c, gb=gb):
                if gb == 0:
                    s.w(t, "gd")
                    s.w(t, "ss")
                s.w(t, "vs")
                for j in range(8):
                    s.inc(t.transpose(pstb2[:, 128 * j:128 * (j + 1)],
                                      Xld2[:, 128 * (8 * gb + j):128 * (8 * gb + j + 1)], identt[:]), "ts", 1)
            blk.tensor(te7)

            def ve15(v, gb=gb):
                s.w(v, "ts")
                s.inc(v.tensor_copy(Xtm2[:, 1024 * gb:1024 * (gb + 1)], pstb2[:]), "vs", 1)
            blk.vector(ve15)
        for gb in range(4):
            def te8(t, c=c, gb=gb):
                s.w(t, "vs")
                for j in range(8):
                    s.inc(t.transpose(pstb2[:, 128 * j:128 * (j + 1)],
                                      zld2[:, 128 * (8 * gb + j):128 * (8 * gb + j + 1)], identt[:]), "ts", 1)
            blk.tensor(te8)

            def ve16(v, gb=gb):
                s.w(v, "ts")
                s.inc(v.tensor_copy(ztm2[:, 1024 * gb:1024 * (gb + 1)], pstb2[:]), "vs", 1)
            blk.vector(ve16)

        def te9(t, c=c, c0=c0):
            s.w(t, "vs")
            s.inc(t.matmul(psCB[:], B_fm[:, c0:c0 + CH], C_fm[:, c0:c0 + CH], start=True, stop=True), "ts", 1)
        blk.tensor(te9)

        def ve17(v, c=c):
            s.w(v, "ts")
            s.w(v, "ld")
            s.inc(v.tensor_tensor(CBt[:], psCB[:], tril_t[:], ALU.mult), "vs", 1)
        blk.vector(ve17)

        def sc6(scx, c=c):
            s.w(scx, "vs")
            s.inc(scx.activation(expcsT2[:], csT[:, H * c:H * (c + 1)], AF.Exp), "ss", 1)
        blk.scalar(sc6)

        for hb in range(4):
            qi = hb % 2

            def syq(sx, c=c, c0=c0, hb=hb, qi=qi):
                s.w(sx, "ts", csq_free[qi])
                s.inc(sx.dma_start(out=csrow_q[qi].ap().rearrange("a (h l) -> a h l", h=16),
                                   in_=cs_dram[16 * hb:16 * (hb + 1), c0:c0 + CH]
                                   .rearrange("(a h) l -> a h l", a=1)), "ld", 16)
            blk.sync(syq)

            def te10(t, c=c, hb=hb, qi=qi):
                s.w(t, "vs", psXP_free[0])
                s.w(t, "ss")
                s.w(t, "ld")
                for j in range(16):
                    s.inc(t.matmul(psXPw[:, 128 * j:128 * (j + 1)], ones1f[:],
                                   csrow_q[qi][0:1, CH * j:CH * (j + 1)],
                                   start=True, stop=True), "ts", 1)
                csq_free[qi] = s.c["ts"]
            blk.tensor(te10)

            def ve19(v, c=c, hb=hb):
                s.w(v, "ts")
                s.w(v, "ss", LM_free[0])
                for j in range(16):
                    h = 16 * hb + j
                    s.inc(v.tensor_scalar(Lh_w[:, 128 * j:128 * (j + 1)], psXPw[:, 128 * j:128 * (j + 1)],
                                          csT[:, H * c + h:H * c + h + 1], 0.0,
                                          ALU.subtract, ALU.min), "vs", 1)
                psXP_free[0] = s.c["vs"]
            blk.vector(ve19)

            def sc7(scx, c=c, hb=hb):
                s.w(scx, "vs")
                s.inc(scx.activation(Lh_w[:], Lh_w[:], AF.Exp), "ss", 1)
                LM_free[0] = s.c["ss"]
            blk.scalar(sc7)

            def ve20(v, c=c, hb=hb):
                s.w(v, "ss")
                s.w(v, "ts", se_free[0])
                for j in range(16):
                    h = 16 * hb + j
                    s.inc(v.scalar_tensor_tensor(Mch_w[:, 128 * j:128 * (j + 1)], Lh_w[:, 128 * j:128 * (j + 1)],
                                                 dtT[:, H * c + h:H * c + h + 1],
                                                 CBt[:], ALU.mult, ALU.mult), "vs", 1)
                    s.inc(v.scalar_tensor_tensor(seff_w[:, 64 * j:64 * (j + 1)], S_in[:, 64 * h:64 * (h + 1)],
                                                 ecT[:, H * c + h:H * c + h + 1],
                                                 snapc[:, 64 * h:64 * (h + 1)], ALU.mult, ALU.add), "vs", 1)
            blk.vector(ve20)

            for wv in range(2):
                def te11(t, c=c, hb=hb, wv=wv, c0=c0):
                    s.w(t, "vs")
                    s.w(t, "vs", psY_free[wv])
                    for sl in range(8):
                        j = 8 * wv + sl
                        h = 16 * hb + j
                        s.inc(t.matmul(psY[:, 64 * sl:64 * (sl + 1)], Mch_w[:, 128 * j:128 * (j + 1)],
                                       Xtm2[:, 64 * h:64 * (h + 1)], start=True, stop=True), "ts", 1)
                        s.inc(t.matmul(psOf[:, 64 * sl:64 * (sl + 1)], C_fm[:, c0:c0 + CH],
                                       seff_w[:, 64 * j:64 * (j + 1)], start=True, stop=True), "ts", 1)
                    if wv == 1:
                        se_free[0] = s.c["ts"]
                blk.tensor(te11)

                def ve21(v, c=c, hb=hb, wv=wv):
                    s.w(v, "ts")
                    for sl in range(8):
                        j = 8 * wv + sl
                        h = 16 * hb + j
                        s.inc(v.tensor_scalar(Ytm[:, 64 * h:64 * (h + 1)], psOf[:, 64 * sl:64 * (sl + 1)],
                                              expcsT2[:, h:h + 1], None, ALU.mult), "vs", 1)
                        s.inc(v.tensor_tensor(Ytm[:, 64 * h:64 * (h + 1)], Ytm[:, 64 * h:64 * (h + 1)],
                                              psY[:, 64 * sl:64 * (sl + 1)], ALU.add), "vs", 1)
                        s.inc(v.scalar_tensor_tensor(Ytm[:, 64 * h:64 * (h + 1)], Xtm2[:, 64 * h:64 * (h + 1)],
                                                     Dbt[:, 64 * h:64 * h + 1], Ytm[:, 64 * h:64 * (h + 1)],
                                                     ALU.mult, ALU.add), "vs", 1)
                    psY_free[wv] = s.c["vs"]
                blk.vector(ve21)

        def ve22(v, c=c):
            s.inc(v.tensor_tensor(ytmbf[:], Ytm[:], ztm2[:], ALU.mult), "vs", 1)
        blk.vector(ve22)

        for gb in range(4):
            def te12(t, c=c, gb=gb):
                s.w(t, "vs")
                for j in range(8):
                    s.inc(t.transpose(pstb2[:, 128 * j:128 * (j + 1)],
                                      ytmbf[:, 128 * (8 * gb + j):128 * (8 * gb + j + 1)], identt[:]), "ts", 1)
            blk.tensor(te12)

            def ve23(v, gb=gb):
                s.w(v, "ts")
                s.inc(v.tensor_copy(ygc[:, 1024 * gb:1024 * (gb + 1)], pstb2[:]), "vs", 1)
            blk.vector(ve23)

        def sy6(sx, c=c, c0=c0):
            s.w(sx, "vs")
            s.inc(sx.dma_start(out=yg_spill[:, c0:c0 + CH].rearrange("(k p) m -> p k m", p=128),
                               in_=ygc[:].rearrange("p (k m) -> p k m", k=32)), "od", 16)
        blk.sync(sy6)
    s.pop()
    s.pop()
    s.pop()
    dbg("yg0", yg_spill[0:128, :], [128, TO], BF16)
    if upto == "E2":
        return fin()

    s.push()
    wbufF = s.sb("wbufF", [128, 32 * 1024], BF16)
    ygk = [s.sb(f"ygk{i}", [128, 512], BF16) for i in range(32)]
    psFb = [s.ps(f"psFb{i}", [128, 512], F32) for i in range(8)]
    yg_free = [0]
    psF_free = [0] * 8
    for dgw in range(2):
        def gpw(g, dgw=dgw):
            s.w(g, "ts")
            s.inc(g.dma_start(out=wbufF[:], in_=W_outprojT[128 * dgw:128 * (dgw + 1), :]), "gd", 16)
        blk.gpsimd(gpw)
        for tt in range(2):
            def gp6(g, tt=tt, dgw=dgw):
                s.w(g, "ts", yg_free[0])
                for k in range(EG):
                    s.inc(g.dma_start(out=ygk[k][:],
                                      in_=yg_spill[128 * k:128 * (k + 1), 512 * tt:512 * (tt + 1)]), "gd", 16)
            blk.gpsimd(gp6)
            for k in range(EG):
                def te13(t, tt=tt, k=k):
                    s.w(t, "gd")
                    if k == 0:
                        for bi in range(8):
                            s.w(t, "vs", psF_free[bi])
                    for dg8 in range(8):
                        s.inc(t.matmul(psFb[dg8][:], wbufF[:, 1024 * k + 128 * dg8:1024 * k + 128 * (dg8 + 1)],
                                       ygk[k][:], start=(k == 0), stop=(k == EG - 1)), "ts", 1)
                    yg_free[0] = s.c["ts"]
                blk.tensor(te13)

            for dg8 in range(8):
                def ve24(v, dgw=dgw, tt=tt, dg8=dg8):
                    s.w(v, "ts")
                    dg = dgw * 8 + dg8
                    s.inc(v.tensor_tensor(resid[dg][:, 512 * tt:512 * (tt + 1)],
                                          resid[dg][:, 512 * tt:512 * (tt + 1)],
                                          psFb[dg8][:], ALU.add), "vs", 1)
                    psF_free[dg8] = s.c["vs"]
                blk.vector(ve24)
    s.pop()
    dbg("resid1", resid[0][:, :], [128, TO], F32)
    if upto == "F":
        return fin()

    def ffn_phase(tag, first):
        s.push()
        ln2 = [s.sb(f"{tag}ln2_{i}", [128, TO], BF16) for i in range(DG)]
        r2 = s.sb(f"{tag}r2", [1, TO], F32)
        n2 = s.sb(f"{tag}n2", [1, TO], F32)
        r2b = s.sb(f"{tag}r2b", [128, TO], F32)
        n2b = s.sb(f"{tag}n2b", [128, TO], F32)
        tmpg = s.sb(f"{tag}tmpg", [128, TO], F32)
        sgw = s.sb(f"{tag}sgw", [128, 16 * 128], BF16)
        s.push()
        ln_stats(f"{tag}l2", resid, TO, r2, n2)
        expand_f32(f"{tag}r2x", r2, TO, lambda a, w: r2b[:, a:a + w])
        expand_f32(f"{tag}n2x", n2, TO, lambda a, w: n2b[:, a:a + w])
        for pg in range(DG):
            def ve(v, pg=pg):
                if pg == 0:
                    s.w(v, "ss")
                s.inc(v.tensor_tensor(tmpg[:], resid[pg][:], r2b[:], ALU.mult), "vs", 1)
                s.inc(v.tensor_tensor(tmpg[:], tmpg[:], n2b[:], ALU.add), "vs", 1)
                s.inc(v.scalar_tensor_tensor(ln2[pg][:], tmpg[:], flnw[:, pg:pg + 1],
                                             flnb[:, pg:pg + 1].broadcast_to([128, TO]),
                                             ALU.mult, ALU.add), "vs", 1)
            blk.vector(ve)

        if first:
            amax = s.sb("amax", [128, DG], F32)
            wraw = s.sb(f"{tag}wraw", [128, 16 * 128], BF16)
            sg_od = [0]
            for mo in range(DG):
                def gp(g, mo=mo):
                    s.w(g, "vs")
                    s.w(g, "ss")
                    s.inc(g.dma_start(out=wraw[:], in_=ffn_WT[128 * mo:128 * (mo + 1), :]), "gd", 16)
                blk.gpsimd(gp)

                def ve25(v, mo=mo):
                    s.w(v, "gd")
                    s.inc(v.tensor_reduce(amax[:, mo:mo + 1], wraw[:], mybir.AxisListType.X,
                                          ALU.max, apply_absolute_value=True), "vs", 1)
                blk.vector(ve25)

                def sc8(scx, mo=mo):
                    s.w(scx, "gd")
                    s.w(scx, "od", sg_od[0])
                    s.inc(scx.activation(sgw[:], wraw[:], AF.Sign), "ss", 1)
                blk.scalar(sc8)

                def sy7(sx, mo=mo):
                    s.w(sx, "ss")
                    s.inc(sx.dma_start(out=signW_spill[:, 2048 * mo:2048 * (mo + 1)],
                                       in_=sgw[:]), "od", 16)
                    sg_od[0] = s.c["od"]
                blk.sync(sy7)
            am1 = s.sb("am1", [128, 1], F32)
            amaxc = s.sb("amaxc", [128, DG], F32)
            amr = s.sb("amr", [1, 128], F32)
            amrc = s.sb("amrc", [1, 128], F32)
            ams = s.sb("ams", [1, 1], F32)
            psam = s.ps("psam", [128, 128], F32)

            def sc26(scx):
                s.w(scx, "vs")
                s.inc(scx.copy(amaxc[:], amax[:]), "ss", 1)
            blk.scalar(sc26)

            def ve26(v):
                s.w(v, "ss")
                s.inc(v.tensor_reduce(am1[:], amaxc[:], mybir.AxisListType.X, ALU.max), "vs", 1)
            blk.vector(ve26)

            def te14(t):
                s.w(t, "vs")
                s.inc(t.transpose(psam[0:1, 0:128], am1[:], identf[:]), "ts", 1)
            blk.tensor(te14)

            def ve27a(v):
                s.w(v, "ts")
                s.inc(v.tensor_copy(amr[:], psam[0:1, 0:128]), "vs", 1)
            blk.vector(ve27a)

            def sc27(scx):
                s.w(scx, "vs")
                s.inc(scx.copy(amrc[:], amr[:]), "ss", 1)
            blk.scalar(sc27)

            def ve27(v):
                s.w(v, "ss")
                s.inc(v.tensor_reduce(ams[:], amrc[:], mybir.AxisListType.X, ALU.max), "vs", 1)
            blk.vector(ve27)

            def te15(t):
                s.w(t, "vs")
                s.inc(t.matmul(psam[:, 0:1], ones1f[:], ams[:], start=True, stop=True), "ts", 1)
            blk.tensor(te15)

            def ve28(v):
                s.w(v, "ts")
                s.inc(v.tensor_scalar(gscale[:], gscale[:], psam[:, 0:1], None, ALU.mult), "vs", 1)
            blk.vector(ve28)

        s.pop()
        if first:
            dbg(tag + "amax", amax[:, :], [128, DG], F32)
            dbg(tag + "am1", am1[:, :], [128, 1], F32)
            dbg(tag + "ams", ams[:, :], [1, 1], F32)
        dbg(tag + "gsc", gscale[:, :], [128, DG], F32)
        dbg(tag + "rstd", r2[0:1, :], [1, TO], F32)
        dbg(tag + "nmr", n2[0:1, :], [1, TO], F32)
        dbg(tag + "ln2", ln2[0][:, :], [128, TO], BF16)
        psG = [s.ps(f"{tag}psG{i}", [128, 512], F32) for i in range(8)]
        bank_G = [0] * 8
        sg_free = [0]
        for dg in range(DG):
            def gp7(g, dg=dg):
                s.w(g, "ts", sg_free[0])
                if dg == 0:
                    s.w(g, "od")
                    s.w(g, "ss")
                s.inc(g.dma_start(out=sgw[:], in_=signW_spill[:, 2048 * dg:2048 * (dg + 1)]), "gd", 16)
            blk.gpsimd(gp7)
            bids = [(2 * dg) % 8, (2 * dg + 1) % 8]

            def te16(t, dg=dg, bids=bids):
                s.w(t, "gd")
                if dg == 0:
                    s.w(t, "vs")
                for bi in bids:
                    s.w(t, "vs", bank_G[bi])
                for k in range(DG):
                    for ti in range(2):
                        s.inc(t.matmul(psG[bids[ti]][:], sgw[:, 128 * k:128 * (k + 1)],
                                       ln2[k][:, 512 * ti:512 * (ti + 1)],
                                       start=(k == 0), stop=(k == DG - 1)), "ts", 1)
                sg_free[0] = s.c["ts"]
            blk.tensor(te16)

            def ve29(v, dg=dg, bids=bids):
                s.w(v, "ts")
                for ti in range(2):
                    s.inc(v.scalar_tensor_tensor(tmpg[:, 0:512], psG[bids[ti]][:], gscale[:, dg:dg + 1],
                                                 fbeta[:, dg:dg + 1].broadcast_to([128, 512]),
                                                 ALU.mult, ALU.add), "vs", 1)
                    s.inc(v.tensor_tensor(resid[dg][:, 512 * ti:512 * (ti + 1)],
                                          resid[dg][:, 512 * ti:512 * (ti + 1)],
                                          tmpg[:, 0:512], ALU.add), "vs", 1)
                for bi in bids:
                    bank_G[bi] = s.c["vs"]
            blk.vector(ve29)
        s.pop()

    ffn_phase("g", True)
    dbg("resid2", resid[0][:, :], [128, TO], F32)
    if upto == "G":
        return fin()

    s.push()
    rstd3_bt = s.sb("rstd3_bt", [128, TO], F32)
    nmr3_bt = s.sb("nmr3_bt", [128, TO], F32)
    xcb = [s.sb(f"xcb{eg}", [128, TO], BF16) for eg in range(EGL)]

    s.push()
    rn = [s.sb(f"rn{i}", [128, TO], BF16) for i in range(DG)]
    rn_warm = [s.sb(f"rnw{i}", [128, WARM], BF16) for i in range(DG)]
    s.push()
    r3 = s.sb("r3", [1, TO], F32)
    n3 = s.sb("n3", [1, TO], F32)
    ln_stats("l3", resid, TO, r3, n3)
    expand_f32("r3x", r3, TO, lambda a, w: rstd3_bt[:, a:a + w])
    expand_f32("n3x", n3, TO, lambda a, w: nmr3_bt[:, a:a + w])
    ln3t = s.sb("ln3t", [128, TO], F32)
    rr3 = s.sb("rr3", [1, TO], F32)
    rr3b = s.sb("rr3b", [128, TO], F32)
    sq3 = s.sb("sq3", [128, 512], F32)
    ps3 = s.ps("ps3", [1, 512], F32)
    for st in range(2):
        c0 = st * 512
        for pg in range(DG):
            def ve(v, pg=pg, st=st, c0=c0):
                s.w(v, "ss")
                s.inc(v.tensor_tensor(ln3t[:, 0:512], resid[pg][:, c0:c0 + 512],
                                      rstd3_bt[:, c0:c0 + 512], ALU.mult), "vs", 1)
                s.inc(v.tensor_tensor(ln3t[:, 0:512], ln3t[:, 0:512],
                                      nmr3_bt[:, c0:c0 + 512], ALU.add), "vs", 1)
            blk.vector(ve)

            def scq(scx, pg=pg):
                s.w(scx, "vs")
                if pg > 0:
                    s.w(scx, "ts")
                s.inc(scx.activation(sq3[:], ln3t[:, 0:512], AF.Square), "ss", 1)
            blk.scalar(scq)

            def te(t, pg=pg):
                s.w(t, "ss")
                s.inc(t.matmul(ps3[:], onescol[:], sq3[:], start=(pg == 0), stop=(pg == DG - 1)), "ts", 1)
            blk.tensor(te)

        def ve30(v, c0=c0):
            s.w(v, "ts")
            s.inc(v.tensor_scalar(rr3[:, c0:c0 + 512], ps3[:], 1.0 / D, None, ALU.mult), "vs", 1)
        blk.vector(ve30)

    def sc9(scx):
        s.w(scx, "vs")
        s.inc(scx.activation(rr3[:], rr3[:], AF.Ln, bias=eps_t[0:1, 0:1]), "ss", 1)
        s.inc(scx.activation(rr3[:], rr3[:], AF.Exp, scale=-0.5), "ss", 1)
    blk.scalar(sc9)
    expand_f32("rr3x", rr3, TO, lambda a, w: rr3b[:, a:a + w])
    for pg in range(DG):
        def ve(v, pg=pg):
            if pg == 0:
                s.w(v, "ss")
            s.inc(v.tensor_tensor(ln3t[:], resid[pg][:], rstd3_bt[:], ALU.mult), "vs", 1)
            s.inc(v.tensor_tensor(ln3t[:], ln3t[:], nmr3_bt[:], ALU.add), "vs", 1)
            s.inc(v.tensor_tensor(ln3t[:], ln3t[:], rr3b[:], ALU.mult), "vs", 1)
            s.inc(v.tensor_scalar(rn[pg][:], ln3t[:], nrmw[:, pg:pg + 1], None, ALU.mult), "vs", 1)
        blk.vector(ve)
    dbg("hr3", r3[0:1, :], [1, TO], F32)
    dbg("hn3", n3[0:1, :], [1, TO], F32)
    dbg("hrr3", rr3[0:1, :], [1, TO], F32)
    s.pop()
    dbg("rn", rn[0][:, :], [128, TO], BF16)

    def gp8(g):
        s.w(g, "vs")
        for i in range(DG):
            s.inc(g.dma_start(out=c1_in[128 * i:128 * (i + 1), :], in_=rn[i][:]), "gd", 16)
        s.w(g, "gd")
        s.inc(g.collective_compute("AllGather", ALU.bypass, replica_groups=PAIRS,
                                   ins=[c1_in.ap().opt()], outs=[c1_out.ap().opt()]), "cc", 1)
        s.w(g, "cc")
        for i in range(DG):
            s.inc(g.dma_start(out=rn_warm[i][:], in_=c1_out[128 * i:128 * (i + 1), TO - WARM:TO]), "gd", 16)
    blk.gpsimd(gp8)

    def ve31(v):
        s.w(v, "gd")
        for i in range(DG):
            s.inc(v.tensor_scalar(rn_warm[i][:], rn_warm[i][:], rrecv_t[:, 0:1], None, ALU.mult), "vs", 1)
    blk.vector(ve31)

    s.push()
    rnh = [s.sb(f"rnh{i}", [128, TO], BF16) for i in range(DG)]
    wbufH = [s.sb(f"wbufH{i}", [128, 16 * 128], BF16) for i in range(2)]
    sktile = s.sb("sktile", [128, TO], BF16)
    psH = [s.ps(f"psH{i}", [128, 512], F32) for i in range(8)]
    bank_H = [0] * 8
    w_H = [0, 0]
    sk_od = [0]
    for thalf in range(2):
        def gph(g, thalf=thalf):
            s.w(g, "cc")
            if thalf > 0:
                s.w(g, "ts")
            for i in range(DG):
                s.inc(g.dma_start(out=rnh[i][:],
                                  in_=c1_out[thalf * D + 128 * i:thalf * D + 128 * (i + 1), :]), "gd", 16)
        blk.gpsimd(gph)
        for eg in range(EGL):
            wb = wbufH[eg % 2]

            def gp(g, eg=eg, wb=wb, thalf=thalf):
                s.w(g, "ts", w_H[eg % 2])
                s.inc(g.dma_start(out=wb[:], in_=W_skip[128 * eg:128 * (eg + 1), :]), "gd", 16)
            blk.gpsimd(gp)
            bids = [(2 * eg) % 8, (2 * eg + 1) % 8]

            def te(t, eg=eg, wb=wb, bids=bids):
                s.w(t, "gd")
                for bi in bids:
                    s.w(t, "ss", bank_H[bi])
                for k in range(DG):
                    for ti in range(2):
                        s.inc(t.matmul(psH[bids[ti]][:], wb[:, 128 * k:128 * (k + 1)],
                                       rnh[k][:, 512 * ti:512 * (ti + 1)],
                                       start=(k == 0), stop=(k == DG - 1)), "ts", 1)
                w_H[eg % 2] = s.c["ts"]
            blk.tensor(te)

            def sc(scx, eg=eg, bids=bids):
                s.w(scx, "ts")
                s.w(scx, "od", sk_od[0])
                for ti in range(2):
                    s.inc(scx.activation(sktile[:, 512 * ti:512 * (ti + 1)], psH[bids[ti]][:], AF.Silu), "ss", 1)
                for bi in bids:
                    bank_H[bi] = s.c["ss"]
            blk.scalar(sc)

            def sy(sx, eg=eg, thalf=thalf):
                s.w(sx, "ss")
                s.inc(sx.dma_start(out=skip_spill[128 * eg:128 * (eg + 1), thalf * TO:(thalf + 1) * TO],
                                   in_=sktile[:]), "od", 16)
                sk_od[0] = s.c["od"]
            blk.sync(sy)
    s.pop()

    s.push()
    s.push()
    xcbh_t = [s.sb(f"xcbht{i}", [128, TO], BF16) for i in range(2)]
    wbufI = [s.sb(f"wbufI{i}", [128, 16 * 128], BF16) for i in range(2)]
    xin_eg = [s.sb(f"xin{i}", [128, WARM + TO], BF16) for i in range(2)]
    cacc2 = s.sb("cacc2", [128, TO], F32)
    psI = [s.ps(f"psI{i}", [128, 512], F32) for i in range(6)]
    bank_I = [0] * 6
    w_I = [0, 0]
    xin_I = [0, 0]
    xcbh_od = [0, 0]
    for eg in range(EG):
        wb = wbufI[eg % 2]
        xie = xin_eg[eg % 2]

        def gp(g, eg=eg, wb=wb):
            s.w(g, "ts", w_I[eg % 2])
            s.inc(g.dma_start(out=wb[:], in_=W_in[128 * eg:128 * (eg + 1), :]), "gd", 16)
        blk.gpsimd(gp)
        bids = [(3 * eg) % 6, (3 * eg + 1) % 6, (3 * eg + 2) % 6]

        def te(t, eg=eg, wb=wb, bids=bids):
            s.w(t, "gd")
            for bi in bids:
                s.w(t, "ss", bank_I[bi])
            for k in range(DG):
                s.inc(t.matmul(psI[bids[0]][:, 0:WARM], wb[:, 128 * k:128 * (k + 1)], rn_warm[k][:],
                               start=(k == 0), stop=(k == DG - 1)), "ts", 1)
                for ti in range(2):
                    s.inc(t.matmul(psI[bids[1 + ti]][:], wb[:, 128 * k:128 * (k + 1)],
                                   rn[k][:, 512 * ti:512 * (ti + 1)],
                                   start=(k == 0), stop=(k == DG - 1)), "ts", 1)
            w_I[eg % 2] = s.c["ts"]
        blk.tensor(te)

        def sc(scx, eg=eg, bids=bids, xie=xie):
            s.w(scx, "ts")
            s.w(scx, "vs", xin_I[eg % 2])
            s.inc(scx.copy(xie[:, 0:WARM], psI[bids[0]][:, 0:WARM]), "ss", 1)
            for ti in range(2):
                s.inc(scx.copy(xie[:, WARM + 512 * ti:WARM + 512 * (ti + 1)], psI[bids[1 + ti]][:]), "ss", 1)
            for bi in bids:
                bank_I[bi] = s.c["ss"]
        blk.scalar(sc)

        def ve(v, eg=eg, xie=xie):
            s.w(v, "ss")
            s.inc(v.tensor_scalar(cacc2[:], xie[:, WARM - 3:WARM - 3 + TO],
                                  cbw[:, 4 * eg:4 * eg + 1], None, ALU.mult), "vs", 1)
            for k in range(1, 4):
                s.inc(v.scalar_tensor_tensor(cacc2[:], xie[:, WARM - 3 + k:WARM - 3 + k + TO],
                                             cbw[:, 4 * eg + k:4 * eg + k + 1], cacc2[:],
                                             ALU.mult, ALU.add), "vs", 1)
            xin_I[eg % 2] = s.c["vs"]
        blk.vector(ve)

        if eg < EGL:
            def sc2(scx, eg=eg):
                s.w(scx, "vs")
                s.inc(scx.activation(xcb[eg][:], cacc2[:], AF.Silu, bias=cbb[:, eg:eg + 1]), "ss", 1)
            blk.scalar(sc2)
        else:
            def sc2(scx, eg=eg):
                s.w(scx, "vs")
                s.w(scx, "od", xcbh_od[eg % 2])
                s.inc(scx.activation(xcbh_t[eg % 2][:], cacc2[:], AF.Silu, bias=cbb[:, eg:eg + 1]), "ss", 1)
            blk.scalar(sc2)

            def syh(sx, eg=eg):
                s.w(sx, "ss")
                s.inc(sx.dma_start(out=c3b_in[128 * (eg - EGL):128 * (eg - EGL + 1), :],
                                   in_=xcbh_t[eg % 2][:]), "od", 16)
                xcbh_od[eg % 2] = s.c["od"]
            blk.sync(syh)
    s.pop()
    dbg("xcb0", xcb[0][:, :], [128, TO], BF16)

    wd1 = s.sb("wd1", [128, 32 * 128], BF16)
    wBt = s.sb("wBt", [128, 32 * 16], BF16)
    wCt = s.sb("wCt", [128, 32 * 16], BF16)
    d1t = s.sb("d1t", [128, TO], BF16)
    B2t = s.sb("B2t", [NB, TO], F32)
    C2tt = s.sb("C2tt", [NB, TO], F32)
    psJ = [s.ps(f"psJ{i}", [128, 512], F32) for i in range(6)]

    def gp9(g):
        s.inc(g.dma_start(out=wd1[:].rearrange("p (k m) -> p k m", k=32),
                          in_=W_d1.ap().rearrange("(k p) m -> p k m", p=128)), "gd", 16)
        s.inc(g.dma_start(out=wBt[:].rearrange("p (k m) -> p k m", k=32),
                          in_=W_Bp.ap().rearrange("(k p) m -> p k m", p=128)), "gd", 16)
        s.inc(g.dma_start(out=wCt[:].rearrange("p (k m) -> p k m", k=32),
                          in_=W_Cp.ap().rearrange("(k p) m -> p k m", p=128)), "gd", 16)
    blk.gpsimd(gp9)

    xhk = [s.sb(f"xhk{i}", [128, TO], BF16) for i in range(2)]
    xhk_free = [0, 0]
    for ti in range(2):
        for k in range(EG):
            if k >= EGL:
                def gph2(g, k=k, ti=ti):
                    s.w(g, "ts", xhk_free[k % 2])
                    if k == EGL and ti == 0:
                        s.w(g, "od")
                    s.inc(g.dma_start(out=xhk[k % 2][:],
                                      in_=c3b_in[128 * (k - EGL):128 * (k - EGL + 1), :]), "gd", 16)
                blk.gpsimd(gph2)

            def te17(t, k=k, ti=ti):
                s.w(t, "gd")
                if k == 0 and ti == 0:
                    s.w(t, "ss")
                rhs = xcb[k][:, 512 * ti:512 * (ti + 1)] if k < EGL \
                    else xhk[k % 2][:, 512 * ti:512 * (ti + 1)]
                s.inc(t.matmul(psJ[ti][:], wd1[:, 128 * k:128 * (k + 1)], rhs,
                               start=(k == 0), stop=(k == EG - 1)), "ts", 1)
                s.inc(t.matmul(psJ[2 + ti][0:16, :], wBt[:, 16 * k:16 * (k + 1)], rhs,
                               start=(k == 0), stop=(k == EG - 1)), "ts", 1)
                s.inc(t.matmul(psJ[4 + ti][0:16, :], wCt[:, 16 * k:16 * (k + 1)], rhs,
                               start=(k == 0), stop=(k == EG - 1)), "ts", 1)
                if k >= EGL:
                    xhk_free[k % 2] = s.c["ts"]
            blk.tensor(te17)

    def sc10(scx):
        s.w(scx, "ts")
        for ti in range(2):
            s.inc(scx.copy(d1t[:, 512 * ti:512 * (ti + 1)], psJ[ti][:]), "ss", 1)
            s.inc(scx.copy(B2t[:, 512 * ti:512 * (ti + 1)], psJ[2 + ti][0:16, :]), "ss", 1)
            s.inc(scx.copy(C2tt[:, 512 * ti:512 * (ti + 1)], psJ[4 + ti][0:16, :]), "ss", 1)
    blk.scalar(sc10)

    wd2 = s.sb("wd2", [128, 32 * 128], BF16)
    bdc = s.sb("bdc", [128, EG], F32)
    dtile = [s.sb(f"dtile{i}", [128, TO], F32) for i in range(2)]

    def gp10(g):
        s.w(g, "ss")
        s.inc(g.dma_start(out=wd2[:], in_=W_d2[:]), "gd", 16)
    blk.gpsimd(gp10)

    def sy9(sx):
        for eg in range(EG):
            s.inc(sx.dma_start(out=bdc[:, eg:eg + 1], in_=b_d2[128 * eg:128 * (eg + 1), :]), "ld", 16)
    blk.sync(sy9)
    dt_free = [0, 0]
    psJ_free = [0] * 6
    for mo in range(EG):
        dte = dtile[mo % 2]
        bids = [(2 * mo) % 6, (2 * mo + 1) % 6]

        def te18(t, mo=mo, bids=bids):
            if mo == 0:
                s.w(t, "gd")
                s.w(t, "ld")
            for bi in bids:
                s.w(t, "ss", psJ_free[bi])
            for ti in range(2):
                s.inc(t.matmul(psJ[bids[ti]][:], wd2[:, 128 * mo:128 * (mo + 1)],
                               d1t[:, 512 * ti:512 * (ti + 1)], start=True, stop=True), "ts", 1)
        blk.tensor(te18)

        def sc11(scx, mo=mo, bids=bids, dte=dte):
            s.w(scx, "ts")
            s.w(scx, "od", dt_free[mo % 2])
            for ti in range(2):
                s.inc(scx.activation(dte[:, 512 * ti:512 * (ti + 1)], psJ[bids[ti]][:],
                                     AF.Exp, bias=bdc[:, mo:mo + 1]), "ss", 1)
                s.inc(scx.activation(dte[:, 512 * ti:512 * (ti + 1)], dte[:, 512 * ti:512 * (ti + 1)],
                                     AF.Ln, bias=one_t[:, 0:1]), "ss", 1)
            for bi in bids:
                psJ_free[bi] = s.c["ss"]
        blk.scalar(sc11)

        def sy10(sx, mo=mo, dte=dte):
            s.w(sx, "ss")
            s.inc(sx.dma_start(out=delta_spill[128 * mo:128 * (mo + 1), :], in_=dte[:]), "od", 16)
            dt_free[mo % 2] = s.c["od"]
        blk.sync(sy10)

    def gp11(g):
        s.w(g, "od")
        s.inc(g.dma_start(out=c3f_in[0:EL, :], in_=delta_spill[EL:E, :]), "gd", 16)
    blk.gpsimd(gp11)

    def sy11(sx):
        s.w(sx, "ss")
        s.inc(sx.dma_start(out=c3f_in[EL:EL + NB, :], in_=B2t[:]), "od", 16)
        s.inc(sx.dma_start(out=c3f_in[EL + NB:EL + 2 * NB, :], in_=C2tt[:]), "od", 16)
    blk.sync(sy11)

    def gp12(g):
        s.w(g, "od")
        s.w(g, "gd")
        s.inc(g.collective_compute("AllGather", ALU.bypass, replica_groups=PAIRS,
                                   ins=[c3f_in.ap().opt()], outs=[c3f_out.ap().opt()]), "cc", 1)
        s.w(g, "cc")
        s.inc(g.collective_compute("AllGather", ALU.bypass, replica_groups=PAIRS,
                                   ins=[c3b_in.ap().opt()], outs=[c3b_out.ap().opt()]), "cc", 1)
    blk.gpsimd(gp12)

    B_full = s.sb("B_full", [NB, T], BF16)
    C_full = s.sb("C_full", [NB, T], BF16)

    def gp13(g):
        s.w(g, "cc")
        s.inc(g.dma_start(out=B_full[:, 0:TO], in_=c3f_out[EL:EL + NB, :]), "gd", 16)
        s.inc(g.dma_start(out=B_full[:, TO:T], in_=c3f_out[NF3 + EL:NF3 + EL + NB, :]), "gd", 16)
        s.inc(g.dma_start(out=C_full[:, 0:TO], in_=c3f_out[EL + NB:EL + 2 * NB, :]), "gd", 16)
        s.inc(g.dma_start(out=C_full[:, TO:T], in_=c3f_out[NF3 + EL + NB:NF3 + EL + 2 * NB, :]), "gd", 16)
    blk.gpsimd(gp13)

    def sy12(sx):
        s.w(sx, "gd")
        s.inc(sx.dma_start(out=Brow_dram[:], in_=B_full[:]), "od", 16)
        s.inc(sx.dma_start(out=Crow_dram[:], in_=C_full[:]), "od", 16)
    blk.sync(sy12)
    s.pop()
    s.pop()

    s.push()
    Brow_n = [s.sb(f"Brow_n{i}", [1, T], BF16) for i in range(2)]
    Crow_n = [s.sb(f"Crow_n{i}", [1, T], BF16) for i in range(2)]
    brow_free = [0, 0]
    dsc = s.sb("dsc", [128, T], F32)
    xsc = s.sb("xsc", [128, T], BF16)
    wsc = s.sb("wsc", [128, T], F32)
    d_loc = s.sb("d_loc", [128, TO], F32)
    d_s0 = s.sb("d_s0", [128, TO], F32)
    d_s1 = s.sb("d_s1", [128, TO], F32)
    x_s0 = s.sb("x_s0", [128, TO], BF16)
    x_s1 = s.sb("x_s1", [128, TO], BF16)
    a_n = [s.sb(f"a_n{i}", [128, T], F32) for i in range(2)]
    u_n = s.sb("u_n", [128, T], F32)
    h_n = s.sb("h_n", [128, T], F32)
    y_eg = s.sb("y_eg", [128, T], F32)
    sk_eg = s.sb("sk_eg", [128, T], BF16)
    psBt = s.ps("psBt", [128, 2048], F32)
    psCt = s.ps("psCt", [128, 2048], F32)
    ysk_od2 = [0]
    u_done = [0]
    q_done = [0]
    scan_done = [0, 0]
    a_done = [0, 0]
    teB_done = [0]
    hq_done = [0, 0]
    hfree = [0, 0]
    row_gd = [0] * 16
    for eg in range(EGL):
        def gp(g, eg=eg):
            s.w(g, "vs")
            s.w(g, "od", ysk_od2[0])
            s.inc(g.dma_start(out=d_loc[:], in_=delta_spill[128 * eg:128 * (eg + 1), :]), "gd", 16)
            s.inc(g.dma_start(out=d_s0[:], in_=c3f_out[128 * eg:128 * (eg + 1), :]), "gd", 16)
            s.inc(g.dma_start(out=d_s1[:], in_=c3f_out[NF3 + 128 * eg:NF3 + 128 * (eg + 1), :]), "gd", 16)
            s.inc(g.dma_start(out=x_s0[:], in_=c3b_out[128 * eg:128 * (eg + 1), :]), "gd", 16)
            s.inc(g.dma_start(out=x_s1[:], in_=c3b_out[EL + 128 * eg:EL + 128 * (eg + 1), :]), "gd", 16)
            s.inc(g.dma_start(out=sk_eg[:], in_=skip_spill[128 * eg:128 * (eg + 1), :]), "gd", 16)
            for j in range(2):
                s.w(g, "ts", brow_free[j])
                if eg == 0 and j == 0:
                    s.w(g, "od")
                s.inc(g.dma_start(out=Brow_n[j][:], in_=Brow_dram[j:j + 1, :]), "gd", 16)
                s.inc(g.dma_start(out=Crow_n[j][:], in_=Crow_dram[j:j + 1, :]), "gd", 16)
                row_gd[j] = s.c["gd"]
        blk.gpsimd(gp)

        def ve(v, eg=eg):
            s.w(v, "gd")
            mask = selr0_t[:, 0:1].broadcast_to([128, TO])
            s.inc(v.select(dsc[:, 0:TO], mask, d_loc[:], d_s0[:]), "vs", 1)
            s.inc(v.select(dsc[:, TO:T], mask, d_s1[:], d_loc[:]), "vs", 1)
            s.inc(v.select(xsc[:, 0:TO], mask, xcb[eg][:], x_s0[:]), "vs", 1)
            s.inc(v.select(xsc[:, TO:T], mask, x_s1[:], xcb[eg][:]), "vs", 1)
            s.inc(v.tensor_tensor(wsc[:], dsc[:], xsc[:], ALU.mult), "vs", 1)
        blk.vector(ve)
        for n in range(NB):
            n2i = n % 2

            def teB(t, eg=eg, n=n, n2i=n2i):
                s.w(t, "gd", row_gd[n])
                s.w(t, "vs", u_done[0])
                for q4 in range(4):
                    s.inc(t.matmul(psBt[:, 512 * q4:512 * (q4 + 1)], ones1[:],
                                   Brow_n[n2i][0:1, 512 * q4:512 * (q4 + 1)],
                                   start=True, stop=True), "ts", 1)
                teB_done[0] = s.c["ts"]
            blk.tensor(teB)

            def sc12(scx, eg=eg, n=n, n2i=n2i):
                s.w(scx, "vs", scan_done[n2i])
                s.w(scx, "gs", hfree[n2i])
                if n == 0:
                    s.w(scx, "vs")
                s.inc(scx.activation(a_n[n2i][:], dsc[:], AF.Exp,
                                     scale=Ab_t[:, NB * eg + n:NB * eg + n + 1]), "ss", 1)
                a_done[n2i] = s.c["ss"]
            blk.scalar(sc12)

            def teC(t, eg=eg, n=n, n2i=n2i):
                s.w(t, "vs", q_done[0])
                for q4 in range(4):
                    s.inc(t.matmul(psCt[:, 512 * q4:512 * (q4 + 1)], ones1[:],
                                   Crow_n[n2i][0:1, 512 * q4:512 * (q4 + 1)],
                                   start=True, stop=True), "ts", 1)
                brow_free[n2i] = s.c["ts"]
            blk.tensor(teC)

            def ve32(v, eg=eg, n=n, n2i=n2i):
                s.w(v, "ts", teB_done[0])
                s.w(v, "ss", a_done[n2i])
                s.inc(v.tensor_tensor(u_n[:], wsc[:], psBt[:], ALU.mult), "vs", 1)
                u_done[0] = s.c["vs"]
                s.inc(v.tensor_tensor_scan(h_n[:], a_n[n2i][:], u_n[:], 0.0, ALU.mult, ALU.add), "vs", 1)
                scan_done[n2i] = s.c["vs"]
                s.w(v, "ts")
                s.inc(v.tensor_tensor(a_n[n2i][:], h_n[:], psCt[:], ALU.mult), "vs", 1)
                q_done[0] = s.c["vs"]
                hq_done[n2i] = s.c["vs"]
            blk.vector(ve32)

            def gpy(g, eg=eg, n=n, n2i=n2i):
                if n + 2 < NB:
                    s.w(g, "ts", brow_free[n2i])
                    s.inc(g.dma_start(out=Brow_n[n2i][:], in_=Brow_dram[n + 2:n + 3, :]), "gd", 16)
                    s.inc(g.dma_start(out=Crow_n[n2i][:], in_=Crow_dram[n + 2:n + 3, :]), "gd", 16)
                    row_gd[n + 2] = s.c["gd"]
                s.w(g, "vs", hq_done[n2i])
                if n == 0:
                    s.inc(g.tensor_copy(y_eg[:], a_n[0][:]), "gs", 1)
                else:
                    s.inc(g.tensor_tensor(y_eg[:], y_eg[:], a_n[n2i][:], ALU.add), "gs", 1)
                hfree[n2i] = s.c["gs"]
            blk.gpsimd(gpy)

        def ve33(v, eg=eg):
            s.w(v, "od", ysk_od2[0])
            s.w(v, "gs")
            s.inc(v.scalar_tensor_tensor(y_eg[:], xsc[:], wdcol[:, eg:eg + 1], y_eg[:],
                                         ALU.mult, ALU.add), "vs", 1)
            s.inc(v.tensor_tensor(sk_eg[:], y_eg[:], sk_eg[:], ALU.mult), "vs", 1)
        blk.vector(ve33)

        def sy13(sx, eg=eg):
            s.w(sx, "vs")
            s.inc(sx.dma_start(out=ysk_spill[128 * eg:128 * (eg + 1), :], in_=sk_eg[:]), "od", 16)
            ysk_od2[0] = s.c["od"]
        blk.sync(sy13)
    s.pop()
    dbg("delta0", delta_spill[0:128, :], [128, TO], F32)
    dbg("ysk5", ysk_spill[128 * 5:128 * 6, :], [128, T], BF16)
    dbg("ysk15", ysk_spill[128 * 15:128 * 16, :], [128, T], BF16)
    dbg("skip0", skip_spill[0:128, :], [128, T], BF16)
    dbg("ysk0", ysk_spill[0:128, :], [128, T], BF16)
    dbg("ds0", c3f_out[0:128, :], [128, TO], F32)
    dbg("ds1", c3f_out[NF3:NF3 + 128, :], [128, TO], F32)

    s.push()
    wbufO = s.sb("wbufO", [128, 16 * 8 * 128], BF16)
    yskt = [s.sb(f"yskt{i}", [128, 512], BF16) for i in range(16)]
    otile = s.sb("otile", [128, 512], F32)
    dbg_ps = s.sb("dbg_ps", [128, 512], F32)
    dbg_yt = s.sb("dbg_yt", [128, 512], BF16)
    dbg_wo = s.sb("dbg_wo", [128, 128], BF16)
    psO2 = [s.ps(f"psO2_{i}", [128, 512], F32) for i in range(8)]
    ysk_free = [0, 0]
    psO2_free = [0] * 8
    for dgw in range(2):
        def gpw2(g, dgw=dgw):
            s.w(g, "ts")
            s.inc(g.dma_start(out=wbufO[:], in_=W_out_in[128 * dgw:128 * (dgw + 1), :]), "gd", 16)
        blk.gpsimd(gpw2)
        for tt in range(4):
            def gp15(g, tt=tt, dgw=dgw):
                s.w(g, "ts", ysk_free[0])
                for k in range(EGL):
                    s.inc(g.dma_start(out=yskt[k][:],
                                      in_=ysk_spill[128 * k:128 * (k + 1), 512 * tt:512 * (tt + 1)]), "gd", 16)
            blk.gpsimd(gp15)
            for k in range(EGL):
                def te20(t, tt=tt, k=k, dgw=dgw):
                    s.w(t, "gd")
                    if k == 0:
                        for bi in range(8):
                            s.w(t, "vs", psO2_free[bi])
                    for dg8 in range(8):
                        s.inc(t.matmul(psO2[dg8][:], wbufO[:, 1024 * k + 128 * dg8:1024 * k + 128 * (dg8 + 1)],
                                       yskt[k][:], start=(k == 0), stop=(k == EGL - 1)), "ts", 1)
                    ysk_free[0] = s.c["ts"]
                blk.tensor(te20)
                if dgw == 0 and tt == 0 and k == 0:
                    def vdbg(v):
                        s.w(v, "ts")
                        s.inc(v.tensor_copy(dbg_yt[:], yskt[0][:]), "vs", 1)
                        s.inc(v.tensor_copy(dbg_wo[:], wbufO[:, 0:128]), "vs", 1)
                    blk.vector(vdbg)
            for dg8 in range(8):
                def ve34(v, dgw=dgw, tt=tt, dg8=dg8):
                    s.w(v, "ts")
                    s.w(v, "od")
                    s.inc(v.tensor_copy(otile[:], psO2[dg8][:]), "vs", 1)
                    if dgw == 0 and tt == 0 and dg8 == 0:
                        s.inc(v.tensor_copy(dbg_ps[:], psO2[0][:]), "vs", 1)
                    psO2_free[dg8] = s.c["vs"]
                blk.vector(ve34)

                def sy14(sx, dgw=dgw, tt=tt, dg8=dg8):
                    s.w(sx, "vs")
                    dg = dgw * 8 + dg8
                    j = tt % 2
                    half = 0 if tt < 2 else 1
                    s.inc(sx.dma_start(out=c4_in[j][half * D + 128 * dg:half * D + 128 * (dg + 1), :],
                                       in_=otile[:]), "od", 16)
                blk.sync(sy14)
    dbg("h6ps", dbg_ps[:, :], [128, 512], F32)
    dbg("h6yt", dbg_yt[:, :], [128, 512], BF16)
    dbg("h6wo", dbg_wo[:, :], [128, 128], BF16)
    dbg("c4pre", c4_in[0][0:128, :], [128, 512], F32)
    dbg("c4preb", c4_in[0][D:D + 128, :], [128, 512], F32)
    for j in range(2):
        def gp16(g, j=j):
            s.w(g, "od")
            s.w(g, "cc")
            s.inc(g.collective_compute("ReduceScatter", ALU.add, replica_groups=PAIRS,
                                       ins=[c4_in[j].ap().opt()], outs=[c4_out[j].ap().opt()]), "cc", 1)
        blk.gpsimd(gp16)

    rs_t = s.sb("rs_t", [128, 512], F32)
    ln3b = s.sb("ln3b", [128, 512], F32)
    rs_free = [0]
    for j in range(2):
        for dg in range(DG):
            def gp17(g, j=j, dg=dg):
                s.w(g, "cc")
                s.w(g, "vs", rs_free[0])
                s.inc(g.dma_start(out=rs_t[:], in_=c4_out[j][128 * dg:128 * (dg + 1), :]), "gd", 16)
            blk.gpsimd(gp17)

            def ve35(v, j=j, dg=dg):
                s.w(v, "gd")
                s.inc(v.tensor_tensor(ln3b[:], resid[dg][:, 512 * j:512 * (j + 1)],
                                      rstd3_bt[:, 512 * j:512 * (j + 1)], ALU.mult), "vs", 1)
                s.inc(v.tensor_tensor(ln3b[:], ln3b[:], nmr3_bt[:, 512 * j:512 * (j + 1)], ALU.add), "vs", 1)
                s.inc(v.tensor_tensor(ln3b[:], ln3b[:], rs_t[:], ALU.add), "vs", 1)
                s.inc(v.tensor_tensor(resid[dg][:, 512 * j:512 * (j + 1)],
                                      resid[dg][:, 512 * j:512 * (j + 1)], ln3b[:], ALU.add), "vs", 1)
                rs_free[0] = s.c["vs"]
            blk.vector(ve35)
    s.pop()
    s.pop()
    dbg("resid3", resid[0][:, :], [128, TO], F32)
    dbg("c4o0", c4_out[0][0:128, :], [128, 512], F32)
    dbg("c4i0a", c4_in[0][0:128, :], [128, 512], F32)
    dbg("c4i0b", c4_in[0][D:D + 128, :], [128, 512], F32)
    if upto == "H":
        return fin()

    ffn_phase("i", False)
    return fin()


def _prep_core_inputs(core, inp):
    b, r = core // 2, core % 2
    t0 = r * TO
    f32 = np.float32

    def c(a):
        return np.ascontiguousarray(np.asarray(a), dtype=f32)

    x = np.asarray(inp["L0_attn_a_x"])
    xT = c(x[b, t0:t0 + TO, :].T)
    if r == 0:
        xwT = np.zeros((D, WARM), f32)
    else:
        xwT = c(x[b, t0 - WARM:t0, :].T)

    def roll(a, ax):
        return np.ascontiguousarray(np.roll(a, -r * EL, axis=ax))

    def pack(WT, K, MO, M):
        a = WT.reshape(K, 128, MO, M).transpose(2, 1, 0, 3)
        return np.ascontiguousarray(a.reshape(MO * 128, K * M))

    return {
        "xT": xT, "xwT": xwT,
        "W_inprojT": pack(c(np.asarray(inp["L0_attn_a_in_proj_w"]).T), 16, 64, 128),
        "W_xprojT": pack(c(np.asarray(inp["L0_attn_a_x_proj_w"]).T)[:, 0:4096], 32, 32, 128),
        "W_xptail": c(np.asarray(inp["L0_attn_a_x_proj_w"]).T)[:, 4096:4416].copy(),
        "conv_a_w": c(inp["L0_attn_a_conv_w"]),
        "conv_a_b": c(inp["L0_attn_a_conv_b"]).reshape(E, 1),
        "W_dtT": c(np.asarray(inp["L0_attn_a_dt_proj_w"]).T),
        "dt_b_col": c(inp["L0_attn_a_dt_proj_b"]).reshape(H, 1),
        "Alog_a": c(inp["L0_attn_a_A_log"]).reshape(H, 1),
        "D_a_row": np.repeat(c(inp["L0_attn_a_D"]), PDIM).reshape(1, E),
        "W_outprojT": pack(c(np.asarray(inp["L0_attn_a_out_proj_w"]).T), 32, 2, 1024),
        "ffn_WT": pack(c(np.asarray(inp["L0_ffn_weight"]).T), 16, 16, 128),
        "ffn_gamma": c(inp["L0_ffn_gamma"]).reshape(D, 1),
        "ffn_beta": c(inp["L0_ffn_beta"]).reshape(D, 1),
        "ffn_lnw": c(inp["L0_ffn_ln_weight"]).reshape(D, 1),
        "ffn_lnb": c(inp["L0_ffn_ln_bias"]).reshape(D, 1),
        "norm_w": c(inp["L0_attn_b_norm_w"]).reshape(D, 1),
        "W_skip": pack(roll(c(inp["L0_attn_b_W_skip"]), 1)[:, 0:EL], 16, 16, 128),
        "W_in": pack(roll(c(inp["L0_attn_b_W_in"]), 1), 16, 32, 128),
        "conv_b_w": roll(c(np.asarray(inp["L0_attn_b_conv_w"]).reshape(4, E).T), 0),
        "conv_b_b": roll(c(inp["L0_attn_b_conv_b"]), 0).reshape(E, 1),
        "W_d1": roll(c(inp["L0_attn_b_W_delta_1"]), 0),
        "W_d2": roll(c(inp["L0_attn_b_W_delta_2"]), 1),
        "b_d2": roll(c(inp["L0_attn_b_b_delta_2"]), 0).reshape(E, 1),
        "W_Bp": roll(c(inp["L0_attn_b_W_B"]), 0),
        "W_Cp": roll(c(inp["L0_attn_b_W_C"]), 0),
        "Alog_b": roll(c(inp["L0_attn_b_A_log"]), 0),
        "W_D_col": roll(c(inp["L0_attn_b_W_D"]), 0).reshape(E, 1),
        "W_out_in": pack(roll(c(inp["L0_attn_b_W_out"]), 0)[0:EL, :], 16, 2, 1024),
        "m_send": np.full((128, 1), 1.0 - r, f32),
        "r_recv": np.full((128, 1), float(r), f32),
        "sel_r0": np.full((128, 1), 1 - r, np.uint8),
    }


def kernel(**inputs):
    nc = build_nc()
    in_maps = [_prep_core_inputs(cc, inputs) for cc in range(8)]
    res = run_bass_kernel_spmd(nc, in_maps, core_ids=list(range(8)))
    o = np.zeros((4, T, D), np.float32)
    for cc in range(8):
        b, r = cc // 2, cc % 2
        o[b, r * TO:(r + 1) * TO, :] = np.asarray(res.results[cc]["out"]).T
    return o
